# version 27
# baseline (speedup 1.0000x reference)
import numpy as np
import ml_dtypes
from contextlib import ExitStack
from itertools import zip_longest

import concourse.bass as bass
import concourse.mybir as mybir
from concourse.bass_utils import run_bass_kernel_spmd

F32 = mybir.dt.float32
BF16 = mybir.dt.bfloat16
I32 = mybir.dt.int32
I8 = mybir.dt.int8
AF = mybir.ActivationFunctionType
ALU = mybir.AluOpType
AX = mybir.AxisListType
BF = ml_dtypes.bfloat16

SEQ, D, NH, HD, WIN = 8192, 2048, 16, 128, 512
NCORES = 8
R = SEQ // NCORES
HL = WIN
T1 = R + HL
DH = 8192
KC = D // 128
KCH = DH // 128
NTC = T1 // 128
NTO = R // 128
EPS = 1e-5
CEXP = -12.0
MASKV = -30000.0
CONSTS = [0.0, 1.0, CEXP, -0.5, -1.0, EPS]

ENGINES = ("tensor", "vector", "scalar", "gpsimd", "sync")
NDMASEM = 8
LAST_SCHED = None


class _Op:
    def __init__(self, eng, fn, reads, writes, dma, tag, idx, fence, fsz=0):
        self.eng, self.fn = eng, fn
        self.reads, self.writes = tuple(reads), tuple(writes)
        self.dma, self.tag, self.idx = dma, tag, idx
        self.fsz = fsz
        self.fence = fence
        self.needs_inc = False
        self.done = None
        self.deps = None
        self.waits = None


class Sched:
    def __init__(self):
        self.ops = []
        self._fence_deps = ()
        self._col = None

    def add(self, eng, fn, reads=(), writes=(), dma=False, tag="", fsz=0):
        assert eng in ENGINES, eng
        if self._col is not None:
            self._col.append(lambda: self.add(eng, fn, reads, writes, dma, tag, fsz))
            return None
        o = _Op(eng, fn, reads, writes, dma, tag, len(self.ops), self._fence_deps, fsz)
        self.ops.append(o)
        return o

    def begin_collect(self):
        assert self._col is None
        self._col = []

    def end_collect(self):
        c, self._col = self._col, None
        return c

    @staticmethod
    def _ordered(d, o):
        return (d.eng == o.eng and not d.dma and not o.dma and
                (d.eng == "tensor" or (d.fsz >= 2048 and d.eng in ("vector", "scalar"))))

    def dma(self, eng, out, in_, reads=(), writes=(), tag="", **kw):
        fn = (lambda out=out, in_=in_, kw=kw: (lambda e: e.dma_start(out=out, in_=in_, **kw)))()
        return self.add(eng, fn, reads=reads, writes=writes, dma=True, tag=tag)

    def fence(self):
        last = {}
        for o in self.ops:
            last[(o.eng, o.dma)] = o
        self._fence_deps = tuple(last.values())

    def resolve(self):
        last_w = {}
        readers = {}
        for o in self.ops:
            deps = set(o.fence)
            for r in o.reads:
                w = last_w.get(r)
                if w is not None:
                    deps.add(w)
            for w_ in o.writes:
                lw = last_w.get(w_)
                if lw is not None:
                    deps.add(lw)
                deps.update(readers.get(w_, ()))
            deps.discard(o)
            o.deps = deps
            for r in o.reads:
                readers.setdefault(r, []).append(o)
            for w_ in o.writes:
                last_w[w_] = o
                readers[w_] = []
        for o in self.ops:
            for d in o.deps:
                if not self._ordered(d, o):
                    d.needs_inc = True
        cnt = {}
        dma_idx = {}
        for o in self.ops:
            if o.dma:
                i = dma_idx.get(o.eng, 0)
                dma_idx[o.eng] = i + 1
                key = ("dma", o.eng, i % NDMASEM)
                cnt[key] = 16 * (i // NDMASEM + 1)
                o.done = (key, cnt[key])
                o.dma_semkey = key
            elif o.needs_inc:
                key = ("eng", o.eng)
                cnt[key] = cnt.get(key, 0) + 1
                o.done = (key, cnt[key])
        self.final_counts = dict(cnt)
        marks = {e: {} for e in ENGINES}
        for o in self.ops:
            wl = {}
            for d in o.deps:
                if self._ordered(d, o):
                    continue
                assert d.done is not None, (d.tag, o.tag)
                k, v = d.done
                if wl.get(k, 0) < v:
                    wl[k] = v
            ww = marks[o.eng]
            o.waits = []
            for k, v in sorted(wl.items(), key=lambda kv: str(kv[0])):
                if ww.get(k, 0) < v:
                    o.waits.append((k, v))
                    ww[k] = v

    def emit(self, nc, blk, sems):
        by_eng = {}
        for o in self.ops:
            by_eng.setdefault(o.eng, []).append(o)
        for eng in ENGINES:
            if eng not in by_eng:
                continue
            ops = by_eng[eng]

            def body(e, ops=ops, eng=eng):
                for o in ops:
                    for k, v in o.waits:
                        e.wait_ge(sems[k], v)
                    inst = o.fn(e)
                    if o.dma:
                        inst.then_inc(sems[o.dma_semkey], 16)
                    elif o.needs_inc:
                        inst.then_inc(sems[("eng", eng)], 1)
                if eng == "sync":
                    for k, c in sorted(self.final_counts.items(), key=lambda kv: str(kv[0])):
                        if k[0] == "dma":
                            e.wait_ge(sems[k], c)
            getattr(blk, eng)(body)


def host_tables(core_id):
    p0 = core_id * R
    pos = np.arange(p0 - HL, p0 + R, dtype=np.float64)
    pos = np.where(pos < 0, 0, pos).astype(np.float32)
    inv = (1.0 / (np.float32(10000.0) ** (np.arange(0, HD, 2, dtype=np.float32) / np.float32(HD)))).astype(np.float32)
    fr = (pos[None, :] * inv[:, None]).astype(np.float32)
    cos = np.cos(fr).astype(np.float32)
    sin = np.sin(fr).astype(np.float32)
    cosF = np.empty((HD, T1), np.float32)
    sinF = np.empty((HD, T1), np.float32)
    cosF[0::2] = cos; cosF[1::2] = cos
    sinF[0::2] = -sin; sinF[1::2] = sin
    s = np.float32(HD ** (-0.5))
    return np.concatenate([cosF, sinF, cosF * s, sinF * s], axis=1).astype(BF)


def host_masks(core_id):
    j = np.arange(128)[:, None]
    i = np.arange(WIN)[None, :]
    tiles = []
    for jc in range(8):
        jl = jc * 128 + j
        valid = (jl > i) & (jl <= i + WIN)
        tiles.append(np.where(valid, 0.0, MASKV).astype(np.float32))
    main = np.concatenate(tiles, axis=1)
    if core_id == 0:
        b0 = np.full((128, 4 * WIN), MASKV, np.float32)
    else:
        b0 = np.concatenate(tiles[:4], axis=1)
    return main.astype(BF), b0.astype(BF)


def host_ident():
    ident = np.eye(128, dtype=np.float32).astype(BF)
    sw = np.arange(128)
    sw = (sw // 2) * 2 + (1 - sw % 2)
    pswap = np.eye(128, dtype=np.float32)[sw].astype(BF)
    return ident, pswap


def host_consts():
    c = np.zeros((128, 8), np.float32)
    c[:, :len(CONSTS)] = np.array(CONSTS, np.float32)[None, :]
    return c


def build_program(phases=("wprep", "attn", "ffn1", "attnb", "ffn2"), debug=()):
    nc = bass.Bass("TRN2", target_bir_lowering=False)
    sch = Sched()

    def din(name, shape, dtype=F32):
        return nc.dram_tensor(name, list(shape), dtype, kind="ExternalInput").ap()

    x_in = din("x_in", [T1, D])
    aWq = din("aWq", [D, D]); aWk = din("aWk", [D, D])
    aWv = din("aWv", [D, D]); aWo = din("aWo", [D, D])
    bWqkv = din("bWqkv", [3 * D, D]); bWo = din("bWo", [D, D])
    W1 = din("W1", [DH, D]); W2 = din("W2", [D, DH])
    b1v = din("b1v", [DH]); b2v = din("b2v", [D])
    lnw = din("lnw", [D]); lnb = din("lnb", [D])
    tabs = din("tabs", [128, 4 * T1], BF16)
    maskM = din("maskM", [128, 8 * WIN], BF16)
    maskB0 = din("maskB0", [128, 4 * WIN], BF16)
    identI = din("identI", [128, 128], BF16)
    pswapI = din("pswapI", [128, 128], BF16)
    constsI = din("constsI", [128, 8])

    out = nc.dram_tensor("out", [R, D], F32, kind="ExternalOutput").ap()

    dbg_tensors = {}

    def dbg(name, ap, rg, shape, dtype):
        if name in debug:
            t = nc.dram_tensor(name, list(shape), dtype, kind="ExternalOutput").ap()
            dbg_tensors[name] = t
            sch.dma("sync", t, ap, reads=[rg], tag="dump_" + name)

    def scr(name, shape, dtype=BF16):
        kind = "ExternalOutput" if name in debug else "Internal"
        return nc.dram_tensor(name, list(shape), dtype, kind=kind).ap()

    sc_qW1T = scr("sc_qW1T", [D, DH]); sc_qW2T = scr("sc_qW2T", [DH, D])
    sc_X = scr("sc_X", [R, D], F32)

    ctx = ExitStack()
    with ctx:
        def sb(name, shape, dtype=BF16, stack=None):
            return (stack or ctx).enter_context(nc.sbuf_tensor(name, shape, dtype))

        pb = [ctx.enter_context(nc.psum_tensor(f"pb{i}", [128, 512], F32)) for i in range(8)]
        PB = [f"pb{i}" for i in range(8)]
        strip0 = sb("strip0", [128, 16 * 512], BF16)
        strip1 = sb("strip1", [128, 16 * 512], BF16)
        sconsts = sb("sconsts", [128, 8], F32)
        ident = sb("ident", [128, 128], BF16)
        pswap = sb("pswap", [128, 128], BF16)
        ones_r = sb("ones_r", [1, 128], BF16)
        ones_rf = sb("ones_rf", [1, 128], F32)
        ones_c = sb("ones_c", [128, 1], BF16)
        ln2stats = sb("ln2stats", [128, 2 * NTO], F32)

        for ci, cv in enumerate(CONSTS):
            nc.const_aps.register(F32, cv, sconsts[:, ci:ci + 1])

        sch.dma("sync", sconsts[:], constsI, writes=["consts"], tag="ld_consts")
        sch.dma("sync", ident[:], identI, writes=["ident"], tag="ld_ident")
        sch.dma("sync", pswap[:], pswapI, writes=["pswap"], tag="ld_pswap")
        sch.add("vector", lambda e: e.memset(ones_r[:], 1.0), writes=["ones_r"])
        sch.add("vector", lambda e: e.memset(ones_rf[:], 1.0), writes=["ones_rf"])
        sch.add("vector", lambda e: e.memset(ones_c[:], 1.0), writes=["ones_c"])

        def quant_chunk(src_ap, src_rg, F, qt, tag):
            G = F // 32
            (amax, amaxr), (ei, eir), (rcp, rcpr), (scl, sclr), (i8t, i8r), (qo, qor) = (
                qt["amax"], qt["ei"], qt["recip"], qt["scale"], qt["i8"], qt["q"])
            sch.add("vector", lambda e: e.tensor_reduce(
                amax[:, 0:G].unsqueeze(2), src_ap.rearrange("p (g b) -> p g b", b=32), AX.X, ALU.max,
                apply_absolute_value=True), reads=[src_rg], writes=[amaxr], tag=tag + "am")
            sch.add("vector", lambda e: e.tensor_single_scalar(
                ei[:, 0:G], amax[:, 0:G].bitcast(I32), 23, ALU.logical_shift_right),
                reads=[amaxr], writes=[eir], tag=tag + "e")
            sch.add("vector", lambda e: e.tensor_scalar(
                rcp[:, 0:G].bitcast(I32), ei[:, 0:G], -1, 260, ALU.mult, ALU.add),
                reads=[eir], writes=[rcpr], tag=tag + "r1")
            sch.add("vector", lambda e: e.tensor_single_scalar(
                rcp[:, 0:G].bitcast(I32), rcp[:, 0:G].bitcast(I32), 23, ALU.logical_shift_left),
                reads=[rcpr], writes=[rcpr], tag=tag + "r2")
            sch.add("vector", lambda e: e.tensor_scalar(
                scl[:, 0:G].bitcast(I32), ei[:, 0:G], 1, -6, ALU.mult, ALU.add),
                reads=[eir], writes=[sclr], tag=tag + "s1")
            sch.add("vector", lambda e: e.tensor_single_scalar(
                scl[:, 0:G].bitcast(I32), scl[:, 0:G].bitcast(I32), 23, ALU.logical_shift_left),
                reads=[sclr], writes=[sclr], tag=tag + "s2")
            rB = rcp[:, 0:G].unsqueeze(2).broadcast_to([128, G, 32])
            sB = scl[:, 0:G].unsqueeze(2).broadcast_to([128, G, 32])
            sch.add("vector", lambda e: e.tensor_tensor(
                i8t[:, 0:F].rearrange("p (g b) -> p g b", b=32),
                src_ap.rearrange("p (g b) -> p g b", b=32), rB, ALU.mult),
                reads=[src_rg, rcpr], writes=[i8r], tag=tag + "i8", fsz=F)
            sch.add("vector", lambda e: e.tensor_tensor(
                qo[:, 0:F].rearrange("p (g b) -> p g b", b=32),
                i8t[:, 0:F].rearrange("p (g b) -> p g b", b=32), sB, ALU.mult),
                reads=[i8r, sclr], writes=[qor], tag=tag + "q", fsz=F)

        def make_qt(pre, Fmax, stack, share_i8=None):
            G = Fmax // 32
            return dict(
                amax=(sb(pre + "amax", [128, G], F32, stack), pre + "amax"),
                ei=(sb(pre + "ei", [128, G], I32, stack), pre + "ei"),
                recip=(sb(pre + "recip", [128, G], F32, stack), pre + "recip"),
                scale=(sb(pre + "scale", [128, G], F32, stack), pre + "scale"),
                i8=(share_i8 if share_i8 is not None else (sb(pre + "i8", [128, Fmax], I8, stack), pre + "i8")),
                q=(sb(pre + "q", [128, Fmax], BF16, stack), pre + "q"),
            )

        def transpose_pack(src_ap, src_rg, chunk_cols, banks, evac, tag, perm=None):
            pmat, pmn = (perm, "pswap") if perm is not None else (ident, "ident")
            nb = len(banks)
            for gi in range(0, len(chunk_cols), 8):
                n = min(8, len(chunk_cols) - gi)
                bsel = banks[(gi // 8) % nb]
                brg = PB[bsel]
                for j in range(n):
                    col = chunk_cols[gi + j]
                    sch.add("tensor", (lambda bsel=bsel, j=j, col=col, pmat=pmat: (lambda e: e.transpose(
                        pb[bsel][:].bitcast(BF16)[:, j * 128:(j + 1) * 128],
                        src_ap[:, col:col + 128], pmat[:])))(),
                        reads=[src_rg, pmn], writes=[brg], tag=tag + f"T{gi + j}")
                evac(bsel, gi, n, brg)

        def ln_chunk(xap, xrg, stats, stats_rg, scol, sqscr, sqrg, tag, save_to=None):
            rN = 1.0 / D
            sch.add("vector", lambda e: e.tensor_reduce(
                stats[:, scol + 2:scol + 3], xap, AX.X, ALU.add),
                reads=[xrg], writes=[stats_rg], tag=tag + "sm")
            sch.add("scalar", lambda e: e.activation(sqscr, xap, AF.Square, bias=0.0, scale=1.0,
                                                     accum_out=stats[:, scol + 3:scol + 4]),
                    reads=[xrg, "consts"], writes=[stats_rg, sqrg], tag=tag + "sq")
            sch.add("vector", lambda e: e.tensor_single_scalar(
                stats[:, scol:scol + 1], stats[:, scol + 2:scol + 3], rN, ALU.mult),
                reads=[stats_rg], writes=[stats_rg], tag=tag + "mn")
            sch.add("vector", lambda e: e.tensor_tensor(
                stats[:, scol + 4:scol + 5], stats[:, scol:scol + 1], stats[:, scol:scol + 1], ALU.mult),
                reads=[stats_rg], writes=[stats_rg], tag=tag + "m2")
            sch.add("vector", lambda e: e.tensor_single_scalar(
                stats[:, scol + 5:scol + 6], stats[:, scol + 3:scol + 4], rN, ALU.mult),
                reads=[stats_rg], writes=[stats_rg], tag=tag + "v1")
            sch.add("vector", lambda e: e.tensor_tensor(
                stats[:, scol + 1:scol + 2], stats[:, scol + 5:scol + 6], stats[:, scol + 4:scol + 5],
                ALU.subtract),
                reads=[stats_rg], writes=[stats_rg], tag=tag + "vr")
            if save_to is not None:
                sv, svrg, svcol = save_to
                sch.add("vector", lambda e: e.tensor_copy(sv[:, svcol:svcol + 1], stats[:, scol:scol + 1]),
                        reads=[stats_rg], writes=[svrg], tag=tag + "sv")
            sch.add("scalar", lambda e: e.activation(stats[:, scol + 1:scol + 2], stats[:, scol + 1:scol + 2],
                                                     AF.Ln, bias=EPS, scale=1.0),
                    reads=[stats_rg, "consts"], writes=[stats_rg], tag=tag + "ln")
            sch.add("scalar", lambda e: e.activation(stats[:, scol + 1:scol + 2], stats[:, scol + 1:scol + 2],
                                                     AF.Exp, bias=0.0, scale=-0.5),
                    reads=[stats_rg, "consts"], writes=[stats_rg], tag=tag + "ex")
            if save_to is not None:
                sv, svrg, svcol = save_to
                sch.add("vector", lambda e: e.tensor_copy(sv[:, svcol + 1:svcol + 2], stats[:, scol + 1:scol + 2]),
                        reads=[stats_rg], writes=[svrg], tag=tag + "sv2")

        def ln_apply(xap, xrg, stats, stats_rg, scol, outap, outrg, tag):
            sch.add("vector", lambda e: e.tensor_scalar(
                outap, xap, stats[:, scol:scol + 1], stats[:, scol + 1:scol + 2],
                ALU.subtract, ALU.mult), reads=[xrg, stats_rg], writes=[outrg], tag=tag + "ap", fsz=D)

        def mk_wstrip(wbuf, wrg, w_dram, grow, tbanks, wcb, wcbpre, tag):
            for rc in range(4):
                wt = wcb[rc % 2]; wtrg = wcbpre + str(rc % 2)
                sch.dma("gpsimd", wt[:],
                        w_dram[grow * 512 + rc * 128: grow * 512 + (rc + 1) * 128, :],
                        writes=[wtrg], tag=f"{tag}ld{rc}")
                for gi in range(2):
                    bsel = tbanks[gi]
                    for j in range(8):
                        ic = gi * 8 + j
                        sch.add("tensor", (lambda bsel=bsel, j=j, ic=ic, wt=wt: (lambda e: e.transpose(
                            pb[bsel][:].bitcast(BF16)[:, j * 128:(j + 1) * 128],
                            wt[:, ic * 128:(ic + 1) * 128], ident[:])))(),
                            reads=[wtrg, "ident"], writes=[PB[bsel]], tag=f"{tag}T{rc}_{ic}")
                    sch.add("vector", (lambda bsel=bsel, gi=gi, rc=rc, wbuf=wbuf: (lambda e: e.tensor_copy(
                        wbuf[:].rearrange("p (kc o) -> p kc o", o=512)[:, gi * 8:gi * 8 + 8, rc * 128:(rc + 1) * 128],
                        pb[bsel][:].bitcast(BF16)[:, 0:1024].rearrange("p (c q) -> p c q", q=128))))(),
                        reads=[PB[bsel]], writes=[wrg], tag=f"{tag}e{rc}_{gi}", fsz=1024)

        if "wprep" in phases:
            wp = ExitStack()
            with wp:
                NWR = 4
                wbig = [sb(f"wbig{i}", [128, 2048], F32, wp) for i in range(NWR)]
                qtw = [make_qt(f"wq{i}_", 2048, wp) for i in range(NWR)]
                wtev = [sb(f"wtev{i}", [128, 1024], BF16, wp) for i in range(2 * NWR)]
                _wn = [0]

                def prep_weight(src_ap, dest_scT, o_chunks, k_cols, do_quant, tag, screg, dest_scT2=None,
                                screg2=None):
                    npc = k_cols // 2048
                    dview = dest_scT.rearrange("(kc p) o -> p kc o", p=128)
                    dview2 = dest_scT2.rearrange("(kc p) o -> p kc o", p=128) if dest_scT2 is not None else None
                    for oc in range(o_chunks):
                        for p in range(npc):
                            par = _wn[0] % NWR
                            _wn[0] += 1
                            w = wbig[par]; wrg = f"wbig{par}"
                            qtc = qtw[par]
                            sch.dma("sync", w[:], src_ap[oc * 128:(oc + 1) * 128, p * 2048:(p + 1) * 2048],
                                    writes=[wrg], tag=f"{tag}ld{oc}_{p}")
                            if do_quant:
                                quant_chunk(w[:], wrg, 2048, qtc, f"{tag}qz{oc}_{p}")
                            else:
                                sch.add("vector", (lambda w=w, qtc=qtc: (lambda e: e.tensor_copy(
                                    qtc["q"][0][:, 0:2048], w[:])))(),
                                    reads=[wrg], writes=[qtc["q"][1]], tag=f"{tag}cast{oc}_{p}", fsz=2048)
                            tsrc, tsrc_rg = qtc["q"][0], qtc["q"][1]

                            def mk_evac(oc, p, dv, sfx, sreg, par):
                                def evac(bsel, gi, n, brg):
                                    ev = wtev[par + (NWR if sfx == "p" else 0)]
                                    evrg = f"wtev{par + (NWR if sfx == 'p' else 0)}"
                                    sch.add("vector", (lambda ev=ev, bsel=bsel, n=n: (lambda e: e.tensor_copy(
                                        ev[:, 0:n * 128], pb[bsel][:].bitcast(BF16)[:, 0:n * 128])))(),
                                        reads=[brg], writes=[evrg], tag=f"{tag}tev{sfx}{oc}_{p}_{gi}", fsz=n * 128)
                                    sch.dma("gpsimd",
                                            dv[:, p * 16 + gi:p * 16 + gi + n, oc * 128:(oc + 1) * 128],
                                            ev[:, 0:n * 128].rearrange("p (c q) -> p c q", q=128),
                                            reads=[evrg], writes=[f"{sreg}#{oc % 4}"], tag=f"{tag}st{sfx}{oc}_{p}_{gi}")
                                return evac

                            cols = [c * 128 for c in range(16)]
                            transpose_pack(tsrc, tsrc_rg, cols, [4, 5], mk_evac(oc, p, dview, "n", screg, par),
                                           f"{tag}t{oc}_{p}_")
                            if dview2 is not None:
                                transpose_pack(tsrc, tsrc_rg, cols, [6, 7], mk_evac(oc, p, dview2, "p", screg2, par),
                                               f"{tag}tp{oc}_{p}_", perm=pswap)

                prep_weight(W1, sc_qW1T, KCH, D, True, "pw1", "scqW1T")
                prep_weight(W2, sc_qW2T, KC, DH, True, "pw2", "scqW2T")
            sch.fence()

        if "attn" in phases:
            ap_ = ExitStack()
            with ap_:
                xln_fm = sb("xln_fm", [128, KC * T1], BF16, ap_)
                attnout = sb("attnout", [128, NH * R], BF16, ap_)

                a1 = ExitStack()
                with a1:
                    xc = [sb(f"a_xc{i}", [128, D], F32, a1) for i in range(2)]
                    xlntm = [sb(f"a_xt{i}", [128, D], BF16, a1) for i in range(2)]
                    lstats = sb("a_ls", [128, 16], F32, a1)
                    sqs1 = sb("a_sq", [128, D], F32, a1)

                    def a1_chain(tc):
                        xt = xc[tc % 2]; xtrg = f"a_xc{tc % 2}"
                        lt = xlntm[tc % 2]; ltrg = f"a_xt{tc % 2}"
                        scol = (tc % 2) * 8
                        sch.begin_collect()
                        sch.dma("sync", xt[:], x_in[tc * 128:(tc + 1) * 128, :], writes=[xtrg], tag=f"a_lx{tc}")
                        ln_chunk(xt[:, 0:D], xtrg, lstats, f"a_ls{tc % 2}", scol, sqs1[:], "a_sq", f"a_l{tc}")
                        ln_apply(xt[:, 0:D], xtrg, lstats, f"a_ls{tc % 2}", scol, lt[:, 0:D], ltrg, f"a_l{tc}")
                        chain = sch.end_collect()

                        def do_T(tc=tc, lt=lt, ltrg=ltrg):
                            def evx(bsel, gi, n, brg):
                                sch.add("vector", (lambda bsel=bsel, gi=gi, n=n: (lambda e: e.tensor_copy(
                                    xln_fm.rearrange("p (c t) -> p c t", t=T1)[:, gi:gi + n, tc * 128:(tc + 1) * 128],
                                    pb[bsel][:].bitcast(BF16)[:, 0:n * 128].rearrange("p (c q) -> p c q", q=128))))(),
                                    reads=[brg], writes=["xln_fm"], tag=f"a_xe{tc}_{gi}")
                            transpose_pack(lt, ltrg, [c * 128 for c in range(KC)], [4, 5], evx, f"a_xT{tc}_")
                        chain.append(do_T)
                        return chain

                    for pi in range(0, NTC, 2):
                        chains = [a1_chain(pi), a1_chain(pi + 1)]
                        for st in zip_longest(*chains):
                            for t in st:
                                if t is not None:
                                    t()
                sch.fence()
                dbg("dbg_xln", xln_fm[:], "xln_fm", [128, KC * T1], BF16)

                pass

                a2 = ExitStack()
                with a2:
                    tabsb = sb("tabsb", [128, 4 * T1], BF16, a2)
                    maskm = sb("maskm", [128, 8 * WIN], BF16, a2)
                    maskb0 = sb("maskb0", [128, 4 * WIN], BF16, a2)
                    sch.dma("sync", tabsb[:], tabs, writes=["tabsb"], tag="ld_tabs")
                    sch.dma("sync", maskm[:], maskM, writes=["maskm"], tag="ld_maskm")
                    sch.dma("sync", maskb0[:], maskB0, writes=["maskb0"], tag="ld_maskb0")
                    q_fm = sb("q_fm", [128, 4 * R], BF16, a2)
                    k_fm = sb("k_fm", [128, 4 * T1], BF16, a2)
                    v_tm = sb("v_tm", [128, NTC * 512], BF16, a2)
                    p_sbr = [sb(f"p_sb{i}", [128, 8 * WIN], BF16, a2) for i in range(2)]
                    rt = [sb(f"a_rt{i}", [128, 512], BF16, a2) for i in range(4)]
                    ssumr = [sb(f"a_ss{i}", [1, 512], F32, a2) for i in range(2)]
                    srecipr = [sb(f"a_sr{i}", [1, 512], F32, a2) for i in range(2)]
                    recipBr = [sb(f"a_rB{i}", [128, 512], F32, a2) for i in range(2)]
                    wcb2 = [strip1[:, 0:D], strip1[:, D:2 * D]]

                    SWAPM = []
                    for i in range(0, 32, 2):
                        SWAPM += [i + 1, i]
                    for g in range(4):
                        for (wsrc, dst, dstrg, nt, toff, ctab, stab) in (
                            (aWq, q_fm, "q_fm", 2, HL, 2 * T1, 3 * T1),
                            (aWk, k_fm, "k_fm", 3, 0, 0, T1),
                        ):
                            mk_wstrip(strip0, "strip0", wsrc, g, (3, 4), wcb2, "a_wc", f"a_w{dstrg}{g}")
                            for ho in range(4):
                                for tt in range(nt):
                                    t0 = toff + tt * 512
                                    bank = (ho * nt + tt) % 2
                                    for kc2 in range(KC):
                                        sch.add("tensor", (lambda bank=bank, kc2=kc2, ho=ho, t0=t0: (lambda e: e.matmul(
                                            pb[bank][:],
                                            strip0[:, kc2 * 512 + ho * 128: kc2 * 512 + (ho + 1) * 128],
                                            xln_fm[:, kc2 * T1 + t0: kc2 * T1 + t0 + 512],
                                            start=(kc2 == 0), stop=(kc2 == KC - 1))))(),
                                            reads=(["strip0", "xln_fm"] + ([PB[bank]] if kc2 else [])),
                                            writes=[PB[bank]], tag=f"a_p{dstrg}{g}_{ho}_{tt}_{kc2}")
                                    ctt = ctab + t0
                                    stt_ = stab + t0
                                    dcol = ho * (R if dstrg == "q_fm" else T1) + tt * 512
                                    pr = ((ho * nt + tt) % 2) * 2
                                    r0, r1_ = rt[pr], rt[pr + 1]
                                    r0n, r1n = f"a_rt{pr}", f"a_rt{pr + 1}"
                                    sch.add("vector", (lambda bank=bank, r0=r0: (lambda e: e.tensor_copy(
                                        r0[:], pb[bank][:])))(),
                                        reads=[PB[bank]], writes=[r0n], tag=f"a_rv{dstrg}{g}_{ho}_{tt}")
                                    sch.add("vector", (lambda r0=r0, r1_=r1_: (lambda e: e.stream_shuffle(
                                        r1_[:], r0[:], SWAPM)))(),
                                        reads=[r0n], writes=[r1n], tag=f"a_rs{dstrg}{g}_{ho}_{tt}")
                                    sch.add("vector", (lambda ctt=ctt, r0=r0: (lambda e: e.tensor_tensor(
                                        r0[:], r0[:], tabsb[:, ctt:ctt + 512], ALU.mult)))(),
                                        reads=[r0n, "tabsb"], writes=[r0n], tag=f"a_r1{dstrg}{g}_{ho}_{tt}")
                                    sch.add("vector", (lambda stt_=stt_, r1_=r1_: (lambda e: e.tensor_tensor(
                                        r1_[:], r1_[:], tabsb[:, stt_:stt_ + 512], ALU.mult)))(),
                                        reads=[r1n, "tabsb"], writes=[r1n], tag=f"a_r2{dstrg}{g}_{ho}_{tt}")
                                    sch.add("vector", (lambda dst=dst, dcol=dcol, r0=r0, r1_=r1_: (lambda e: e.tensor_tensor(
                                        dst[:, dcol:dcol + 512], r0[:], r1_[:], ALU.add)))(),
                                        reads=[r0n, r1n], writes=[dstrg], tag=f"a_r3{dstrg}{g}_{ho}_{tt}")

                        mk_wstrip(strip0, "strip0", aWv, g, (3, 4), wcb2, "a_wc", f"a_wv{g}")
                        for tc in range(NTC):
                            for kc2 in range(KC):
                                sch.add("tensor", (lambda kc2=kc2, tc=tc: (lambda e: e.matmul(
                                    pb[2][:],
                                    xln_fm[:, kc2 * T1 + tc * 128: kc2 * T1 + (tc + 1) * 128],
                                    strip0[:, kc2 * 512:(kc2 + 1) * 512],
                                    start=(kc2 == 0), stop=(kc2 == KC - 1))))(),
                                    reads=(["strip0", "xln_fm"] + ([PB[2]] if kc2 else [])),
                                    writes=[PB[2]], tag=f"a_pv{g}_{tc}_{kc2}")
                            sch.add("vector", (lambda tc=tc: (lambda e: e.tensor_copy(
                                v_tm[:, tc * 512:(tc + 1) * 512], pb[2][:])))(),
                                reads=[PB[2]], writes=["v_tm"], tag=f"a_ve{g}_{tc}")

                        if g == 0:
                            dbg("dbg_q", q_fm[:], "q_fm", [128, 4 * R], BF16)
                            dbg("dbg_k", k_fm[:], "k_fm", [128, 4 * T1], BF16)
                            dbg("dbg_v", v_tm[:], "v_tm", [128, NTC * 512], BF16)
                        def u_stage1(ho, b):
                            h = g * 4 + ho
                            up = b
                            p_sb = p_sbr[up]; p_rg = f"p_sb{up}"
                            for jc in range(8):
                                bank = 3 + (jc % 2)
                                kcol = ho * T1 + b * 512 + jc * 128
                                qcol = ho * R + b * 512
                                sch.add("tensor", (lambda bank=bank, kcol=kcol, qcol=qcol: (lambda e: e.matmul(
                                    pb[bank][:], k_fm[:, kcol:kcol + 128], q_fm[:, qcol:qcol + 512],
                                    start=True, stop=False)))(),
                                    reads=["k_fm", "q_fm"], writes=[PB[bank]], tag=f"a_s{h}_{b}_{jc}")
                                if b == 0 and jc < 4:
                                    mt, mtrg, mcol = maskb0, "maskb0", jc * 512
                                else:
                                    mt, mtrg, mcol = maskm, "maskm", jc * 512
                                sch.add("tensor", (lambda bank=bank, mt=mt, mcol=mcol: (lambda e: e.matmul(
                                    pb[bank][:], ident[:], mt[:, mcol:mcol + 512],
                                    start=False, stop=True)))(),
                                    reads=[mtrg, "ident", PB[bank]], writes=[PB[bank]], tag=f"a_m{h}_{b}_{jc}")
                                sch.add("scalar", (lambda bank=bank, jc=jc, p_sb=p_sb: (lambda e: e.activation(
                                    p_sb[:, jc * 512:(jc + 1) * 512], pb[bank][:], AF.Exp,
                                    bias=CEXP, scale=1.0)))(),
                                    reads=[PB[bank], "consts"], writes=[p_rg], tag=f"a_e{h}_{b}_{jc}")

                        def u_sums(ho, b, su_b):
                            h = g * 4 + ho
                            up = b
                            p_sb = p_sbr[up]; p_rg = f"p_sb{up}"
                            ssum = ssumr[up]; ss_rg = f"a_ss{up}"
                            for jc in range(8):
                                sch.add("tensor", (lambda jc=jc, p_sb=p_sb, su_b=su_b: (lambda e: e.matmul(
                                    pb[su_b][0:1, :], ones_c[:], p_sb[:, jc * 512:(jc + 1) * 512],
                                    start=(jc == 0), stop=(jc == 7))))(),
                                    reads=(["ones_c", p_rg] + ([PB[su_b]] if jc else [])),
                                    writes=[PB[su_b]], tag=f"a_su{h}_{b}_{jc}")
                            sch.add("vector", (lambda ssum=ssum, su_b=su_b: (lambda e: e.tensor_copy(
                                ssum[:], pb[su_b][0:1, :])))(),
                                reads=[PB[su_b]], writes=[ss_rg], tag=f"a_se{h}_{b}")
                            sch.add("scalar", (lambda ssum=ssum, srecip=srecipr[up]: (lambda e: e.activation(
                                srecip[:], ssum[:], AF.Ln, bias=0.0, scale=1.0)))(),
                                reads=[ss_rg, "consts"], writes=[f"a_sr{up}"], tag=f"a_rl{h}_{b}")
                            sch.add("scalar", (lambda srecip=srecipr[up]: (lambda e: e.activation(
                                srecip[:], srecip[:], AF.Exp, bias=0.0, scale=-1.0)))(),
                                reads=[f"a_sr{up}", "consts"], writes=[f"a_sr{up}"], tag=f"a_re{h}_{b}")

                        def u_stage2(ho, b, rb_b, av_b):
                            h = g * 4 + ho
                            up = b
                            p_sb = p_sbr[up]; p_rg = f"p_sb{up}"
                            ssum = ssumr[up]; ss_rg = f"a_ss{up}"
                            srecip = srecipr[up]; sr_rg = f"a_sr{up}"
                            recipB = recipBr[up]; rB_rg = f"a_rB{up}"
                            sch.add("tensor", (lambda srecip=srecip, rb_b=rb_b: (lambda e: e.matmul(
                                pb[rb_b][:], ones_rf[:], srecip[:], start=True, stop=True)))(),
                                reads=[sr_rg, "ones_rf"], writes=[PB[rb_b]], tag=f"a_rb{h}_{b}")
                            sch.add("vector", (lambda recipB=recipB, rb_b=rb_b: (lambda e: e.tensor_copy(
                                recipB[:], pb[rb_b][:])))(),
                                reads=[PB[rb_b]], writes=[rB_rg], tag=f"a_rbe{h}_{b}")
                            for jc in range(8):
                                vcol = (4 * b + jc) * 512 + ho * 128
                                sch.add("tensor", (lambda jc=jc, vcol=vcol, p_sb=p_sb, av_b=av_b: (lambda e: e.matmul(
                                    pb[av_b][:], v_tm[:, vcol:vcol + 128], p_sb[:, jc * 512:(jc + 1) * 512],
                                    start=(jc == 0), stop=(jc == 7))))(),
                                    reads=(["v_tm", p_rg] + ([PB[av_b]] if jc else [])),
                                    writes=[PB[av_b]], tag=f"a_av{h}_{b}_{jc}")
                            acol = h * R + b * 512
                            sch.add("vector", (lambda acol=acol, recipB=recipB, av_b=av_b: (lambda e: e.tensor_tensor(
                                attnout[:, acol:acol + 512], pb[av_b][:], recipB[:], ALU.mult)))(),
                                reads=[PB[av_b], rB_rg], writes=["attnout"], tag=f"a_ao{h}_{b}")
                            if h == 0 and b == 1:
                                dbg("dbg_p", p_sb[:], p_rg, [128, 8 * WIN], BF16)
                                dbg("dbg_sum", ssum[:], ss_rg, [1, 512], F32)
                                dbg("dbg_recipB", recipB[:], rB_rg, [128, 512], F32)

                        for ho in range(4):
                            u_stage1(ho, 0)
                            u_stage1(ho, 1)
                            u_sums(ho, 0, 5)
                            u_sums(ho, 1, 0)
                            u_stage2(ho, 0, 6, 7)
                            u_stage2(ho, 1, 1, 2)
                sch.fence()
                dbg("dbg_attnout", attnout[:], "attnout", [128, NH * R], BF16)

                a3 = ExitStack()
                with a3:
                    xres = [sb(f"a_xr{i}", [128, D], F32, a3) for i in range(NTO)]
                    wcb3 = [sb(f"a_w3c{i}", [128, D], BF16, a3) for i in range(2)]
                    for tch in range(NTO):
                        sch.dma("sync", xres[tch][:], x_in[HL + tch * 128: HL + (tch + 1) * 128, :],
                                writes=[f"a_xr{tch}"], tag=f"a_lr{tch}")
                    for ot in range(4):
                        wbuf = strip0 if ot % 2 == 0 else strip1
                        wrg = "strip0" if ot % 2 == 0 else "strip1"
                        mk_wstrip(wbuf, wrg, aWo, ot, (6, 7), wcb3, "a_w3c", f"a_wo{ot}")
                        for tch in range(NTO):
                            bank = tch % 4
                            for ic in range(KC):
                                sch.add("tensor", (lambda bank=bank, ic=ic, tch=tch, wbuf=wbuf: (lambda e: e.matmul(
                                    pb[bank][:], attnout[:, ic * R + tch * 128: ic * R + (tch + 1) * 128],
                                    wbuf[:, ic * 512:(ic + 1) * 512],
                                    start=(ic == 0), stop=(ic == KC - 1))))(),
                                    reads=(["attnout", wrg] + ([PB[bank]] if ic else [])),
                                    writes=[PB[bank]], tag=f"a_o{ot}_{tch}_{ic}")
                            sch.add("vector", (lambda bank=bank, tch=tch, ot=ot: (lambda e: e.tensor_tensor(
                                xres[tch][:, ot * 512:(ot + 1) * 512], pb[bank][:],
                                xres[tch][:, ot * 512:(ot + 1) * 512], ALU.add)))(),
                                reads=[PB[bank], f"a_xr{tch}"], writes=[f"a_xr{tch}"], tag=f"a_or{ot}_{tch}")
                    for tch in range(NTO):
                        sch.dma("gpsimd", sc_X[tch * 128:(tch + 1) * 128, :], xres[tch][:],
                                reads=[f"a_xr{tch}"], writes=[f"scX{tch}_{q}" for q in range(4)],
                                tag=f"a_sx{tch}")
            sch.fence()

        def ffn_phase(idx, x_src, final):
            fp_ = ExitStack()
            with fp_:
                pre = f"f{idx}"
                gamB = sb(pre + "gamB", [128, D], BF16, fp_)
                betB = sb(pre + "betB", [128, D], BF16, fp_)
                b1t = [sb(pre + f"b1t{i}", [1, 512], BF16, fp_) for i in range(2)]
                b2t = [sb(pre + f"b2t{i}", [1, 512], BF16, fp_) for i in range(2)]
                xc2 = [sb(pre + f"xc{i}", [128, D], F32, fp_) for i in range(2)]
                xb2 = [sb(pre + f"xb{i}", [128, D], BF16, fp_) for i in range(2)]
                t0b = sb(pre + "t0", [128, D], F32, fp_)
                lstat = sb(pre + "ls", [128, 16], F32, fp_)
                lnq_fm = sb(pre + "lnq", [128, KC * 512], BF16, fp_)
                hq_fm = sb(pre + "hq", [128, KCH * 512], BF16, fp_)
                hg = [sb(pre + f"hg{i}", [128, 4 * 512], BF16, fp_) for i in range(2)]
                res = [sb(pre + f"res{i}", [128, D], BF16, fp_) for i in range(4)]
                xs2 = [sb(pre + f"xs{i}", [128, 512], F32, fp_) for i in range(2)]
                u2r = [sb(pre + f"u2{i}", [128, 512], F32, fp_) for i in range(2)]
                f2r = [sb(pre + f"f2{i}", [128, 512], BF16, fp_) for i in range(2)]
                qt2 = [make_qt(pre + "q0", D, fp_), make_qt(pre + "q1", D, fp_)]

                rw_ = ExitStack()
                with rw_:
                    rows = [sb(pre + f"lnr{i}", [1, 512], BF16, rw_) for i in range(2)]
                    for nm, srcv, dst, dstrg in (("g", lnw, gamB, pre + "gamB"),
                                                 ("b", lnb, betB, pre + "betB")):
                        for j in range(4):
                            row = rows[j % 2]; rowrg = pre + f"lnr{j % 2}"
                            sch.dma("gpsimd", row[:], srcv.rearrange("(a b) -> a b", a=1)[0:1, j * 512:(j + 1) * 512],
                                    writes=[rowrg], tag=f"{pre}ld{nm}{j}")
                            sch.add("tensor", (lambda row=row: (lambda e: e.matmul(
                                pb[0][:], ones_r[:], row[0:1, :], start=True, stop=True)))(),
                                reads=[rowrg, "ones_r"], writes=[PB[0]], tag=f"{pre}gb{nm}{j}")
                            sch.add("vector", (lambda dst=dst, j=j: (lambda e: e.tensor_copy(
                                dst[:, j * 512:(j + 1) * 512], pb[0][:])))(),
                                reads=[PB[0]], writes=[dstrg], tag=f"{pre}ge{nm}{j}")
                sch.fence()

                scT_w1 = sc_qW1T.rearrange("(kc p) o -> p kc o", p=128)
                scT_w2 = sc_qW2T.rearrange("(kc p) o -> p kc o", p=128)

                def ln_chain(half, tci, tbanks):
                    tch = half * 4 + tci
                    xt = xc2[tci % 2]; xtrg = pre + f"xc{tci % 2}"
                    xbt = xb2[tci % 2]; xbrg = pre + f"xb{tci % 2}"
                    scol = (tci % 2) * 8
                    lsrg = pre + f"ls{tci % 2}"
                    qtc = qt2[tci % 2]
                    sch.begin_collect()
                    sch.dma("sync", xt[:], x_src[tch * 128:(tch + 1) * 128, :], writes=[xtrg],
                            reads=[f"scX{tch}_{q}" for q in range(4)], tag=f"{pre}lx{tch}")
                    ln_chunk(xt[:, 0:D], xtrg, lstat, lsrg, scol, t0b[:], pre + "t0", f"{pre}l2{tch}",
                             save_to=(ln2stats, "ln2stats", tch * 2))
                    ln_apply(xt[:, 0:D], xtrg, lstat, lsrg, scol, xbt[:, 0:D], xbrg, f"{pre}l2{tch}")
                    ln_chunk(xbt[:, 0:D], xbrg, lstat, lsrg, scol, t0b[:], pre + "t0", f"{pre}l3{tch}")
                    sch.add("vector", (lambda xbt=xbt, xt=xt, scol=scol: (lambda e: e.tensor_scalar(
                        xt[:], xbt[:], lstat[:, scol:scol + 1], lstat[:, scol + 1:scol + 2],
                        ALU.subtract, ALU.mult)))(),
                        reads=[xbrg, lsrg], writes=[xtrg], tag=f"{pre}l3a{tch}")
                    sch.add("vector", (lambda xt=xt: (lambda e: e.tensor_tensor(
                        xt[:], xt[:], gamB[:], ALU.mult)))(),
                        reads=[xtrg, pre + "gamB"], writes=[xtrg], tag=f"{pre}l3b{tch}", fsz=D)
                    sch.add("vector", (lambda xt=xt, xbt=xbt: (lambda e: e.tensor_tensor(
                        xbt[:], xt[:], betB[:], ALU.add)))(),
                        reads=[xtrg, pre + "betB"], writes=[xbrg], tag=f"{pre}l3c{tch}", fsz=D)
                    quant_chunk(xbt[:, 0:D], xbrg, D, qtc, f"{pre}qz{tch}")
                    chain = sch.end_collect()

                    def do_T(tci=tci, tch=tch, qtc=qtc, tbanks=tbanks):
                        def ev2(bsel, gi, n, brg):
                            sch.add("vector", (lambda bsel=bsel, gi=gi, n=n: (lambda e: e.tensor_copy(
                                lnq_fm.rearrange("p (c t) -> p c t", t=512)[:, gi:gi + n, tci * 128:(tci + 1) * 128],
                                pb[bsel][:].bitcast(BF16)[:, 0:n * 128].rearrange("p (c q) -> p c q", q=128))))(),
                                reads=[brg], writes=[pre + f"lnq{tci}"], tag=f"{pre}lqe{tch}_{gi}", fsz=n * 128)
                        transpose_pack(qtc["q"][0], qtc["q"][1], [c * 128 for c in range(KC)], tbanks,
                                       ev2, f"{pre}lqT{tch}_")
                    chain.append(do_T)
                    return chain

                def ln_stage(half, tbanks=(4, 5)):
                    for pi in range(2):
                        chains = [ln_chain(half, pi * 2 + j, tbanks) for j in (0, 1)]
                        for st in zip_longest(*chains):
                            for t in st:
                                if t is not None:
                                    t()

                def w1_stage(half):
                    pend_T = []
                    for ot in range(16):
                        wbuf = strip0 if ot % 2 == 0 else strip1
                        wrg = "strip0" if ot % 2 == 0 else "strip1"
                        sch.dma("sync" if ot % 2 == 0 else "gpsimd",
                                wbuf[:].rearrange("p (kc o) -> p kc o", o=512),
                                scT_w1[:, :, ot * 512:(ot + 1) * 512], reads=[f"scqW1T#{i}" for i in range(4)], writes=[wrg], tag=f"{pre}w1l{half}_{ot}")
                        hgb = hg[ot % 2]; hgrg = pre + f"hg{ot % 2}"
                        bt = b1t[ot % 2]; btrg = pre + f"b1t{ot % 2}"
                        sch.dma("gpsimd", bt[:], b1v.rearrange("(a b) -> a b", a=1)[0:1, ot * 512:(ot + 1) * 512],
                                writes=[btrg], tag=f"{pre}w1bl{half}_{ot}")
                        for tci in range(4):
                            bank = tci
                            for kc2 in range(KC):
                                sch.add("tensor", (lambda bank=bank, kc2=kc2, tci=tci, wbuf=wbuf: (lambda e: e.matmul(
                                    pb[bank][:], lnq_fm[:, kc2 * 512 + tci * 128: kc2 * 512 + (tci + 1) * 128],
                                    wbuf[:, kc2 * 512:(kc2 + 1) * 512],
                                    start=(kc2 == 0), stop=False)))(),
                                    reads=([pre + f"lnq{tci}", wrg] + ([PB[bank]] if kc2 else [])),
                                    writes=[PB[bank]], tag=f"{pre}w1m{half}_{ot}_{tci}_{kc2}")
                            sch.add("tensor", (lambda bank=bank, bt=bt: (lambda e: e.matmul(
                                pb[bank][:], ones_r[:], bt[:],
                                start=False, stop=True)))(),
                                reads=[btrg, "ones_r", PB[bank]], writes=[PB[bank]], tag=f"{pre}w1b{half}_{ot}_{tci}")
                            fb = f2r[tci % 2]; fbrg = pre + f"f2{tci % 2}"
                            sch.add("vector", (lambda bank=bank, fb=fb: (lambda e: e.tensor_copy(
                                fb[:], pb[bank][:])))(),
                                reads=[PB[bank]], writes=[fbrg], tag=f"{pre}w1e{half}_{ot}_{tci}")
                            sch.add("scalar", (lambda hgb=hgb, tci=tci, fb=fb: (lambda e: e.activation(
                                hgb[:, tci * 512:(tci + 1) * 512], fb[:], AF.Gelu_apprx_tanh,
                                bias=0.0, scale=1.0)))(),
                                reads=[fbrg, "consts"], writes=[hgrg], tag=f"{pre}w1g{half}_{ot}_{tci}")
                        qtc = qt2[ot % 2]
                        quant_chunk(hgb[:, 0:2048], hgrg, 2048, qtc, f"{pre}hz{half}_{ot}")

                        def hq_transposes(ot=ot, qtc=qtc):
                            for grp2 in range(2):
                                bsel = 4 + grp2
                                for oc_ in range(2):
                                    for tci_ in range(4):
                                        col = tci_ * 512 + (grp2 * 2 + oc_) * 128
                                        slot = oc_ * 4 + tci_
                                        sch.add("tensor", (lambda bsel=bsel, slot=slot, col=col, qtc=qtc: (lambda e: e.transpose(
                                            pb[bsel][:].bitcast(BF16)[:, slot * 128:(slot + 1) * 128],
                                            qtc["q"][0][:, col:col + 128], ident[:])))(),
                                            reads=[qtc["q"][1], "ident"], writes=[PB[bsel]], tag=f"{pre}hT{half}_{ot}_{grp2}_{oc_}_{tci_}")
                                    dest = hq_fm[:, (ot * 4 + grp2 * 2) * 512: (ot * 4 + grp2 * 2 + 2) * 512]
                                    sch.add("vector", (lambda bsel=bsel, dest=dest: (lambda e: e.tensor_copy(
                                        dest.rearrange("p (a b q) -> p a b q", b=4, q=128),
                                        pb[bsel][:].bitcast(BF16)[:, 0:1024].rearrange("p (a b q) -> p a b q", b=4, q=128))))(),
                                        reads=[PB[bsel]], writes=[pre + "hq"], tag=f"{pre}hqe{half}_{ot}_{grp2}")
                        pend_T.append(hq_transposes)
                        if len(pend_T) > 1:
                            pend_T.pop(0)()
                    while pend_T:
                        pend_T.pop(0)()

                def w2_stage(half, hook=None):
                    tbase = half * 4
                    for tci in range(4):
                        tch = tbase + tci
                        xt = xc2[tci % 2]; xtrg = pre + f"xc{tci % 2}"
                        sch.dma("sync", xt[:], x_src[tch * 128:(tch + 1) * 128, :], writes=[xtrg],
                            reads=[f"scX{tch}_{q}" for q in range(4)], tag=f"{pre}rx{tch}")
                        sch.add("vector", (lambda xt=xt, tci=tci, tch=tch: (lambda e: e.tensor_scalar(
                            res[tci][:], xt[:], ln2stats[:, tch * 2:tch * 2 + 1], ln2stats[:, tch * 2 + 1:tch * 2 + 2],
                            ALU.subtract, ALU.mult)))(),
                            reads=[xtrg, "ln2stats"], writes=[pre + f"res{tci}"], tag=f"{pre}rr{tch}", fsz=D)

                    for o2 in range(4):
                        bt2 = b2t[o2 % 2]; bt2rg = pre + f"b2t{o2 % 2}"
                        sch.dma("gpsimd", bt2[:], b2v.rearrange("(a b) -> a b", a=1)[0:1, o2 * 512:(o2 + 1) * 512],
                                writes=[bt2rg], tag=f"{pre}w2bl{half}_{o2}")
                        for kg in range(4):
                            wbuf = strip0 if kg % 2 == 0 else strip1
                            wrg = "strip0" if kg % 2 == 0 else "strip1"
                            sch.dma("sync" if kg % 2 == 0 else "gpsimd",
                                    wbuf[:].rearrange("p (kc o) -> p kc o", o=512),
                                    scT_w2[:, kg * 16:(kg + 1) * 16, o2 * 512:(o2 + 1) * 512],
                                    reads=[f"scqW2T#{i}" for i in range(4)], writes=[wrg], tag=f"{pre}w2l{half}_{o2}_{kg}")
                            for tci in range(4):
                                bank = 4 + tci
                                for kk in range(16):
                                    kc2 = kg * 16 + kk
                                    sch.add("tensor", (lambda bank=bank, kc2=kc2, kk=kk, tci=tci, wbuf=wbuf: (lambda e: e.matmul(
                                        pb[bank][:], hq_fm[:, kc2 * 512 + tci * 128: kc2 * 512 + (tci + 1) * 128],
                                        wbuf[:, kk * 512:(kk + 1) * 512],
                                        start=(kc2 == 0), stop=False)))(),
                                        reads=([pre + "hq", wrg] + ([PB[bank]] if kc2 else [])),
                                        writes=[PB[bank]], tag=f"{pre}w2m{half}_{o2}_{tci}_{kc2}")
                        for tci in range(4):
                            tch = tbase + tci
                            bank = 4 + tci
                            sch.add("tensor", (lambda bank=bank, bt2=bt2: (lambda e: e.matmul(
                                pb[bank][:], ones_r[:], bt2[:],
                                start=False, stop=True)))(),
                                reads=[bt2rg, "ones_r", PB[bank]], writes=[PB[bank]], tag=f"{pre}w2b{half}_{o2}_{tci}")
                            xsl = xs2[tci % 2]; xslrg = pre + f"xs{tci % 2}"
                            sch.dma("sync", xsl[:], x_src[tch * 128:(tch + 1) * 128, o2 * 512:(o2 + 1) * 512],
                                    reads=[f"scX{tch}_{o2}"], writes=[xslrg], tag=f"{pre}w2x{half}_{o2}_{tci}")
                            ub = u2r[tci % 2]; ubrg = pre + f"u2{tci % 2}"
                            fb = f2r[tci % 2]; fbrg = pre + f"f2{tci % 2}"
                            sch.add("vector", (lambda bank=bank, ub=ub: (lambda e: e.tensor_copy(ub[:], pb[bank][:])))(),
                                    reads=[PB[bank]], writes=[ubrg], tag=f"{pre}w2e{half}_{o2}_{tci}")
                            sch.add("vector", (lambda tci=tci, o2=o2, ub=ub, fb=fb: (lambda e: e.scalar_tensor_tensor(
                                fb[:], ub[:], 1.0, res[tci][:, o2 * 512:(o2 + 1) * 512], ALU.mult, ALU.add)))(),
                                reads=[ubrg, pre + f"res{tci}"], writes=[fbrg], tag=f"{pre}w2f{half}_{o2}_{tci}")
                            sch.add("vector", (lambda xsl=xsl, fb=fb: (lambda e: e.tensor_tensor(
                                xsl[:], fb[:], xsl[:], ALU.add)))(),
                                reads=[fbrg, xslrg], writes=[xslrg], tag=f"{pre}w2a{half}_{o2}_{tci}")
                            dst = out if final else sc_X
                            sch.dma("gpsimd", dst[tch * 128:(tch + 1) * 128, o2 * 512:(o2 + 1) * 512], xsl[:],
                                    reads=[xslrg], writes=([f"scX{tch}_{o2}"] if not final else []),
                                    tag=f"{pre}w2s{half}_{o2}_{tci}")
                        if hook is not None:
                            hook(o2)

                def ln1_hook(o2):
                    if o2 in (0, 1):
                        chains = [ln_chain(1, o2 * 2 + j, (0, 1)) for j in (0, 1)]
                        for st in zip_longest(*chains):
                            for t in st:
                                if t is not None:
                                    t()

                ln_stage(0)
                w1_stage(0)
                w2_stage(0, hook=ln1_hook)
                w1_stage(1)
                w2_stage(1)
            sch.fence()

        if "ffn1" in phases:
            ffn_phase(1, sc_X, final=False)

        if "attnb" in phases:
            bp = ExitStack()
            with bp:
                xln3_fm = sb("b_xln", [128, KC * R], BF16, bp)
                v3_fm = sb("b_v3", [128, KC * R], BF16, bp)
                xc3 = [sb(f"b_xc{i}", [128, D], F32, bp) for i in range(2)]
                xb3 = [sb(f"b_xb{i}", [128, D], BF16, bp) for i in range(2)]
                lstat3 = sb("b_ls", [128, 16], F32, bp)
                sqs3 = sb("b_sq", [128, D], F32, bp)
                xsl3 = [sb(f"b_xs{i}", [128, 512], F32, bp) for i in range(2)]

                def b_chain(tch):
                    xt = xc3[tch % 2]; xtrg = f"b_xc{tch % 2}"
                    xbt = xb3[tch % 2]; xbrg = f"b_xb{tch % 2}"
                    scol = (tch % 2) * 8
                    sch.begin_collect()
                    sch.dma("sync", xt[:], sc_X[tch * 128:(tch + 1) * 128, :], writes=[xtrg],
                            reads=[f"scX{tch}_{q}" for q in range(4)], tag=f"b_lx{tch}")
                    ln_chunk(xt[:, 0:D], xtrg, lstat3, f"b_ls{tch % 2}", scol, sqs3[:], "b_sq", f"b_l{tch}")
                    ln_apply(xt[:, 0:D], xtrg, lstat3, f"b_ls{tch % 2}", scol, xbt[:, 0:D], xbrg, f"b_l{tch}")
                    chain = sch.end_collect()

                    def do_T(tch=tch, xbt=xbt, xbrg=xbrg):
                        def evb(bsel, gi, n, brg):
                            sch.add("vector", (lambda bsel=bsel, gi=gi, n=n: (lambda e: e.tensor_copy(
                                xln3_fm.rearrange("p (c t) -> p c t", t=R)[:, gi:gi + n, tch * 128:(tch + 1) * 128],
                                pb[bsel][:].bitcast(BF16)[:, 0:n * 128].rearrange("p (c q) -> p c q", q=128))))(),
                                reads=[brg], writes=["b_xln"], tag=f"b_xe{tch}_{gi}")
                        transpose_pack(xbt, xbrg, [c * 128 for c in range(KC)], [4, 5], evb, f"b_xT{tch}_")
                    chain.append(do_T)
                    return chain

                for pi in range(0, NTO, 2):
                    chains = [b_chain(pi), b_chain(pi + 1)]
                    for st in zip_longest(*chains):
                        for t in st:
                            if t is not None:
                                t()

                wcbb = [sb(f"b_wc{i}", [128, D], BF16, bp) for i in range(2)]
                for g in range(4):
                    wbuf = strip0 if g % 2 == 0 else strip1
                    wrg = "strip0" if g % 2 == 0 else "strip1"
                    mk_wstrip(wbuf, wrg, bWqkv[2 * D:3 * D, :], g, (6, 7), wcbb, "b_wc", f"b_wv{g}")
                    for oo in range(4):
                        for tt in range(2):
                            bank = oo
                            for kc2 in range(KC):
                                sch.add("tensor", (lambda bank=bank, kc2=kc2, oo=oo, tt=tt, wbuf=wbuf: (lambda e: e.matmul(
                                    pb[bank][:], wbuf[:, kc2 * 512 + oo * 128: kc2 * 512 + (oo + 1) * 128],
                                    xln3_fm[:, kc2 * R + tt * 512: kc2 * R + tt * 512 + 512],
                                    start=(kc2 == 0), stop=(kc2 == KC - 1))))(),
                                    reads=([wrg, "b_xln"] + ([PB[bank]] if kc2 else [])),
                                    writes=[PB[bank]], tag=f"b_v{g}_{oo}_{tt}_{kc2}")
                            oc = g * 4 + oo
                            sch.add("vector", (lambda bank=bank, oc=oc, tt=tt: (lambda e: e.tensor_copy(
                                v3_fm[:, oc * R + tt * 512: oc * R + tt * 512 + 512], pb[bank][:])))(),
                                reads=[PB[bank]], writes=["b_v3"], tag=f"b_ve{g}_{oo}_{tt}")

                for ot in range(4):
                    wbuf = strip0 if ot % 2 == 0 else strip1
                    wrg = "strip0" if ot % 2 == 0 else "strip1"
                    mk_wstrip(wbuf, wrg, bWo, ot, (0, 1), wcbb, "b_wc", f"b_wo{ot}")
                    for tch in range(NTO):
                        bank = 4 + tch % 4
                        for ic in range(KC):
                            sch.add("tensor", (lambda bank=bank, ic=ic, tch=tch, wbuf=wbuf: (lambda e: e.matmul(
                                pb[bank][:], v3_fm[:, ic * R + tch * 128: ic * R + (tch + 1) * 128],
                                wbuf[:, ic * 512:(ic + 1) * 512],
                                start=(ic == 0), stop=(ic == KC - 1))))(),
                                reads=(["b_v3", wrg] + ([PB[bank]] if ic else [])),
                                writes=[PB[bank]], tag=f"b_o{ot}_{tch}_{ic}")
                        xsl = xsl3[tch % 2]; xslrg = f"b_xs{tch % 2}"
                        sch.dma("sync", xsl[:], sc_X[tch * 128:(tch + 1) * 128, ot * 512:(ot + 1) * 512],
                                reads=[f"scX{tch}_{ot}"], writes=[xslrg], tag=f"b_lxs{ot}_{tch}")
                        sch.add("vector", (lambda bank=bank, xsl=xsl: (lambda e: e.tensor_tensor(
                            xsl[:], pb[bank][:], xsl[:], ALU.add)))(),
                            reads=[PB[bank], xslrg], writes=[xslrg], tag=f"b_or{ot}_{tch}")
                        sch.dma("gpsimd", sc_X[tch * 128:(tch + 1) * 128, ot * 512:(ot + 1) * 512], xsl[:],
                                reads=[xslrg], writes=[f"scX{tch}_{ot}"], tag=f"b_sxs{ot}_{tch}")
            sch.fence()

        if "ffn2" in phases:
            ffn_phase(2, sc_X, final=True)
        else:
            tmpo = sb("tmpo", [128, D], F32)
            for tch in range(NTO):
                sch.dma("sync", tmpo[:], sc_X[tch * 128:(tch + 1) * 128, :], writes=["tmpo"], tag=f"d_l{tch}")
                sch.dma("sync", out[tch * 128:(tch + 1) * 128, :], tmpo[:], reads=["tmpo"], tag=f"d_s{tch}")

        sch.resolve()
        global LAST_SCHED
        LAST_SCHED = sch
        semkeys = set()
        for o in sch.ops:
            if o.done:
                semkeys.add(o.done[0])
            for k, _ in o.waits:
                semkeys.add(k)
        sems = {k: ctx.enter_context(nc.semaphore("_".join(str(p) for p in k))) for k in sorted(semkeys, key=str)}
        blk = ctx.enter_context(nc.Block())
        sch.emit(nc, blk, sems)

    return nc


def make_in_maps(inputs):
    x = np.ascontiguousarray(np.asarray(inputs["L0_attn_a_x"], np.float32).reshape(SEQ, D))
    ident, pswap = host_ident()
    consts = host_consts()
    base = dict(
        aWq=np.ascontiguousarray(inputs["L0_attn_a_Wq"], dtype=np.float32),
        aWk=np.ascontiguousarray(inputs["L0_attn_a_Wk"], dtype=np.float32),
        aWv=np.ascontiguousarray(inputs["L0_attn_a_Wv"], dtype=np.float32),
        aWo=np.ascontiguousarray(inputs["L0_attn_a_Wo"], dtype=np.float32),
        bWqkv=np.ascontiguousarray(inputs["L0_attn_b_weight_qkv"], dtype=np.float32),
        bWo=np.ascontiguousarray(inputs["L0_attn_b_weight_out"], dtype=np.float32),
        W1=np.ascontiguousarray(inputs["L0_ffn_W1"], dtype=np.float32),
        W2=np.ascontiguousarray(inputs["L0_ffn_W2"], dtype=np.float32),
        b1v=np.ascontiguousarray(inputs["L0_ffn_b1"], dtype=np.float32),
        b2v=np.ascontiguousarray(inputs["L0_ffn_b2"], dtype=np.float32),
        lnw=np.ascontiguousarray(inputs["L0_ffn_ln_w"], dtype=np.float32),
        lnb=np.ascontiguousarray(inputs["L0_ffn_ln_b"], dtype=np.float32),
        identI=ident, pswapI=pswap, constsI=consts,
    )
    maps = []
    for c in range(NCORES):
        p0 = c * R
        if c == 0:
            xs = np.concatenate([np.zeros((HL, D), np.float32), x[:R]], axis=0)
        else:
            xs = x[p0 - HL: p0 + R]
        mm_, b0 = host_masks(c)
        m = dict(base)
        m.update(x_in=np.ascontiguousarray(xs), tabs=host_tables(c), maskM=mm_, maskB0=b0)
        maps.append(m)
    return maps


def kernel(**inputs):
    maps = make_in_maps(inputs)
    nc = build_program()
    res = run_bass_kernel_spmd(nc, maps, core_ids=list(range(NCORES)))
    outs = [np.asarray(r["out"], np.float32) for r in res.results]
    return np.concatenate(outs, axis=0).reshape(1, SEQ, D)


# version 28
# speedup vs baseline: 1.0165x; 1.0165x over previous
import numpy as np
import ml_dtypes
from contextlib import ExitStack
from itertools import zip_longest

import concourse.bass as bass
import concourse.mybir as mybir
from concourse.bass_utils import run_bass_kernel_spmd

F32 = mybir.dt.float32
BF16 = mybir.dt.bfloat16
I32 = mybir.dt.int32
I8 = mybir.dt.int8
AF = mybir.ActivationFunctionType
ALU = mybir.AluOpType
AX = mybir.AxisListType
BF = ml_dtypes.bfloat16

SEQ, D, NH, HD, WIN = 8192, 2048, 16, 128, 512
NCORES = 8
R = SEQ // NCORES
HL = WIN
T1 = R + HL
DH = 8192
KC = D // 128
KCH = DH // 128
NTC = T1 // 128
NTO = R // 128
EPS = 1e-5
CEXP = -12.0
MASKV = -30000.0
CONSTS = [0.0, 1.0, CEXP, -0.5, -1.0, EPS]

ENGINES = ("tensor", "vector", "scalar", "gpsimd", "sync")
NDMASEM = 8
LAST_SCHED = None


class _Op:
    def __init__(self, eng, fn, reads, writes, dma, tag, idx, fence, fsz=0):
        self.eng, self.fn = eng, fn
        self.reads, self.writes = tuple(reads), tuple(writes)
        self.dma, self.tag, self.idx = dma, tag, idx
        self.fsz = fsz
        self.fence = fence
        self.needs_inc = False
        self.done = None
        self.deps = None
        self.waits = None


class Sched:
    def __init__(self):
        self.ops = []
        self._fence_deps = ()
        self._col = None

    def add(self, eng, fn, reads=(), writes=(), dma=False, tag="", fsz=0):
        assert eng in ENGINES, eng
        if self._col is not None:
            self._col.append(lambda: self.add(eng, fn, reads, writes, dma, tag, fsz))
            return None
        o = _Op(eng, fn, reads, writes, dma, tag, len(self.ops), self._fence_deps, fsz)
        self.ops.append(o)
        return o

    def begin_collect(self):
        assert self._col is None
        self._col = []

    def end_collect(self):
        c, self._col = self._col, None
        return c

    @staticmethod
    def _ordered(d, o):
        return (d.eng == o.eng and not d.dma and not o.dma and
                (d.eng == "tensor" or (d.fsz >= 2048 and d.eng in ("vector", "scalar"))))

    def dma(self, eng, out, in_, reads=(), writes=(), tag="", **kw):
        fn = (lambda out=out, in_=in_, kw=kw: (lambda e: e.dma_start(out=out, in_=in_, **kw)))()
        return self.add(eng, fn, reads=reads, writes=writes, dma=True, tag=tag)

    def fence(self):
        last = {}
        for o in self.ops:
            last[(o.eng, o.dma)] = o
        self._fence_deps = tuple(last.values())

    def resolve(self):
        last_w = {}
        readers = {}
        for o in self.ops:
            deps = set(o.fence)
            for r in o.reads:
                w = last_w.get(r)
                if w is not None:
                    deps.add(w)
            for w_ in o.writes:
                lw = last_w.get(w_)
                if lw is not None:
                    deps.add(lw)
                deps.update(readers.get(w_, ()))
            deps.discard(o)
            o.deps = deps
            for r in o.reads:
                readers.setdefault(r, []).append(o)
            for w_ in o.writes:
                last_w[w_] = o
                readers[w_] = []
        for o in self.ops:
            for d in o.deps:
                if not self._ordered(d, o):
                    d.needs_inc = True
        cnt = {}
        dma_idx = {}
        for o in self.ops:
            if o.dma:
                i = dma_idx.get(o.eng, 0)
                dma_idx[o.eng] = i + 1
                key = ("dma", o.eng, i % NDMASEM)
                cnt[key] = 16 * (i // NDMASEM + 1)
                o.done = (key, cnt[key])
                o.dma_semkey = key
            elif o.needs_inc:
                key = ("eng", o.eng)
                cnt[key] = cnt.get(key, 0) + 1
                o.done = (key, cnt[key])
        self.final_counts = dict(cnt)
        marks = {e: {} for e in ENGINES}
        for o in self.ops:
            wl = {}
            for d in o.deps:
                if self._ordered(d, o):
                    continue
                assert d.done is not None, (d.tag, o.tag)
                k, v = d.done
                if wl.get(k, 0) < v:
                    wl[k] = v
            ww = marks[o.eng]
            o.waits = []
            for k, v in sorted(wl.items(), key=lambda kv: str(kv[0])):
                if ww.get(k, 0) < v:
                    o.waits.append((k, v))
                    ww[k] = v

    def emit(self, nc, blk, sems):
        by_eng = {}
        for o in self.ops:
            by_eng.setdefault(o.eng, []).append(o)
        for eng in ENGINES:
            if eng not in by_eng:
                continue
            ops = by_eng[eng]

            def body(e, ops=ops, eng=eng):
                for o in ops:
                    for k, v in o.waits:
                        e.wait_ge(sems[k], v)
                    inst = o.fn(e)
                    if o.dma:
                        inst.then_inc(sems[o.dma_semkey], 16)
                    elif o.needs_inc:
                        inst.then_inc(sems[("eng", eng)], 1)
                if eng == "sync":
                    for k, c in sorted(self.final_counts.items(), key=lambda kv: str(kv[0])):
                        if k[0] == "dma":
                            e.wait_ge(sems[k], c)
            getattr(blk, eng)(body)


def host_tables(core_id):
    p0 = core_id * R
    pos = np.arange(p0 - HL, p0 + R, dtype=np.float64)
    pos = np.where(pos < 0, 0, pos).astype(np.float32)
    inv = (1.0 / (np.float32(10000.0) ** (np.arange(0, HD, 2, dtype=np.float32) / np.float32(HD)))).astype(np.float32)
    fr = (pos[None, :] * inv[:, None]).astype(np.float32)
    cos = np.cos(fr).astype(np.float32)
    sin = np.sin(fr).astype(np.float32)
    cosF = np.empty((HD, T1), np.float32)
    sinF = np.empty((HD, T1), np.float32)
    cosF[0::2] = cos; cosF[1::2] = cos
    sinF[0::2] = -sin; sinF[1::2] = sin
    s = np.float32(HD ** (-0.5))
    return np.concatenate([cosF, sinF, cosF * s, sinF * s], axis=1).astype(BF)


def host_masks(core_id):
    j = np.arange(128)[:, None]
    i = np.arange(WIN)[None, :]
    tiles = []
    for jc in range(8):
        jl = jc * 128 + j
        valid = (jl > i) & (jl <= i + WIN)
        tiles.append(np.where(valid, 0.0, MASKV).astype(np.float32))
    main = np.concatenate(tiles, axis=1)
    if core_id == 0:
        b0 = np.full((128, 4 * WIN), MASKV, np.float32)
    else:
        b0 = np.concatenate(tiles[:4], axis=1)
    return main.astype(BF), b0.astype(BF)


def host_ident():
    ident = np.eye(128, dtype=np.float32).astype(BF)
    sw = np.arange(128)
    sw = (sw // 2) * 2 + (1 - sw % 2)
    pswap = np.eye(128, dtype=np.float32)[sw].astype(BF)
    return ident, pswap


def host_consts():
    c = np.zeros((128, 8), np.float32)
    c[:, :len(CONSTS)] = np.array(CONSTS, np.float32)[None, :]
    return c


def build_program(phases=("wprep", "attn", "ffn1", "attnb", "ffn2"), debug=()):
    nc = bass.Bass("TRN2", target_bir_lowering=False)
    sch = Sched()

    def din(name, shape, dtype=F32):
        return nc.dram_tensor(name, list(shape), dtype, kind="ExternalInput").ap()

    x_in = din("x_in", [T1, D])
    aWq = din("aWq", [D, D]); aWk = din("aWk", [D, D])
    aWv = din("aWv", [D, D]); aWo = din("aWo", [D, D])
    bWqkv = din("bWqkv", [3 * D, D]); bWo = din("bWo", [D, D])
    W1 = din("W1", [DH, D]); W2 = din("W2", [D, DH])
    b1v = din("b1v", [DH]); b2v = din("b2v", [D])
    lnw = din("lnw", [D]); lnb = din("lnb", [D])
    tabs = din("tabs", [128, 4 * T1], BF16)
    maskM = din("maskM", [128, 8 * WIN], BF16)
    maskB0 = din("maskB0", [128, 4 * WIN], BF16)
    identI = din("identI", [128, 128], BF16)
    pswapI = din("pswapI", [128, 128], BF16)
    constsI = din("constsI", [128, 8])

    out = nc.dram_tensor("out", [R, D], F32, kind="ExternalOutput").ap()

    dbg_tensors = {}

    def dbg(name, ap, rg, shape, dtype):
        if name in debug:
            t = nc.dram_tensor(name, list(shape), dtype, kind="ExternalOutput").ap()
            dbg_tensors[name] = t
            sch.dma("sync", t, ap, reads=[rg], tag="dump_" + name)

    def scr(name, shape, dtype=BF16):
        kind = "ExternalOutput" if name in debug else "Internal"
        return nc.dram_tensor(name, list(shape), dtype, kind=kind).ap()

    sc_qW1T = scr("sc_qW1T", [D, DH]); sc_qW2T = scr("sc_qW2T", [DH, D])
    sc_X = scr("sc_X", [R, D], F32)

    ctx = ExitStack()
    with ctx:
        def sb(name, shape, dtype=BF16, stack=None):
            return (stack or ctx).enter_context(nc.sbuf_tensor(name, shape, dtype))

        pb = [ctx.enter_context(nc.psum_tensor(f"pb{i}", [128, 512], F32)) for i in range(8)]
        PB = [f"pb{i}" for i in range(8)]
        strip0 = sb("strip0", [128, 16 * 512], BF16)
        strip1 = sb("strip1", [128, 16 * 512], BF16)
        sconsts = sb("sconsts", [128, 8], F32)
        ident = sb("ident", [128, 128], BF16)
        pswap = sb("pswap", [128, 128], BF16)
        ones_r = sb("ones_r", [1, 128], BF16)
        ones_rf = sb("ones_rf", [1, 128], F32)
        ones_c = sb("ones_c", [128, 1], BF16)
        ln2stats = sb("ln2stats", [128, 2 * NTO], F32)

        for ci, cv in enumerate(CONSTS):
            nc.const_aps.register(F32, cv, sconsts[:, ci:ci + 1])

        sch.dma("sync", sconsts[:], constsI, writes=["consts"], tag="ld_consts")
        sch.dma("sync", ident[:], identI, writes=["ident"], tag="ld_ident")
        sch.dma("sync", pswap[:], pswapI, writes=["pswap"], tag="ld_pswap")
        sch.add("vector", lambda e: e.memset(ones_r[:], 1.0), writes=["ones_r"])
        sch.add("vector", lambda e: e.memset(ones_rf[:], 1.0), writes=["ones_rf"])
        sch.add("vector", lambda e: e.memset(ones_c[:], 1.0), writes=["ones_c"])

        def quant_chunk(src_ap, src_rg, F, qt, tag):
            G = F // 32
            (amax, amaxr), (ei, eir), (rcp, rcpr), (scl, sclr), (i8t, i8r), (qo, qor) = (
                qt["amax"], qt["ei"], qt["recip"], qt["scale"], qt["i8"], qt["q"])
            sch.add("vector", lambda e: e.tensor_reduce(
                amax[:, 0:G].unsqueeze(2), src_ap.rearrange("p (g b) -> p g b", b=32), AX.X, ALU.max,
                apply_absolute_value=True), reads=[src_rg], writes=[amaxr], tag=tag + "am")
            sch.add("vector", lambda e: e.tensor_single_scalar(
                ei[:, 0:G], amax[:, 0:G].bitcast(I32), 23, ALU.logical_shift_right),
                reads=[amaxr], writes=[eir], tag=tag + "e")
            sch.add("vector", lambda e: e.tensor_scalar(
                rcp[:, 0:G].bitcast(I32), ei[:, 0:G], -1, 260, ALU.mult, ALU.add),
                reads=[eir], writes=[rcpr], tag=tag + "r1")
            sch.add("vector", lambda e: e.tensor_single_scalar(
                rcp[:, 0:G].bitcast(I32), rcp[:, 0:G].bitcast(I32), 23, ALU.logical_shift_left),
                reads=[rcpr], writes=[rcpr], tag=tag + "r2")
            sch.add("vector", lambda e: e.tensor_scalar(
                scl[:, 0:G].bitcast(I32), ei[:, 0:G], 1, -6, ALU.mult, ALU.add),
                reads=[eir], writes=[sclr], tag=tag + "s1")
            sch.add("vector", lambda e: e.tensor_single_scalar(
                scl[:, 0:G].bitcast(I32), scl[:, 0:G].bitcast(I32), 23, ALU.logical_shift_left),
                reads=[sclr], writes=[sclr], tag=tag + "s2")
            rB = rcp[:, 0:G].unsqueeze(2).broadcast_to([128, G, 32])
            sB = scl[:, 0:G].unsqueeze(2).broadcast_to([128, G, 32])
            sch.add("vector", lambda e: e.tensor_tensor(
                i8t[:, 0:F].rearrange("p (g b) -> p g b", b=32),
                src_ap.rearrange("p (g b) -> p g b", b=32), rB, ALU.mult),
                reads=[src_rg, rcpr], writes=[i8r], tag=tag + "i8", fsz=F)
            sch.add("vector", lambda e: e.tensor_tensor(
                qo[:, 0:F].rearrange("p (g b) -> p g b", b=32),
                i8t[:, 0:F].rearrange("p (g b) -> p g b", b=32), sB, ALU.mult),
                reads=[i8r, sclr], writes=[qor], tag=tag + "q", fsz=F)

        def make_qt(pre, Fmax, stack, share_i8=None):
            G = Fmax // 32
            return dict(
                amax=(sb(pre + "amax", [128, G], F32, stack), pre + "amax"),
                ei=(sb(pre + "ei", [128, G], I32, stack), pre + "ei"),
                recip=(sb(pre + "recip", [128, G], F32, stack), pre + "recip"),
                scale=(sb(pre + "scale", [128, G], F32, stack), pre + "scale"),
                i8=(share_i8 if share_i8 is not None else (sb(pre + "i8", [128, Fmax], I8, stack), pre + "i8")),
                q=(sb(pre + "q", [128, Fmax], BF16, stack), pre + "q"),
            )

        def transpose_pack(src_ap, src_rg, chunk_cols, banks, evac, tag, perm=None):
            pmat, pmn = (perm, "pswap") if perm is not None else (ident, "ident")
            nb = len(banks)
            for gi in range(0, len(chunk_cols), 8):
                n = min(8, len(chunk_cols) - gi)
                bsel = banks[(gi // 8) % nb]
                brg = PB[bsel]
                for j in range(n):
                    col = chunk_cols[gi + j]
                    sch.add("tensor", (lambda bsel=bsel, j=j, col=col, pmat=pmat: (lambda e: e.transpose(
                        pb[bsel][:].bitcast(BF16)[:, j * 128:(j + 1) * 128],
                        src_ap[:, col:col + 128], pmat[:])))(),
                        reads=[src_rg, pmn], writes=[brg], tag=tag + f"T{gi + j}")
                evac(bsel, gi, n, brg)

        def ln_chunk(xap, xrg, stats, stats_rg, scol, sqscr, sqrg, tag, save_to=None):
            rN = 1.0 / D
            sch.add("vector", lambda e: e.tensor_reduce(
                stats[:, scol + 2:scol + 3], xap, AX.X, ALU.add),
                reads=[xrg], writes=[stats_rg], tag=tag + "sm")
            sch.add("scalar", lambda e: e.activation(sqscr, xap, AF.Square, bias=0.0, scale=1.0,
                                                     accum_out=stats[:, scol + 3:scol + 4]),
                    reads=[xrg, "consts"], writes=[stats_rg, sqrg], tag=tag + "sq")
            sch.add("vector", lambda e: e.tensor_single_scalar(
                stats[:, scol:scol + 1], stats[:, scol + 2:scol + 3], rN, ALU.mult),
                reads=[stats_rg], writes=[stats_rg], tag=tag + "mn")
            sch.add("vector", lambda e: e.tensor_tensor(
                stats[:, scol + 4:scol + 5], stats[:, scol:scol + 1], stats[:, scol:scol + 1], ALU.mult),
                reads=[stats_rg], writes=[stats_rg], tag=tag + "m2")
            sch.add("vector", lambda e: e.tensor_single_scalar(
                stats[:, scol + 5:scol + 6], stats[:, scol + 3:scol + 4], rN, ALU.mult),
                reads=[stats_rg], writes=[stats_rg], tag=tag + "v1")
            sch.add("vector", lambda e: e.tensor_tensor(
                stats[:, scol + 1:scol + 2], stats[:, scol + 5:scol + 6], stats[:, scol + 4:scol + 5],
                ALU.subtract),
                reads=[stats_rg], writes=[stats_rg], tag=tag + "vr")
            if save_to is not None:
                sv, svrg, svcol = save_to
                sch.add("vector", lambda e: e.tensor_copy(sv[:, svcol:svcol + 1], stats[:, scol:scol + 1]),
                        reads=[stats_rg], writes=[svrg], tag=tag + "sv")
            sch.add("scalar", lambda e: e.activation(stats[:, scol + 1:scol + 2], stats[:, scol + 1:scol + 2],
                                                     AF.Ln, bias=EPS, scale=1.0),
                    reads=[stats_rg, "consts"], writes=[stats_rg], tag=tag + "ln")
            sch.add("scalar", lambda e: e.activation(stats[:, scol + 1:scol + 2], stats[:, scol + 1:scol + 2],
                                                     AF.Exp, bias=0.0, scale=-0.5),
                    reads=[stats_rg, "consts"], writes=[stats_rg], tag=tag + "ex")
            if save_to is not None:
                sv, svrg, svcol = save_to
                sch.add("vector", lambda e: e.tensor_copy(sv[:, svcol + 1:svcol + 2], stats[:, scol + 1:scol + 2]),
                        reads=[stats_rg], writes=[svrg], tag=tag + "sv2")

        def ln_apply(xap, xrg, stats, stats_rg, scol, outap, outrg, tag):
            sch.add("vector", lambda e: e.tensor_scalar(
                outap, xap, stats[:, scol:scol + 1], stats[:, scol + 1:scol + 2],
                ALU.subtract, ALU.mult), reads=[xrg, stats_rg], writes=[outrg], tag=tag + "ap", fsz=D)

        def mk_wstrip(wbuf, wrg, w_dram, grow, tbanks, wcb, wcbpre, tag):
            for rc in range(4):
                wt = wcb[rc % 2]; wtrg = wcbpre + str(rc % 2)
                sch.dma("gpsimd", wt[:],
                        w_dram[grow * 512 + rc * 128: grow * 512 + (rc + 1) * 128, :],
                        writes=[wtrg], tag=f"{tag}ld{rc}")
                for gi in range(2):
                    bsel = tbanks[gi]
                    for j in range(8):
                        ic = gi * 8 + j
                        sch.add("tensor", (lambda bsel=bsel, j=j, ic=ic, wt=wt: (lambda e: e.transpose(
                            pb[bsel][:].bitcast(BF16)[:, j * 128:(j + 1) * 128],
                            wt[:, ic * 128:(ic + 1) * 128], ident[:])))(),
                            reads=[wtrg, "ident"], writes=[PB[bsel]], tag=f"{tag}T{rc}_{ic}")
                    sch.add("vector", (lambda bsel=bsel, gi=gi, rc=rc, wbuf=wbuf: (lambda e: e.tensor_copy(
                        wbuf[:].rearrange("p (kc o) -> p kc o", o=512)[:, gi * 8:gi * 8 + 8, rc * 128:(rc + 1) * 128],
                        pb[bsel][:].bitcast(BF16)[:, 0:1024].rearrange("p (c q) -> p c q", q=128))))(),
                        reads=[PB[bsel]], writes=[wrg], tag=f"{tag}e{rc}_{gi}", fsz=1024)

        if "wprep" in phases:
            wp = ExitStack()
            with wp:
                NWR = 4
                wbig = [sb(f"wbig{i}", [128, 2048], F32, wp) for i in range(NWR)]
                qtw = [make_qt(f"wq{i}_", 2048, wp) for i in range(NWR)]
                wtev = [sb(f"wtev{i}", [128, 1024], BF16, wp) for i in range(2 * NWR)]
                _wn = [0]

                def prep_weight(src_ap, dest_scT, o_chunks, k_cols, do_quant, tag, screg, dest_scT2=None,
                                screg2=None):
                    npc = k_cols // 2048
                    dview = dest_scT.rearrange("(kc p) o -> p kc o", p=128)
                    dview2 = dest_scT2.rearrange("(kc p) o -> p kc o", p=128) if dest_scT2 is not None else None
                    for oc in range(o_chunks):
                        for p in range(npc):
                            par = _wn[0] % NWR
                            _wn[0] += 1
                            w = wbig[par]; wrg = f"wbig{par}"
                            qtc = qtw[par]
                            sch.dma("sync", w[:], src_ap[oc * 128:(oc + 1) * 128, p * 2048:(p + 1) * 2048],
                                    writes=[wrg], tag=f"{tag}ld{oc}_{p}")
                            if do_quant:
                                quant_chunk(w[:], wrg, 2048, qtc, f"{tag}qz{oc}_{p}")
                            else:
                                sch.add("vector", (lambda w=w, qtc=qtc: (lambda e: e.tensor_copy(
                                    qtc["q"][0][:, 0:2048], w[:])))(),
                                    reads=[wrg], writes=[qtc["q"][1]], tag=f"{tag}cast{oc}_{p}", fsz=2048)
                            tsrc, tsrc_rg = qtc["q"][0], qtc["q"][1]

                            def mk_evac(oc, p, dv, sfx, sreg, par):
                                def evac(bsel, gi, n, brg):
                                    ev = wtev[par + (NWR if sfx == "p" else 0)]
                                    evrg = f"wtev{par + (NWR if sfx == 'p' else 0)}"
                                    sch.add("vector", (lambda ev=ev, bsel=bsel, n=n: (lambda e: e.tensor_copy(
                                        ev[:, 0:n * 128], pb[bsel][:].bitcast(BF16)[:, 0:n * 128])))(),
                                        reads=[brg], writes=[evrg], tag=f"{tag}tev{sfx}{oc}_{p}_{gi}", fsz=n * 128)
                                    sch.dma("gpsimd",
                                            dv[:, p * 16 + gi:p * 16 + gi + n, oc * 128:(oc + 1) * 128],
                                            ev[:, 0:n * 128].rearrange("p (c q) -> p c q", q=128),
                                            reads=[evrg], writes=[f"{sreg}#{oc % 4}"], tag=f"{tag}st{sfx}{oc}_{p}_{gi}")
                                return evac

                            cols = [c * 128 for c in range(16)]
                            transpose_pack(tsrc, tsrc_rg, cols, [4, 5], mk_evac(oc, p, dview, "n", screg, par),
                                           f"{tag}t{oc}_{p}_")
                            if dview2 is not None:
                                transpose_pack(tsrc, tsrc_rg, cols, [6, 7], mk_evac(oc, p, dview2, "p", screg2, par),
                                               f"{tag}tp{oc}_{p}_", perm=pswap)

                prep_weight(W1, sc_qW1T, KCH, D, True, "pw1", "scqW1T")
                prep_weight(W2, sc_qW2T, KC, DH, True, "pw2", "scqW2T")
            sch.fence()

        if "attn" in phases:
            ap_ = ExitStack()
            with ap_:
                xln_fm = sb("xln_fm", [128, KC * T1], BF16, ap_)
                attnout = sb("attnout", [128, NH * R], BF16, ap_)

                a1 = ExitStack()
                with a1:
                    xc = [sb(f"a_xc{i}", [128, D], F32, a1) for i in range(2)]
                    xlntm = [sb(f"a_xt{i}", [128, D], BF16, a1) for i in range(2)]
                    lstats = sb("a_ls", [128, 16], F32, a1)
                    sqs1 = sb("a_sq", [128, D], F32, a1)

                    def a1_chain(tc):
                        xt = xc[tc % 2]; xtrg = f"a_xc{tc % 2}"
                        lt = xlntm[tc % 2]; ltrg = f"a_xt{tc % 2}"
                        scol = (tc % 2) * 8
                        sch.begin_collect()
                        sch.dma("sync", xt[:], x_in[tc * 128:(tc + 1) * 128, :], writes=[xtrg], tag=f"a_lx{tc}")
                        ln_chunk(xt[:, 0:D], xtrg, lstats, f"a_ls{tc % 2}", scol, sqs1[:], "a_sq", f"a_l{tc}")
                        ln_apply(xt[:, 0:D], xtrg, lstats, f"a_ls{tc % 2}", scol, lt[:, 0:D], ltrg, f"a_l{tc}")
                        chain = sch.end_collect()

                        def do_T(tc=tc, lt=lt, ltrg=ltrg):
                            def evx(bsel, gi, n, brg):
                                sch.add("vector", (lambda bsel=bsel, gi=gi, n=n: (lambda e: e.tensor_copy(
                                    xln_fm.rearrange("p (c t) -> p c t", t=T1)[:, gi:gi + n, tc * 128:(tc + 1) * 128],
                                    pb[bsel][:].bitcast(BF16)[:, 0:n * 128].rearrange("p (c q) -> p c q", q=128))))(),
                                    reads=[brg], writes=["xln_fm"], tag=f"a_xe{tc}_{gi}")
                            transpose_pack(lt, ltrg, [c * 128 for c in range(KC)], [4, 5], evx, f"a_xT{tc}_")
                        chain.append(do_T)
                        return chain

                    for pi in range(0, NTC, 2):
                        chains = [a1_chain(pi), a1_chain(pi + 1)]
                        for st in zip_longest(*chains):
                            for t in st:
                                if t is not None:
                                    t()
                sch.fence()
                dbg("dbg_xln", xln_fm[:], "xln_fm", [128, KC * T1], BF16)

                pass

                a2 = ExitStack()
                with a2:
                    tabsb = sb("tabsb", [128, 4 * T1], BF16, a2)
                    maskm = sb("maskm", [128, 8 * WIN], BF16, a2)
                    maskb0 = sb("maskb0", [128, 4 * WIN], BF16, a2)
                    sch.dma("sync", tabsb[:], tabs, writes=["tabsb"], tag="ld_tabs")
                    sch.dma("sync", maskm[:], maskM, writes=["maskm"], tag="ld_maskm")
                    sch.dma("sync", maskb0[:], maskB0, writes=["maskb0"], tag="ld_maskb0")
                    q_fm = sb("q_fm", [128, 4 * R], BF16, a2)
                    k_fm = sb("k_fm", [128, 4 * T1], BF16, a2)
                    v_tm = sb("v_tm", [128, NTC * 512], BF16, a2)
                    p_sbr = [sb(f"p_sb{i}", [128, 8 * WIN], BF16, a2) for i in range(2)]
                    rt = [sb(f"a_rt{i}", [128, 512], BF16, a2) for i in range(4)]
                    ssumr = [sb(f"a_ss{i}", [1, 512], F32, a2) for i in range(2)]
                    srecipr = [sb(f"a_sr{i}", [1, 512], F32, a2) for i in range(2)]
                    recipBr = [sb(f"a_rB{i}", [128, 512], F32, a2) for i in range(2)]
                    wcb2 = [strip1[:, 0:D], strip1[:, D:2 * D]]

                    SWAPM = []
                    for i in range(0, 32, 2):
                        SWAPM += [i + 1, i]
                    for g in range(4):
                        for (wsrc, dst, dstrg, nt, toff, ctab, stab) in (
                            (aWq, q_fm, "q_fm", 2, HL, 2 * T1, 3 * T1),
                            (aWk, k_fm, "k_fm", 3, 0, 0, T1),
                        ):
                            mk_wstrip(strip0, "strip0", wsrc, g, (3, 4), wcb2, "a_wc", f"a_w{dstrg}{g}")
                            for ho in range(4):
                                for tt in range(nt):
                                    t0 = toff + tt * 512
                                    bank = (ho * nt + tt) % 2
                                    for kc2 in range(KC):
                                        sch.add("tensor", (lambda bank=bank, kc2=kc2, ho=ho, t0=t0: (lambda e: e.matmul(
                                            pb[bank][:],
                                            strip0[:, kc2 * 512 + ho * 128: kc2 * 512 + (ho + 1) * 128],
                                            xln_fm[:, kc2 * T1 + t0: kc2 * T1 + t0 + 512],
                                            start=(kc2 == 0), stop=(kc2 == KC - 1))))(),
                                            reads=(["strip0", "xln_fm"] + ([PB[bank]] if kc2 else [])),
                                            writes=[PB[bank]], tag=f"a_p{dstrg}{g}_{ho}_{tt}_{kc2}")
                                    ctt = ctab + t0
                                    stt_ = stab + t0
                                    dcol = ho * (R if dstrg == "q_fm" else T1) + tt * 512
                                    pr = ((ho * nt + tt) % 2) * 2
                                    r0, r1_ = rt[pr], rt[pr + 1]
                                    r0n, r1n = f"a_rt{pr}", f"a_rt{pr + 1}"
                                    sch.add("vector", (lambda bank=bank, r0=r0: (lambda e: e.tensor_copy(
                                        r0[:], pb[bank][:])))(),
                                        reads=[PB[bank]], writes=[r0n], tag=f"a_rv{dstrg}{g}_{ho}_{tt}")
                                    sch.add("vector", (lambda r0=r0, r1_=r1_: (lambda e: e.stream_shuffle(
                                        r1_[:], r0[:], SWAPM)))(),
                                        reads=[r0n], writes=[r1n], tag=f"a_rs{dstrg}{g}_{ho}_{tt}")
                                    sch.add("vector", (lambda ctt=ctt, r0=r0: (lambda e: e.tensor_tensor(
                                        r0[:], r0[:], tabsb[:, ctt:ctt + 512], ALU.mult)))(),
                                        reads=[r0n, "tabsb"], writes=[r0n], tag=f"a_r1{dstrg}{g}_{ho}_{tt}")
                                    sch.add("vector", (lambda stt_=stt_, r1_=r1_: (lambda e: e.tensor_tensor(
                                        r1_[:], r1_[:], tabsb[:, stt_:stt_ + 512], ALU.mult)))(),
                                        reads=[r1n, "tabsb"], writes=[r1n], tag=f"a_r2{dstrg}{g}_{ho}_{tt}")
                                    sch.add("vector", (lambda dst=dst, dcol=dcol, r0=r0, r1_=r1_: (lambda e: e.tensor_tensor(
                                        dst[:, dcol:dcol + 512], r0[:], r1_[:], ALU.add)))(),
                                        reads=[r0n, r1n], writes=[dstrg], tag=f"a_r3{dstrg}{g}_{ho}_{tt}")

                        mk_wstrip(strip0, "strip0", aWv, g, (3, 4), wcb2, "a_wc", f"a_wv{g}")
                        for tc in range(NTC):
                            for kc2 in range(KC):
                                sch.add("tensor", (lambda kc2=kc2, tc=tc: (lambda e: e.matmul(
                                    pb[2][:],
                                    xln_fm[:, kc2 * T1 + tc * 128: kc2 * T1 + (tc + 1) * 128],
                                    strip0[:, kc2 * 512:(kc2 + 1) * 512],
                                    start=(kc2 == 0), stop=(kc2 == KC - 1))))(),
                                    reads=(["strip0", "xln_fm"] + ([PB[2]] if kc2 else [])),
                                    writes=[PB[2]], tag=f"a_pv{g}_{tc}_{kc2}")
                            sch.add("vector", (lambda tc=tc: (lambda e: e.tensor_copy(
                                v_tm[:, tc * 512:(tc + 1) * 512], pb[2][:])))(),
                                reads=[PB[2]], writes=["v_tm"], tag=f"a_ve{g}_{tc}")

                        if g == 0:
                            dbg("dbg_q", q_fm[:], "q_fm", [128, 4 * R], BF16)
                            dbg("dbg_k", k_fm[:], "k_fm", [128, 4 * T1], BF16)
                            dbg("dbg_v", v_tm[:], "v_tm", [128, NTC * 512], BF16)
                        def u_stage1(ho, b):
                            h = g * 4 + ho
                            up = b
                            p_sb = p_sbr[up]; p_rg = f"p_sb{up}"
                            for jc in range(8):
                                bank = 3 + (jc % 2)
                                kcol = ho * T1 + b * 512 + jc * 128
                                qcol = ho * R + b * 512
                                sch.add("tensor", (lambda bank=bank, kcol=kcol, qcol=qcol: (lambda e: e.matmul(
                                    pb[bank][:], k_fm[:, kcol:kcol + 128], q_fm[:, qcol:qcol + 512],
                                    start=True, stop=False)))(),
                                    reads=["k_fm", "q_fm"], writes=[PB[bank]], tag=f"a_s{h}_{b}_{jc}")
                                if b == 0 and jc < 4:
                                    mt, mtrg, mcol = maskb0, "maskb0", jc * 512
                                else:
                                    mt, mtrg, mcol = maskm, "maskm", jc * 512
                                sch.add("tensor", (lambda bank=bank, mt=mt, mcol=mcol: (lambda e: e.matmul(
                                    pb[bank][:], ident[:], mt[:, mcol:mcol + 512],
                                    start=False, stop=True)))(),
                                    reads=[mtrg, "ident", PB[bank]], writes=[PB[bank]], tag=f"a_m{h}_{b}_{jc}")
                                sch.add("scalar", (lambda bank=bank, jc=jc, p_sb=p_sb: (lambda e: e.activation(
                                    p_sb[:, jc * 512:(jc + 1) * 512], pb[bank][:], AF.Exp,
                                    bias=CEXP, scale=1.0)))(),
                                    reads=[PB[bank], "consts"], writes=[p_rg], tag=f"a_e{h}_{b}_{jc}")

                        def u_sums(ho, b, su_b):
                            h = g * 4 + ho
                            up = b
                            p_sb = p_sbr[up]; p_rg = f"p_sb{up}"
                            ssum = ssumr[up]; ss_rg = f"a_ss{up}"
                            for jc in range(8):
                                sch.add("tensor", (lambda jc=jc, p_sb=p_sb, su_b=su_b: (lambda e: e.matmul(
                                    pb[su_b][0:1, :], ones_c[:], p_sb[:, jc * 512:(jc + 1) * 512],
                                    start=(jc == 0), stop=(jc == 7))))(),
                                    reads=(["ones_c", p_rg] + ([PB[su_b]] if jc else [])),
                                    writes=[PB[su_b]], tag=f"a_su{h}_{b}_{jc}")
                            sch.add("vector", (lambda ssum=ssum, su_b=su_b: (lambda e: e.tensor_copy(
                                ssum[:], pb[su_b][0:1, :])))(),
                                reads=[PB[su_b]], writes=[ss_rg], tag=f"a_se{h}_{b}")
                            sch.add("scalar", (lambda ssum=ssum, srecip=srecipr[up]: (lambda e: e.activation(
                                srecip[:], ssum[:], AF.Ln, bias=0.0, scale=1.0)))(),
                                reads=[ss_rg, "consts"], writes=[f"a_sr{up}"], tag=f"a_rl{h}_{b}")
                            sch.add("scalar", (lambda srecip=srecipr[up]: (lambda e: e.activation(
                                srecip[:], srecip[:], AF.Exp, bias=0.0, scale=-1.0)))(),
                                reads=[f"a_sr{up}", "consts"], writes=[f"a_sr{up}"], tag=f"a_re{h}_{b}")

                        def u_stage2(ho, b, rb_b, av_b):
                            h = g * 4 + ho
                            up = b
                            p_sb = p_sbr[up]; p_rg = f"p_sb{up}"
                            ssum = ssumr[up]; ss_rg = f"a_ss{up}"
                            srecip = srecipr[up]; sr_rg = f"a_sr{up}"
                            recipB = recipBr[up]; rB_rg = f"a_rB{up}"
                            sch.add("tensor", (lambda srecip=srecip, rb_b=rb_b: (lambda e: e.matmul(
                                pb[rb_b][:], ones_rf[:], srecip[:], start=True, stop=True)))(),
                                reads=[sr_rg, "ones_rf"], writes=[PB[rb_b]], tag=f"a_rb{h}_{b}")
                            sch.add("vector", (lambda recipB=recipB, rb_b=rb_b: (lambda e: e.tensor_copy(
                                recipB[:], pb[rb_b][:])))(),
                                reads=[PB[rb_b]], writes=[rB_rg], tag=f"a_rbe{h}_{b}")
                            for jc in range(8):
                                vcol = (4 * b + jc) * 512 + ho * 128
                                sch.add("tensor", (lambda jc=jc, vcol=vcol, p_sb=p_sb, av_b=av_b: (lambda e: e.matmul(
                                    pb[av_b][:], v_tm[:, vcol:vcol + 128], p_sb[:, jc * 512:(jc + 1) * 512],
                                    start=(jc == 0), stop=(jc == 7))))(),
                                    reads=(["v_tm", p_rg] + ([PB[av_b]] if jc else [])),
                                    writes=[PB[av_b]], tag=f"a_av{h}_{b}_{jc}")
                            acol = h * R + b * 512
                            sch.add("vector", (lambda acol=acol, recipB=recipB, av_b=av_b: (lambda e: e.tensor_tensor(
                                attnout[:, acol:acol + 512], pb[av_b][:], recipB[:], ALU.mult)))(),
                                reads=[PB[av_b], rB_rg], writes=["attnout"], tag=f"a_ao{h}_{b}")
                            if h == 0 and b == 1:
                                dbg("dbg_p", p_sb[:], p_rg, [128, 8 * WIN], BF16)
                                dbg("dbg_sum", ssum[:], ss_rg, [1, 512], F32)
                                dbg("dbg_recipB", recipB[:], rB_rg, [128, 512], F32)

                        for ho in range(4):
                            u_stage1(ho, 0)
                            u_stage1(ho, 1)
                            u_sums(ho, 0, 5)
                            u_sums(ho, 1, 0)
                            u_stage2(ho, 0, 6, 7)
                            u_stage2(ho, 1, 1, 2)
                sch.fence()
                dbg("dbg_attnout", attnout[:], "attnout", [128, NH * R], BF16)

                a3 = ExitStack()
                with a3:
                    xres = [sb(f"a_xr{i}", [128, D], F32, a3) for i in range(NTO)]
                    wcb3 = [sb(f"a_w3c{i}", [128, D], BF16, a3) for i in range(2)]
                    for tch in range(NTO):
                        sch.dma("sync", xres[tch][:], x_in[HL + tch * 128: HL + (tch + 1) * 128, :],
                                writes=[f"a_xr{tch}"], tag=f"a_lr{tch}")
                    for ot in range(4):
                        wbuf = strip0 if ot % 2 == 0 else strip1
                        wrg = "strip0" if ot % 2 == 0 else "strip1"
                        mk_wstrip(wbuf, wrg, aWo, ot, (6, 7), wcb3, "a_w3c", f"a_wo{ot}")
                        for tch in range(NTO):
                            bank = tch % 4
                            for ic in range(KC):
                                sch.add("tensor", (lambda bank=bank, ic=ic, tch=tch, wbuf=wbuf: (lambda e: e.matmul(
                                    pb[bank][:], attnout[:, ic * R + tch * 128: ic * R + (tch + 1) * 128],
                                    wbuf[:, ic * 512:(ic + 1) * 512],
                                    start=(ic == 0), stop=(ic == KC - 1))))(),
                                    reads=(["attnout", wrg] + ([PB[bank]] if ic else [])),
                                    writes=[PB[bank]], tag=f"a_o{ot}_{tch}_{ic}")
                            sch.add("vector", (lambda bank=bank, tch=tch, ot=ot: (lambda e: e.tensor_tensor(
                                xres[tch][:, ot * 512:(ot + 1) * 512], pb[bank][:],
                                xres[tch][:, ot * 512:(ot + 1) * 512], ALU.add)))(),
                                reads=[PB[bank], f"a_xr{tch}"], writes=[f"a_xr{tch}"], tag=f"a_or{ot}_{tch}")
                    for tch in range(NTO):
                        sch.dma("gpsimd", sc_X[tch * 128:(tch + 1) * 128, :], xres[tch][:],
                                reads=[f"a_xr{tch}"], writes=[f"scX{tch}_{q}" for q in range(4)],
                                tag=f"a_sx{tch}")
            sch.fence()

        def ffn_phase(idx, x_src, final):
            fp_ = ExitStack()
            with fp_:
                pre = f"f{idx}"
                gamB = sb(pre + "gamB", [128, D], BF16, fp_)
                betB = sb(pre + "betB", [128, D], BF16, fp_)
                b1t = [sb(pre + f"b1t{i}", [1, 512], BF16, fp_) for i in range(2)]
                b2t = [sb(pre + f"b2t{i}", [1, 512], BF16, fp_) for i in range(2)]
                xc2 = [sb(pre + f"xc{i}", [128, D], F32, fp_) for i in range(2)]
                xb2 = [sb(pre + f"xb{i}", [128, D], BF16, fp_) for i in range(2)]
                t0b = sb(pre + "t0", [128, D], F32, fp_)
                lstat = sb(pre + "ls", [128, 16], F32, fp_)
                lnq_fm = sb(pre + "lnq", [128, KC * 512], BF16, fp_)
                hq_fm = sb(pre + "hq", [128, KCH * 512], BF16, fp_)
                hg = [sb(pre + f"hg{i}", [128, 4 * 512], BF16, fp_) for i in range(2)]
                res = [sb(pre + f"res{i}", [128, D], BF16, fp_) for i in range(4)]
                xs2 = [sb(pre + f"xs{i}", [128, 512], F32, fp_) for i in range(2)]
                f2r = [sb(pre + f"f2{i}", [128, 512], BF16, fp_) for i in range(4)]
                qt2 = [make_qt(pre + "q0", D, fp_), make_qt(pre + "q1", D, fp_)]

                rw_ = ExitStack()
                with rw_:
                    rows = [sb(pre + f"lnr{i}", [1, 512], BF16, rw_) for i in range(2)]
                    for nm, srcv, dst, dstrg in (("g", lnw, gamB, pre + "gamB"),
                                                 ("b", lnb, betB, pre + "betB")):
                        for j in range(4):
                            row = rows[j % 2]; rowrg = pre + f"lnr{j % 2}"
                            sch.dma("gpsimd", row[:], srcv.rearrange("(a b) -> a b", a=1)[0:1, j * 512:(j + 1) * 512],
                                    writes=[rowrg], tag=f"{pre}ld{nm}{j}")
                            sch.add("tensor", (lambda row=row: (lambda e: e.matmul(
                                pb[0][:], ones_r[:], row[0:1, :], start=True, stop=True)))(),
                                reads=[rowrg, "ones_r"], writes=[PB[0]], tag=f"{pre}gb{nm}{j}")
                            sch.add("vector", (lambda dst=dst, j=j: (lambda e: e.tensor_copy(
                                dst[:, j * 512:(j + 1) * 512], pb[0][:])))(),
                                reads=[PB[0]], writes=[dstrg], tag=f"{pre}ge{nm}{j}")
                sch.fence()

                scT_w1 = sc_qW1T.rearrange("(kc p) o -> p kc o", p=128)
                scT_w2 = sc_qW2T.rearrange("(kc p) o -> p kc o", p=128)

                def ln_chain(half, tci, tbanks):
                    tch = half * 4 + tci
                    xt = xc2[tci % 2]; xtrg = pre + f"xc{tci % 2}"
                    xbt = xb2[tci % 2]; xbrg = pre + f"xb{tci % 2}"
                    scol = (tci % 2) * 8
                    lsrg = pre + f"ls{tci % 2}"
                    qtc = qt2[tci % 2]
                    sch.begin_collect()
                    sch.dma("sync", xt[:], x_src[tch * 128:(tch + 1) * 128, :], writes=[xtrg],
                            reads=[f"scX{tch}_{q}" for q in range(4)], tag=f"{pre}lx{tch}")
                    ln_chunk(xt[:, 0:D], xtrg, lstat, lsrg, scol, t0b[:], pre + "t0", f"{pre}l2{tch}",
                             save_to=(ln2stats, "ln2stats", tch * 2))
                    ln_apply(xt[:, 0:D], xtrg, lstat, lsrg, scol, xbt[:, 0:D], xbrg, f"{pre}l2{tch}")
                    ln_chunk(xbt[:, 0:D], xbrg, lstat, lsrg, scol, t0b[:], pre + "t0", f"{pre}l3{tch}")
                    sch.add("vector", (lambda xbt=xbt, xt=xt, scol=scol: (lambda e: e.tensor_scalar(
                        xt[:], xbt[:], lstat[:, scol:scol + 1], lstat[:, scol + 1:scol + 2],
                        ALU.subtract, ALU.mult)))(),
                        reads=[xbrg, lsrg], writes=[xtrg], tag=f"{pre}l3a{tch}")
                    sch.add("vector", (lambda xt=xt: (lambda e: e.tensor_tensor(
                        xt[:], xt[:], gamB[:], ALU.mult)))(),
                        reads=[xtrg, pre + "gamB"], writes=[xtrg], tag=f"{pre}l3b{tch}", fsz=D)
                    sch.add("vector", (lambda xt=xt, xbt=xbt: (lambda e: e.tensor_tensor(
                        xbt[:], xt[:], betB[:], ALU.add)))(),
                        reads=[xtrg, pre + "betB"], writes=[xbrg], tag=f"{pre}l3c{tch}", fsz=D)
                    quant_chunk(xbt[:, 0:D], xbrg, D, qtc, f"{pre}qz{tch}")
                    chain = sch.end_collect()

                    def do_T(tci=tci, tch=tch, qtc=qtc, tbanks=tbanks):
                        def ev2(bsel, gi, n, brg):
                            sch.add("vector", (lambda bsel=bsel, gi=gi, n=n: (lambda e: e.tensor_copy(
                                lnq_fm.rearrange("p (c t) -> p c t", t=512)[:, gi:gi + n, tci * 128:(tci + 1) * 128],
                                pb[bsel][:].bitcast(BF16)[:, 0:n * 128].rearrange("p (c q) -> p c q", q=128))))(),
                                reads=[brg], writes=[pre + f"lnq{tci}"], tag=f"{pre}lqe{tch}_{gi}", fsz=n * 128)
                        transpose_pack(qtc["q"][0], qtc["q"][1], [c * 128 for c in range(KC)], tbanks,
                                       ev2, f"{pre}lqT{tch}_")
                    chain.append(do_T)
                    return chain

                def ln_stage(half, tbanks=(4, 5)):
                    for pi in range(2):
                        chains = [ln_chain(half, pi * 2 + j, tbanks) for j in (0, 1)]
                        for st in zip_longest(*chains):
                            for t in st:
                                if t is not None:
                                    t()

                def w1_stage(half):
                    pend_T = []
                    for ot in range(16):
                        wbuf = strip0 if ot % 2 == 0 else strip1
                        wrg = "strip0" if ot % 2 == 0 else "strip1"
                        sch.dma("sync" if ot % 2 == 0 else "gpsimd",
                                wbuf[:].rearrange("p (kc o) -> p kc o", o=512),
                                scT_w1[:, :, ot * 512:(ot + 1) * 512], reads=[f"scqW1T#{i}" for i in range(4)], writes=[wrg], tag=f"{pre}w1l{half}_{ot}")
                        hgb = hg[ot % 2]; hgrg = pre + f"hg{ot % 2}"
                        bt = b1t[ot % 2]; btrg = pre + f"b1t{ot % 2}"
                        sch.dma("gpsimd", bt[:], b1v.rearrange("(a b) -> a b", a=1)[0:1, ot * 512:(ot + 1) * 512],
                                writes=[btrg], tag=f"{pre}w1bl{half}_{ot}")
                        for tci in range(4):
                            bank = tci
                            for kc2 in range(KC):
                                sch.add("tensor", (lambda bank=bank, kc2=kc2, tci=tci, wbuf=wbuf: (lambda e: e.matmul(
                                    pb[bank][:], lnq_fm[:, kc2 * 512 + tci * 128: kc2 * 512 + (tci + 1) * 128],
                                    wbuf[:, kc2 * 512:(kc2 + 1) * 512],
                                    start=(kc2 == 0), stop=False)))(),
                                    reads=([pre + f"lnq{tci}", wrg] + ([PB[bank]] if kc2 else [])),
                                    writes=[PB[bank]], tag=f"{pre}w1m{half}_{ot}_{tci}_{kc2}")
                            sch.add("tensor", (lambda bank=bank, bt=bt: (lambda e: e.matmul(
                                pb[bank][:], ones_r[:], bt[:],
                                start=False, stop=True)))(),
                                reads=[btrg, "ones_r", PB[bank]], writes=[PB[bank]], tag=f"{pre}w1b{half}_{ot}_{tci}")
                            fb = f2r[tci % 2]; fbrg = pre + f"f2{tci % 2}"
                            sch.add("vector", (lambda bank=bank, fb=fb: (lambda e: e.tensor_copy(
                                fb[:], pb[bank][:])))(),
                                reads=[PB[bank]], writes=[fbrg], tag=f"{pre}w1e{half}_{ot}_{tci}")
                            sch.add("scalar", (lambda hgb=hgb, tci=tci, fb=fb: (lambda e: e.activation(
                                hgb[:, tci * 512:(tci + 1) * 512], fb[:], AF.Gelu_apprx_tanh,
                                bias=0.0, scale=1.0)))(),
                                reads=[fbrg, "consts"], writes=[hgrg], tag=f"{pre}w1g{half}_{ot}_{tci}")
                        qtc = qt2[ot % 2]
                        quant_chunk(hgb[:, 0:2048], hgrg, 2048, qtc, f"{pre}hz{half}_{ot}")

                        def hq_transposes(ot=ot, qtc=qtc):
                            for grp2 in range(2):
                                bsel = 4 + grp2
                                for oc_ in range(2):
                                    for tci_ in range(4):
                                        col = tci_ * 512 + (grp2 * 2 + oc_) * 128
                                        slot = oc_ * 4 + tci_
                                        sch.add("tensor", (lambda bsel=bsel, slot=slot, col=col, qtc=qtc: (lambda e: e.transpose(
                                            pb[bsel][:].bitcast(BF16)[:, slot * 128:(slot + 1) * 128],
                                            qtc["q"][0][:, col:col + 128], ident[:])))(),
                                            reads=[qtc["q"][1], "ident"], writes=[PB[bsel]], tag=f"{pre}hT{half}_{ot}_{grp2}_{oc_}_{tci_}")
                                    dest = hq_fm[:, (ot * 4 + grp2 * 2) * 512: (ot * 4 + grp2 * 2 + 2) * 512]
                                    sch.add("vector", (lambda bsel=bsel, dest=dest: (lambda e: e.tensor_copy(
                                        dest.rearrange("p (a b q) -> p a b q", b=4, q=128),
                                        pb[bsel][:].bitcast(BF16)[:, 0:1024].rearrange("p (a b q) -> p a b q", b=4, q=128))))(),
                                        reads=[PB[bsel]], writes=[pre + "hq"], tag=f"{pre}hqe{half}_{ot}_{grp2}")
                        pend_T.append(hq_transposes)
                        if len(pend_T) > 1:
                            pend_T.pop(0)()
                    while pend_T:
                        pend_T.pop(0)()

                def w2_stage(half, hook=None):
                    tbase = half * 4
                    for tci in range(4):
                        tch = tbase + tci
                        xt = xc2[tci % 2]; xtrg = pre + f"xc{tci % 2}"
                        sch.dma("sync", xt[:], x_src[tch * 128:(tch + 1) * 128, :], writes=[xtrg],
                            reads=[f"scX{tch}_{q}" for q in range(4)], tag=f"{pre}rx{tch}")
                        sch.add("vector", (lambda xt=xt, tci=tci, tch=tch: (lambda e: e.tensor_scalar(
                            res[tci][:], xt[:], ln2stats[:, tch * 2:tch * 2 + 1], ln2stats[:, tch * 2 + 1:tch * 2 + 2],
                            ALU.subtract, ALU.mult)))(),
                            reads=[xtrg, "ln2stats"], writes=[pre + f"res{tci}"], tag=f"{pre}rr{tch}", fsz=D)

                    for o2 in range(4):
                        bt2 = b2t[o2 % 2]; bt2rg = pre + f"b2t{o2 % 2}"
                        sch.dma("gpsimd", bt2[:], b2v.rearrange("(a b) -> a b", a=1)[0:1, o2 * 512:(o2 + 1) * 512],
                                writes=[bt2rg], tag=f"{pre}w2bl{half}_{o2}")
                        for kg in range(4):
                            wbuf = strip0 if kg % 2 == 0 else strip1
                            wrg = "strip0" if kg % 2 == 0 else "strip1"
                            sch.dma("sync" if kg % 2 == 0 else "gpsimd",
                                    wbuf[:].rearrange("p (kc o) -> p kc o", o=512),
                                    scT_w2[:, kg * 16:(kg + 1) * 16, o2 * 512:(o2 + 1) * 512],
                                    reads=[f"scqW2T#{i}" for i in range(4)], writes=[wrg], tag=f"{pre}w2l{half}_{o2}_{kg}")
                            for tci in range(4):
                                bank = 4 + tci
                                for kk in range(16):
                                    kc2 = kg * 16 + kk
                                    sch.add("tensor", (lambda bank=bank, kc2=kc2, kk=kk, tci=tci, wbuf=wbuf: (lambda e: e.matmul(
                                        pb[bank][:], hq_fm[:, kc2 * 512 + tci * 128: kc2 * 512 + (tci + 1) * 128],
                                        wbuf[:, kk * 512:(kk + 1) * 512],
                                        start=(kc2 == 0), stop=False)))(),
                                        reads=([pre + "hq", wrg] + ([PB[bank]] if kc2 else [])),
                                        writes=[PB[bank]], tag=f"{pre}w2m{half}_{o2}_{tci}_{kc2}")
                        for tci in range(4):
                            bank = 4 + tci
                            sch.add("tensor", (lambda bank=bank, bt2=bt2: (lambda e: e.matmul(
                                pb[bank][:], ones_r[:], bt2[:],
                                start=False, stop=True)))(),
                                reads=[bt2rg, "ones_r", PB[bank]], writes=[PB[bank]], tag=f"{pre}w2b{half}_{o2}_{tci}")
                        for tci in range(4):
                            bank = 4 + tci
                            fb = f2r[tci]; fbrg = pre + f"f2{tci}"
                            sch.add("vector", (lambda bank=bank, tci=tci, o2=o2, fb=fb: (lambda e: e.scalar_tensor_tensor(
                                fb[:], pb[bank][:], 1.0, res[tci][:, o2 * 512:(o2 + 1) * 512], ALU.mult, ALU.add)))(),
                                reads=[PB[bank], pre + f"res{tci}"], writes=[fbrg], tag=f"{pre}w2f{half}_{o2}_{tci}")
                        for tci in range(4):
                            tch = tbase + tci
                            fb = f2r[tci]; fbrg = pre + f"f2{tci}"
                            xsl = xs2[tci % 2]; xslrg = pre + f"xs{tci % 2}"
                            sch.dma("sync", xsl[:], x_src[tch * 128:(tch + 1) * 128, o2 * 512:(o2 + 1) * 512],
                                    reads=[f"scX{tch}_{o2}"], writes=[xslrg], tag=f"{pre}w2x{half}_{o2}_{tci}")
                            sch.add("vector", (lambda xsl=xsl, fb=fb: (lambda e: e.tensor_tensor(
                                xsl[:], fb[:], xsl[:], ALU.add)))(),
                                reads=[fbrg, xslrg], writes=[xslrg], tag=f"{pre}w2a{half}_{o2}_{tci}")
                            dst = out if final else sc_X
                            sch.dma("gpsimd", dst[tch * 128:(tch + 1) * 128, o2 * 512:(o2 + 1) * 512], xsl[:],
                                    reads=[xslrg], writes=([f"scX{tch}_{o2}"] if not final else []),
                                    tag=f"{pre}w2s{half}_{o2}_{tci}")
                        if hook is not None:
                            hook(o2)

                def ln1_hook(o2):
                    if o2 in (0, 1):
                        chains = [ln_chain(1, o2 * 2 + j, (0, 1)) for j in (0, 1)]
                        for st in zip_longest(*chains):
                            for t in st:
                                if t is not None:
                                    t()

                ln_stage(0)
                w1_stage(0)
                w2_stage(0, hook=ln1_hook)
                w1_stage(1)
                w2_stage(1)
            sch.fence()

        if "ffn1" in phases:
            ffn_phase(1, sc_X, final=False)

        if "attnb" in phases:
            bp = ExitStack()
            with bp:
                xln3_fm = sb("b_xln", [128, KC * R], BF16, bp)
                v3_fm = sb("b_v3", [128, KC * R], BF16, bp)
                xc3 = [sb(f"b_xc{i}", [128, D], F32, bp) for i in range(2)]
                xb3 = [sb(f"b_xb{i}", [128, D], BF16, bp) for i in range(2)]
                lstat3 = sb("b_ls", [128, 16], F32, bp)
                sqs3 = sb("b_sq", [128, D], F32, bp)
                xsl3 = [sb(f"b_xs{i}", [128, 512], F32, bp) for i in range(2)]

                def b_chain(tch):
                    xt = xc3[tch % 2]; xtrg = f"b_xc{tch % 2}"
                    xbt = xb3[tch % 2]; xbrg = f"b_xb{tch % 2}"
                    scol = (tch % 2) * 8
                    sch.begin_collect()
                    sch.dma("sync", xt[:], sc_X[tch * 128:(tch + 1) * 128, :], writes=[xtrg],
                            reads=[f"scX{tch}_{q}" for q in range(4)], tag=f"b_lx{tch}")
                    ln_chunk(xt[:, 0:D], xtrg, lstat3, f"b_ls{tch % 2}", scol, sqs3[:], "b_sq", f"b_l{tch}")
                    ln_apply(xt[:, 0:D], xtrg, lstat3, f"b_ls{tch % 2}", scol, xbt[:, 0:D], xbrg, f"b_l{tch}")
                    chain = sch.end_collect()

                    def do_T(tch=tch, xbt=xbt, xbrg=xbrg):
                        def evb(bsel, gi, n, brg):
                            sch.add("vector", (lambda bsel=bsel, gi=gi, n=n: (lambda e: e.tensor_copy(
                                xln3_fm.rearrange("p (c t) -> p c t", t=R)[:, gi:gi + n, tch * 128:(tch + 1) * 128],
                                pb[bsel][:].bitcast(BF16)[:, 0:n * 128].rearrange("p (c q) -> p c q", q=128))))(),
                                reads=[brg], writes=["b_xln"], tag=f"b_xe{tch}_{gi}")
                        transpose_pack(xbt, xbrg, [c * 128 for c in range(KC)], [4, 5], evb, f"b_xT{tch}_")
                    chain.append(do_T)
                    return chain

                for pi in range(0, NTO, 2):
                    chains = [b_chain(pi), b_chain(pi + 1)]
                    for st in zip_longest(*chains):
                        for t in st:
                            if t is not None:
                                t()

                wcbb = [sb(f"b_wc{i}", [128, D], BF16, bp) for i in range(2)]
                for g in range(4):
                    wbuf = strip0 if g % 2 == 0 else strip1
                    wrg = "strip0" if g % 2 == 0 else "strip1"
                    mk_wstrip(wbuf, wrg, bWqkv[2 * D:3 * D, :], g, (6, 7), wcbb, "b_wc", f"b_wv{g}")
                    for oo in range(4):
                        for tt in range(2):
                            bank = oo
                            for kc2 in range(KC):
                                sch.add("tensor", (lambda bank=bank, kc2=kc2, oo=oo, tt=tt, wbuf=wbuf: (lambda e: e.matmul(
                                    pb[bank][:], wbuf[:, kc2 * 512 + oo * 128: kc2 * 512 + (oo + 1) * 128],
                                    xln3_fm[:, kc2 * R + tt * 512: kc2 * R + tt * 512 + 512],
                                    start=(kc2 == 0), stop=(kc2 == KC - 1))))(),
                                    reads=([wrg, "b_xln"] + ([PB[bank]] if kc2 else [])),
                                    writes=[PB[bank]], tag=f"b_v{g}_{oo}_{tt}_{kc2}")
                            oc = g * 4 + oo
                            sch.add("vector", (lambda bank=bank, oc=oc, tt=tt: (lambda e: e.tensor_copy(
                                v3_fm[:, oc * R + tt * 512: oc * R + tt * 512 + 512], pb[bank][:])))(),
                                reads=[PB[bank]], writes=["b_v3"], tag=f"b_ve{g}_{oo}_{tt}")

                for ot in range(4):
                    wbuf = strip0 if ot % 2 == 0 else strip1
                    wrg = "strip0" if ot % 2 == 0 else "strip1"
                    mk_wstrip(wbuf, wrg, bWo, ot, (0, 1), wcbb, "b_wc", f"b_wo{ot}")
                    for tch in range(NTO):
                        bank = 4 + tch % 4
                        for ic in range(KC):
                            sch.add("tensor", (lambda bank=bank, ic=ic, tch=tch, wbuf=wbuf: (lambda e: e.matmul(
                                pb[bank][:], v3_fm[:, ic * R + tch * 128: ic * R + (tch + 1) * 128],
                                wbuf[:, ic * 512:(ic + 1) * 512],
                                start=(ic == 0), stop=(ic == KC - 1))))(),
                                reads=(["b_v3", wrg] + ([PB[bank]] if ic else [])),
                                writes=[PB[bank]], tag=f"b_o{ot}_{tch}_{ic}")
                        xsl = xsl3[tch % 2]; xslrg = f"b_xs{tch % 2}"
                        sch.dma("sync", xsl[:], sc_X[tch * 128:(tch + 1) * 128, ot * 512:(ot + 1) * 512],
                                reads=[f"scX{tch}_{ot}"], writes=[xslrg], tag=f"b_lxs{ot}_{tch}")
                        sch.add("vector", (lambda bank=bank, xsl=xsl: (lambda e: e.tensor_tensor(
                            xsl[:], pb[bank][:], xsl[:], ALU.add)))(),
                            reads=[PB[bank], xslrg], writes=[xslrg], tag=f"b_or{ot}_{tch}")
                        sch.dma("gpsimd", sc_X[tch * 128:(tch + 1) * 128, ot * 512:(ot + 1) * 512], xsl[:],
                                reads=[xslrg], writes=[f"scX{tch}_{ot}"], tag=f"b_sxs{ot}_{tch}")
            sch.fence()

        if "ffn2" in phases:
            ffn_phase(2, sc_X, final=True)
        else:
            tmpo = sb("tmpo", [128, D], F32)
            for tch in range(NTO):
                sch.dma("sync", tmpo[:], sc_X[tch * 128:(tch + 1) * 128, :], writes=["tmpo"], tag=f"d_l{tch}")
                sch.dma("sync", out[tch * 128:(tch + 1) * 128, :], tmpo[:], reads=["tmpo"], tag=f"d_s{tch}")

        sch.resolve()
        global LAST_SCHED
        LAST_SCHED = sch
        semkeys = set()
        for o in sch.ops:
            if o.done:
                semkeys.add(o.done[0])
            for k, _ in o.waits:
                semkeys.add(k)
        sems = {k: ctx.enter_context(nc.semaphore("_".join(str(p) for p in k))) for k in sorted(semkeys, key=str)}
        blk = ctx.enter_context(nc.Block())
        sch.emit(nc, blk, sems)

    return nc


def make_in_maps(inputs):
    x = np.ascontiguousarray(np.asarray(inputs["L0_attn_a_x"], np.float32).reshape(SEQ, D))
    ident, pswap = host_ident()
    consts = host_consts()
    base = dict(
        aWq=np.ascontiguousarray(inputs["L0_attn_a_Wq"], dtype=np.float32),
        aWk=np.ascontiguousarray(inputs["L0_attn_a_Wk"], dtype=np.float32),
        aWv=np.ascontiguousarray(inputs["L0_attn_a_Wv"], dtype=np.float32),
        aWo=np.ascontiguousarray(inputs["L0_attn_a_Wo"], dtype=np.float32),
        bWqkv=np.ascontiguousarray(inputs["L0_attn_b_weight_qkv"], dtype=np.float32),
        bWo=np.ascontiguousarray(inputs["L0_attn_b_weight_out"], dtype=np.float32),
        W1=np.ascontiguousarray(inputs["L0_ffn_W1"], dtype=np.float32),
        W2=np.ascontiguousarray(inputs["L0_ffn_W2"], dtype=np.float32),
        b1v=np.ascontiguousarray(inputs["L0_ffn_b1"], dtype=np.float32),
        b2v=np.ascontiguousarray(inputs["L0_ffn_b2"], dtype=np.float32),
        lnw=np.ascontiguousarray(inputs["L0_ffn_ln_w"], dtype=np.float32),
        lnb=np.ascontiguousarray(inputs["L0_ffn_ln_b"], dtype=np.float32),
        identI=ident, pswapI=pswap, constsI=consts,
    )
    maps = []
    for c in range(NCORES):
        p0 = c * R
        if c == 0:
            xs = np.concatenate([np.zeros((HL, D), np.float32), x[:R]], axis=0)
        else:
            xs = x[p0 - HL: p0 + R]
        mm_, b0 = host_masks(c)
        m = dict(base)
        m.update(x_in=np.ascontiguousarray(xs), tabs=host_tables(c), maskM=mm_, maskB0=b0)
        maps.append(m)
    return maps


def kernel(**inputs):
    maps = make_in_maps(inputs)
    nc = build_program()
    res = run_bass_kernel_spmd(nc, maps, core_ids=list(range(NCORES)))
    outs = [np.asarray(r["out"], np.float32) for r in res.results]
    return np.concatenate(outs, axis=0).reshape(1, SEQ, D)


# version 29
# speedup vs baseline: 1.0292x; 1.0125x over previous
import numpy as np
import ml_dtypes
from contextlib import ExitStack
from itertools import zip_longest

import concourse.bass as bass
import concourse.mybir as mybir
from concourse.bass_utils import run_bass_kernel_spmd

F32 = mybir.dt.float32
BF16 = mybir.dt.bfloat16
I32 = mybir.dt.int32
I8 = mybir.dt.int8
AF = mybir.ActivationFunctionType
ALU = mybir.AluOpType
AX = mybir.AxisListType
BF = ml_dtypes.bfloat16

SEQ, D, NH, HD, WIN = 8192, 2048, 16, 128, 512
NCORES = 8
R = SEQ // NCORES
HL = WIN
T1 = R + HL
DH = 8192
KC = D // 128
KCH = DH // 128
NTC = T1 // 128
NTO = R // 128
EPS = 1e-5
CEXP = -12.0
MASKV = -30000.0
CONSTS = [0.0, 1.0, CEXP, -0.5, -1.0, EPS]

ENGINES = ("tensor", "vector", "scalar", "gpsimd", "sync")
NDMASEM = 8
LAST_SCHED = None


class _Op:
    def __init__(self, eng, fn, reads, writes, dma, tag, idx, fence, fsz=0):
        self.eng, self.fn = eng, fn
        self.reads, self.writes = tuple(reads), tuple(writes)
        self.dma, self.tag, self.idx = dma, tag, idx
        self.fsz = fsz
        self.fence = fence
        self.needs_inc = False
        self.done = None
        self.deps = None
        self.waits = None


class Sched:
    def __init__(self):
        self.ops = []
        self._fence_deps = ()
        self._col = None

    def add(self, eng, fn, reads=(), writes=(), dma=False, tag="", fsz=0):
        assert eng in ENGINES, eng
        if self._col is not None:
            self._col.append(lambda: self.add(eng, fn, reads, writes, dma, tag, fsz))
            return None
        o = _Op(eng, fn, reads, writes, dma, tag, len(self.ops), self._fence_deps, fsz)
        self.ops.append(o)
        return o

    def begin_collect(self):
        assert self._col is None
        self._col = []

    def end_collect(self):
        c, self._col = self._col, None
        return c

    @staticmethod
    def _ordered(d, o):
        return (d.eng == o.eng and not d.dma and not o.dma and
                (d.eng == "tensor" or (d.fsz >= 2048 and d.eng in ("vector", "scalar"))))

    def dma(self, eng, out, in_, reads=(), writes=(), tag="", **kw):
        fn = (lambda out=out, in_=in_, kw=kw: (lambda e: e.dma_start(out=out, in_=in_, **kw)))()
        return self.add(eng, fn, reads=reads, writes=writes, dma=True, tag=tag)

    def fence(self):
        last = {}
        for o in self.ops:
            last[(o.eng, o.dma)] = o
        self._fence_deps = tuple(last.values())

    def resolve(self):
        last_w = {}
        readers = {}
        for o in self.ops:
            deps = set(o.fence)
            for r in o.reads:
                w = last_w.get(r)
                if w is not None:
                    deps.add(w)
            for w_ in o.writes:
                lw = last_w.get(w_)
                if lw is not None:
                    deps.add(lw)
                deps.update(readers.get(w_, ()))
            deps.discard(o)
            o.deps = deps
            for r in o.reads:
                readers.setdefault(r, []).append(o)
            for w_ in o.writes:
                last_w[w_] = o
                readers[w_] = []
        for o in self.ops:
            for d in o.deps:
                if not self._ordered(d, o):
                    d.needs_inc = True
        cnt = {}
        dma_idx = {}
        for o in self.ops:
            if o.dma:
                i = dma_idx.get(o.eng, 0)
                dma_idx[o.eng] = i + 1
                key = ("dma", o.eng, i % NDMASEM)
                cnt[key] = 16 * (i // NDMASEM + 1)
                o.done = (key, cnt[key])
                o.dma_semkey = key
            elif o.needs_inc:
                key = ("eng", o.eng)
                cnt[key] = cnt.get(key, 0) + 1
                o.done = (key, cnt[key])
        self.final_counts = dict(cnt)
        marks = {e: {} for e in ENGINES}
        for o in self.ops:
            wl = {}
            for d in o.deps:
                if self._ordered(d, o):
                    continue
                assert d.done is not None, (d.tag, o.tag)
                k, v = d.done
                if wl.get(k, 0) < v:
                    wl[k] = v
            ww = marks[o.eng]
            o.waits = []
            for k, v in sorted(wl.items(), key=lambda kv: str(kv[0])):
                if ww.get(k, 0) < v:
                    o.waits.append((k, v))
                    ww[k] = v

    def emit(self, nc, blk, sems):
        by_eng = {}
        for o in self.ops:
            by_eng.setdefault(o.eng, []).append(o)
        for eng in ENGINES:
            if eng not in by_eng:
                continue
            ops = by_eng[eng]

            def body(e, ops=ops, eng=eng):
                for o in ops:
                    for k, v in o.waits:
                        e.wait_ge(sems[k], v)
                    inst = o.fn(e)
                    if o.dma:
                        inst.then_inc(sems[o.dma_semkey], 16)
                    elif o.needs_inc:
                        inst.then_inc(sems[("eng", eng)], 1)
                if eng == "sync":
                    for k, c in sorted(self.final_counts.items(), key=lambda kv: str(kv[0])):
                        if k[0] == "dma":
                            e.wait_ge(sems[k], c)
            getattr(blk, eng)(body)


def host_tables(core_id):
    p0 = core_id * R
    pos = np.arange(p0 - HL, p0 + R, dtype=np.float64)
    pos = np.where(pos < 0, 0, pos).astype(np.float32)
    inv = (1.0 / (np.float32(10000.0) ** (np.arange(0, HD, 2, dtype=np.float32) / np.float32(HD)))).astype(np.float32)
    fr = (pos[None, :] * inv[:, None]).astype(np.float32)
    cos = np.cos(fr).astype(np.float32)
    sin = np.sin(fr).astype(np.float32)
    cosF = np.empty((HD, T1), np.float32)
    sinF = np.empty((HD, T1), np.float32)
    cosF[0::2] = cos; cosF[1::2] = cos
    sinF[0::2] = -sin; sinF[1::2] = sin
    s = np.float32(HD ** (-0.5))
    return np.concatenate([cosF, sinF, cosF * s, sinF * s], axis=1).astype(BF)


def host_masks(core_id):
    j = np.arange(128)[:, None]
    i = np.arange(WIN)[None, :]
    tiles = []
    for jc in range(8):
        jl = jc * 128 + j
        valid = (jl > i) & (jl <= i + WIN)
        tiles.append(np.where(valid, 0.0, MASKV).astype(np.float32))
    main = np.concatenate(tiles, axis=1)
    if core_id == 0:
        b0 = np.full((128, 4 * WIN), MASKV, np.float32)
    else:
        b0 = np.concatenate(tiles[:4], axis=1)
    return main.astype(BF), b0.astype(BF)


def host_ident():
    ident = np.eye(128, dtype=np.float32).astype(BF)
    sw = np.arange(128)
    sw = (sw // 2) * 2 + (1 - sw % 2)
    pswap = np.eye(128, dtype=np.float32)[sw].astype(BF)
    return ident, pswap


def host_consts():
    c = np.zeros((128, 8), np.float32)
    c[:, :len(CONSTS)] = np.array(CONSTS, np.float32)[None, :]
    return c


def build_program(phases=("wprep", "attn", "ffn1", "attnb", "ffn2"), debug=()):
    nc = bass.Bass("TRN2", target_bir_lowering=False)
    sch = Sched()

    def din(name, shape, dtype=F32):
        return nc.dram_tensor(name, list(shape), dtype, kind="ExternalInput").ap()

    x_in = din("x_in", [T1, D])
    aWq = din("aWq", [D, D]); aWk = din("aWk", [D, D])
    aWv = din("aWv", [D, D]); aWo = din("aWo", [D, D])
    bWqkv = din("bWqkv", [3 * D, D]); bWo = din("bWo", [D, D])
    W1 = din("W1", [DH, D]); W2 = din("W2", [D, DH])
    b1v = din("b1v", [DH]); b2v = din("b2v", [D])
    lnw = din("lnw", [D]); lnb = din("lnb", [D])
    tabs = din("tabs", [128, 4 * T1], BF16)
    maskM = din("maskM", [128, 8 * WIN], BF16)
    maskB0 = din("maskB0", [128, 4 * WIN], BF16)
    identI = din("identI", [128, 128], BF16)
    pswapI = din("pswapI", [128, 128], BF16)
    constsI = din("constsI", [128, 8])

    out = nc.dram_tensor("out", [R, D], F32, kind="ExternalOutput").ap()

    dbg_tensors = {}

    def dbg(name, ap, rg, shape, dtype):
        if name in debug:
            t = nc.dram_tensor(name, list(shape), dtype, kind="ExternalOutput").ap()
            dbg_tensors[name] = t
            sch.dma("sync", t, ap, reads=[rg], tag="dump_" + name)

    def scr(name, shape, dtype=BF16):
        kind = "ExternalOutput" if name in debug else "Internal"
        return nc.dram_tensor(name, list(shape), dtype, kind=kind).ap()

    sc_qW1T = scr("sc_qW1T", [D, DH]); sc_qW2T = scr("sc_qW2T", [DH, D])
    sc_X = scr("sc_X", [R, D], F32)

    ctx = ExitStack()
    with ctx:
        def sb(name, shape, dtype=BF16, stack=None):
            return (stack or ctx).enter_context(nc.sbuf_tensor(name, shape, dtype))

        pb = [ctx.enter_context(nc.psum_tensor(f"pb{i}", [128, 512], F32)) for i in range(8)]
        PB = [f"pb{i}" for i in range(8)]
        strip0 = sb("strip0", [128, 16 * 512], BF16)
        strip1 = sb("strip1", [128, 16 * 512], BF16)
        sconsts = sb("sconsts", [128, 8], F32)
        ident = sb("ident", [128, 128], BF16)
        pswap = sb("pswap", [128, 128], BF16)
        ones_r = sb("ones_r", [1, 128], BF16)
        ones_rf = sb("ones_rf", [1, 128], F32)
        ones_c = sb("ones_c", [128, 1], BF16)
        ln2stats = sb("ln2stats", [128, 2 * NTO], F32)

        for ci, cv in enumerate(CONSTS):
            nc.const_aps.register(F32, cv, sconsts[:, ci:ci + 1])

        sch.dma("sync", sconsts[:], constsI, writes=["consts"], tag="ld_consts")
        sch.dma("sync", ident[:], identI, writes=["ident"], tag="ld_ident")
        sch.dma("sync", pswap[:], pswapI, writes=["pswap"], tag="ld_pswap")
        sch.add("vector", lambda e: e.memset(ones_r[:], 1.0), writes=["ones_r"])
        sch.add("vector", lambda e: e.memset(ones_rf[:], 1.0), writes=["ones_rf"])
        sch.add("vector", lambda e: e.memset(ones_c[:], 1.0), writes=["ones_c"])

        def quant_chunk(src_ap, src_rg, F, qt, tag):
            G = F // 32
            (amax, amaxr), (ei, eir), (rcp, rcpr), (scl, sclr), (i8t, i8r), (qo, qor) = (
                qt["amax"], qt["ei"], qt["recip"], qt["scale"], qt["i8"], qt["q"])
            sch.add("vector", lambda e: e.tensor_reduce(
                amax[:, 0:G].unsqueeze(2), src_ap.rearrange("p (g b) -> p g b", b=32), AX.X, ALU.max,
                apply_absolute_value=True), reads=[src_rg], writes=[amaxr], tag=tag + "am")
            sch.add("vector", lambda e: e.tensor_single_scalar(
                ei[:, 0:G], amax[:, 0:G].bitcast(I32), 23, ALU.logical_shift_right),
                reads=[amaxr], writes=[eir], tag=tag + "e")
            sch.add("vector", lambda e: e.tensor_scalar(
                rcp[:, 0:G].bitcast(I32), ei[:, 0:G], -1, 260, ALU.mult, ALU.add),
                reads=[eir], writes=[rcpr], tag=tag + "r1")
            sch.add("vector", lambda e: e.tensor_single_scalar(
                rcp[:, 0:G].bitcast(I32), rcp[:, 0:G].bitcast(I32), 23, ALU.logical_shift_left),
                reads=[rcpr], writes=[rcpr], tag=tag + "r2")
            sch.add("vector", lambda e: e.tensor_scalar(
                scl[:, 0:G].bitcast(I32), ei[:, 0:G], 1, -6, ALU.mult, ALU.add),
                reads=[eir], writes=[sclr], tag=tag + "s1")
            sch.add("vector", lambda e: e.tensor_single_scalar(
                scl[:, 0:G].bitcast(I32), scl[:, 0:G].bitcast(I32), 23, ALU.logical_shift_left),
                reads=[sclr], writes=[sclr], tag=tag + "s2")
            rB = rcp[:, 0:G].unsqueeze(2).broadcast_to([128, G, 32])
            sB = scl[:, 0:G].unsqueeze(2).broadcast_to([128, G, 32])
            sch.add("vector", lambda e: e.tensor_tensor(
                i8t[:, 0:F].rearrange("p (g b) -> p g b", b=32),
                src_ap.rearrange("p (g b) -> p g b", b=32), rB, ALU.mult),
                reads=[src_rg, rcpr], writes=[i8r], tag=tag + "i8", fsz=F)
            sch.add("vector", lambda e: e.tensor_tensor(
                qo[:, 0:F].rearrange("p (g b) -> p g b", b=32),
                i8t[:, 0:F].rearrange("p (g b) -> p g b", b=32), sB, ALU.mult),
                reads=[i8r, sclr], writes=[qor], tag=tag + "q", fsz=F)

        def make_qt(pre, Fmax, stack, share_i8=None):
            G = Fmax // 32
            return dict(
                amax=(sb(pre + "amax", [128, G], F32, stack), pre + "amax"),
                ei=(sb(pre + "ei", [128, G], I32, stack), pre + "ei"),
                recip=(sb(pre + "recip", [128, G], F32, stack), pre + "recip"),
                scale=(sb(pre + "scale", [128, G], F32, stack), pre + "scale"),
                i8=(share_i8 if share_i8 is not None else (sb(pre + "i8", [128, Fmax], I8, stack), pre + "i8")),
                q=(sb(pre + "q", [128, Fmax], BF16, stack), pre + "q"),
            )

        def transpose_pack(src_ap, src_rg, chunk_cols, banks, evac, tag, perm=None):
            pmat, pmn = (perm, "pswap") if perm is not None else (ident, "ident")
            nb = len(banks)
            for gi in range(0, len(chunk_cols), 8):
                n = min(8, len(chunk_cols) - gi)
                bsel = banks[(gi // 8) % nb]
                brg = PB[bsel]
                for j in range(n):
                    col = chunk_cols[gi + j]
                    sch.add("tensor", (lambda bsel=bsel, j=j, col=col, pmat=pmat: (lambda e: e.transpose(
                        pb[bsel][:].bitcast(BF16)[:, j * 128:(j + 1) * 128],
                        src_ap[:, col:col + 128], pmat[:])))(),
                        reads=[src_rg, pmn], writes=[brg], tag=tag + f"T{gi + j}")
                evac(bsel, gi, n, brg)

        def ln_chunk(xap, xrg, stats, stats_rg, scol, sqscr, sqrg, tag, save_to=None):
            rN = 1.0 / D
            sch.add("vector", lambda e: e.tensor_reduce(
                stats[:, scol + 2:scol + 3], xap, AX.X, ALU.add),
                reads=[xrg], writes=[stats_rg], tag=tag + "sm")
            sch.add("scalar", lambda e: e.activation(sqscr, xap, AF.Square, bias=0.0, scale=1.0,
                                                     accum_out=stats[:, scol + 3:scol + 4]),
                    reads=[xrg, "consts"], writes=[stats_rg, sqrg], tag=tag + "sq")
            sch.add("vector", lambda e: e.tensor_single_scalar(
                stats[:, scol:scol + 1], stats[:, scol + 2:scol + 3], rN, ALU.mult),
                reads=[stats_rg], writes=[stats_rg], tag=tag + "mn")
            sch.add("vector", lambda e: e.tensor_tensor(
                stats[:, scol + 4:scol + 5], stats[:, scol:scol + 1], stats[:, scol:scol + 1], ALU.mult),
                reads=[stats_rg], writes=[stats_rg], tag=tag + "m2")
            sch.add("vector", lambda e: e.tensor_single_scalar(
                stats[:, scol + 5:scol + 6], stats[:, scol + 3:scol + 4], rN, ALU.mult),
                reads=[stats_rg], writes=[stats_rg], tag=tag + "v1")
            sch.add("vector", lambda e: e.tensor_tensor(
                stats[:, scol + 1:scol + 2], stats[:, scol + 5:scol + 6], stats[:, scol + 4:scol + 5],
                ALU.subtract),
                reads=[stats_rg], writes=[stats_rg], tag=tag + "vr")
            if save_to is not None:
                sv, svrg, svcol = save_to
                sch.add("vector", lambda e: e.tensor_copy(sv[:, svcol:svcol + 1], stats[:, scol:scol + 1]),
                        reads=[stats_rg], writes=[svrg], tag=tag + "sv")
            sch.add("scalar", lambda e: e.activation(stats[:, scol + 1:scol + 2], stats[:, scol + 1:scol + 2],
                                                     AF.Ln, bias=EPS, scale=1.0),
                    reads=[stats_rg, "consts"], writes=[stats_rg], tag=tag + "ln")
            sch.add("scalar", lambda e: e.activation(stats[:, scol + 1:scol + 2], stats[:, scol + 1:scol + 2],
                                                     AF.Exp, bias=0.0, scale=-0.5),
                    reads=[stats_rg, "consts"], writes=[stats_rg], tag=tag + "ex")
            if save_to is not None:
                sv, svrg, svcol = save_to
                sch.add("vector", lambda e: e.tensor_copy(sv[:, svcol + 1:svcol + 2], stats[:, scol + 1:scol + 2]),
                        reads=[stats_rg], writes=[svrg], tag=tag + "sv2")

        def ln_apply(xap, xrg, stats, stats_rg, scol, outap, outrg, tag):
            sch.add("vector", lambda e: e.tensor_scalar(
                outap, xap, stats[:, scol:scol + 1], stats[:, scol + 1:scol + 2],
                ALU.subtract, ALU.mult), reads=[xrg, stats_rg], writes=[outrg], tag=tag + "ap", fsz=D)

        def mk_wstrip(wbuf, wrg, w_dram, grow, tbanks, wcb, wcbpre, tag):
            for rc in range(4):
                wt = wcb[rc % 2]; wtrg = wcbpre + str(rc % 2)
                sch.dma("gpsimd", wt[:],
                        w_dram[grow * 512 + rc * 128: grow * 512 + (rc + 1) * 128, :],
                        writes=[wtrg], tag=f"{tag}ld{rc}")
                for gi in range(2):
                    bsel = tbanks[gi]
                    for j in range(8):
                        ic = gi * 8 + j
                        sch.add("tensor", (lambda bsel=bsel, j=j, ic=ic, wt=wt: (lambda e: e.transpose(
                            pb[bsel][:].bitcast(BF16)[:, j * 128:(j + 1) * 128],
                            wt[:, ic * 128:(ic + 1) * 128], ident[:])))(),
                            reads=[wtrg, "ident"], writes=[PB[bsel]], tag=f"{tag}T{rc}_{ic}")
                    sch.add("vector", (lambda bsel=bsel, gi=gi, rc=rc, wbuf=wbuf: (lambda e: e.tensor_copy(
                        wbuf[:].rearrange("p (kc o) -> p kc o", o=512)[:, gi * 8:gi * 8 + 8, rc * 128:(rc + 1) * 128],
                        pb[bsel][:].bitcast(BF16)[:, 0:1024].rearrange("p (c q) -> p c q", q=128))))(),
                        reads=[PB[bsel]], writes=[wrg], tag=f"{tag}e{rc}_{gi}", fsz=1024)

        if "wprep" in phases:
            wp = ExitStack()
            with wp:
                NWR = 4
                wbig = [sb(f"wbig{i}", [128, 2048], F32, wp) for i in range(NWR)]
                qtw = [make_qt(f"wq{i}_", 2048, wp) for i in range(NWR)]
                wtev = [sb(f"wtev{i}", [128, 1024], BF16, wp) for i in range(2 * NWR)]
                _wn = [0]

                def prep_weight(src_ap, dest_scT, o_chunks, k_cols, do_quant, tag, screg, dest_scT2=None,
                                screg2=None):
                    npc = k_cols // 2048
                    dview = dest_scT.rearrange("(kc p) o -> p kc o", p=128)
                    dview2 = dest_scT2.rearrange("(kc p) o -> p kc o", p=128) if dest_scT2 is not None else None
                    for oc in range(o_chunks):
                        for p in range(npc):
                            par = _wn[0] % NWR
                            _wn[0] += 1
                            w = wbig[par]; wrg = f"wbig{par}"
                            qtc = qtw[par]
                            sch.dma("sync", w[:], src_ap[oc * 128:(oc + 1) * 128, p * 2048:(p + 1) * 2048],
                                    writes=[wrg], tag=f"{tag}ld{oc}_{p}")
                            if do_quant:
                                quant_chunk(w[:], wrg, 2048, qtc, f"{tag}qz{oc}_{p}")
                            else:
                                sch.add("vector", (lambda w=w, qtc=qtc: (lambda e: e.tensor_copy(
                                    qtc["q"][0][:, 0:2048], w[:])))(),
                                    reads=[wrg], writes=[qtc["q"][1]], tag=f"{tag}cast{oc}_{p}", fsz=2048)
                            tsrc, tsrc_rg = qtc["q"][0], qtc["q"][1]

                            def mk_evac(oc, p, dv, sfx, sreg, par):
                                def evac(bsel, gi, n, brg):
                                    ev = wtev[par + (NWR if sfx == "p" else 0)]
                                    evrg = f"wtev{par + (NWR if sfx == 'p' else 0)}"
                                    sch.add("vector", (lambda ev=ev, bsel=bsel, n=n: (lambda e: e.tensor_copy(
                                        ev[:, 0:n * 128], pb[bsel][:].bitcast(BF16)[:, 0:n * 128])))(),
                                        reads=[brg], writes=[evrg], tag=f"{tag}tev{sfx}{oc}_{p}_{gi}", fsz=n * 128)
                                    sch.dma("gpsimd",
                                            dv[:, p * 16 + gi:p * 16 + gi + n, oc * 128:(oc + 1) * 128],
                                            ev[:, 0:n * 128].rearrange("p (c q) -> p c q", q=128),
                                            reads=[evrg], writes=[f"{sreg}#{oc % 4}"], tag=f"{tag}st{sfx}{oc}_{p}_{gi}")
                                return evac

                            cols = [c * 128 for c in range(16)]
                            transpose_pack(tsrc, tsrc_rg, cols, [4, 5], mk_evac(oc, p, dview, "n", screg, par),
                                           f"{tag}t{oc}_{p}_")
                            if dview2 is not None:
                                transpose_pack(tsrc, tsrc_rg, cols, [6, 7], mk_evac(oc, p, dview2, "p", screg2, par),
                                               f"{tag}tp{oc}_{p}_", perm=pswap)

                prep_weight(W1, sc_qW1T, KCH, D, True, "pw1", "scqW1T")
                prep_weight(W2, sc_qW2T, KC, DH, True, "pw2", "scqW2T")
            sch.fence()

        if "attn" in phases:
            ap_ = ExitStack()
            with ap_:
                xln_fm = sb("xln_fm", [128, KC * T1], BF16, ap_)
                attnout = sb("attnout", [128, NH * R], BF16, ap_)

                a1 = ExitStack()
                with a1:
                    xc = [sb(f"a_xc{i}", [128, D], F32, a1) for i in range(2)]
                    xlntm = [sb(f"a_xt{i}", [128, D], BF16, a1) for i in range(2)]
                    lstats = sb("a_ls", [128, 16], F32, a1)
                    sqs1 = sb("a_sq", [128, D], F32, a1)

                    def a1_chain(tc):
                        xt = xc[tc % 2]; xtrg = f"a_xc{tc % 2}"
                        lt = xlntm[tc % 2]; ltrg = f"a_xt{tc % 2}"
                        scol = (tc % 2) * 8
                        sch.begin_collect()
                        sch.dma("sync", xt[:], x_in[tc * 128:(tc + 1) * 128, :], writes=[xtrg], tag=f"a_lx{tc}")
                        ln_chunk(xt[:, 0:D], xtrg, lstats, f"a_ls{tc % 2}", scol, sqs1[:], "a_sq", f"a_l{tc}")
                        ln_apply(xt[:, 0:D], xtrg, lstats, f"a_ls{tc % 2}", scol, lt[:, 0:D], ltrg, f"a_l{tc}")
                        chain = sch.end_collect()

                        def do_T(tc=tc, lt=lt, ltrg=ltrg):
                            def evx(bsel, gi, n, brg):
                                sch.add("vector", (lambda bsel=bsel, gi=gi, n=n: (lambda e: e.tensor_copy(
                                    xln_fm.rearrange("p (c t) -> p c t", t=T1)[:, gi:gi + n, tc * 128:(tc + 1) * 128],
                                    pb[bsel][:].bitcast(BF16)[:, 0:n * 128].rearrange("p (c q) -> p c q", q=128))))(),
                                    reads=[brg], writes=["xln_fm"], tag=f"a_xe{tc}_{gi}")
                            transpose_pack(lt, ltrg, [c * 128 for c in range(KC)], [4, 5], evx, f"a_xT{tc}_")
                        chain.append(do_T)
                        return chain

                    for pi in range(0, NTC, 2):
                        chains = [a1_chain(pi), a1_chain(pi + 1)]
                        for st in zip_longest(*chains):
                            for t in st:
                                if t is not None:
                                    t()
                sch.fence()
                dbg("dbg_xln", xln_fm[:], "xln_fm", [128, KC * T1], BF16)

                pass

                a2 = ExitStack()
                with a2:
                    tabsb = sb("tabsb", [128, 4 * T1], BF16, a2)
                    maskm = sb("maskm", [128, 8 * WIN], BF16, a2)
                    maskb0 = sb("maskb0", [128, 4 * WIN], BF16, a2)
                    sch.dma("sync", tabsb[:], tabs, writes=["tabsb"], tag="ld_tabs")
                    sch.dma("sync", maskm[:], maskM, writes=["maskm"], tag="ld_maskm")
                    sch.dma("sync", maskb0[:], maskB0, writes=["maskb0"], tag="ld_maskb0")
                    q_fm = sb("q_fm", [128, 4 * R], BF16, a2)
                    k_fm = sb("k_fm", [128, 4 * T1], BF16, a2)
                    v_tm = sb("v_tm", [128, NTC * 512], BF16, a2)
                    p_sbr = [sb(f"p_sb{i}", [128, 8 * WIN], BF16, a2) for i in range(2)]
                    rt = [sb(f"a_rt{i}", [128, 512], BF16, a2) for i in range(4)]
                    ssumr = [sb(f"a_ss{i}", [1, 512], F32, a2) for i in range(2)]
                    srecipr = [sb(f"a_sr{i}", [1, 512], F32, a2) for i in range(2)]
                    recipBr = [sb(f"a_rB{i}", [128, 512], F32, a2) for i in range(2)]
                    wcb2 = [strip1[:, 0:D], strip1[:, D:2 * D]]

                    SWAPM = []
                    for i in range(0, 32, 2):
                        SWAPM += [i + 1, i]
                    for g in range(4):
                        for (wsrc, dst, dstrg, nt, toff, ctab, stab) in (
                            (aWq, q_fm, "q_fm", 2, HL, 2 * T1, 3 * T1),
                            (aWk, k_fm, "k_fm", 3, 0, 0, T1),
                        ):
                            mk_wstrip(strip0, "strip0", wsrc, g, (3, 4), wcb2, "a_wc", f"a_w{dstrg}{g}")
                            for ho in range(4):
                                for tt in range(nt):
                                    t0 = toff + tt * 512
                                    bank = (ho * nt + tt) % 2
                                    for kc2 in range(KC):
                                        sch.add("tensor", (lambda bank=bank, kc2=kc2, ho=ho, t0=t0: (lambda e: e.matmul(
                                            pb[bank][:],
                                            strip0[:, kc2 * 512 + ho * 128: kc2 * 512 + (ho + 1) * 128],
                                            xln_fm[:, kc2 * T1 + t0: kc2 * T1 + t0 + 512],
                                            start=(kc2 == 0), stop=(kc2 == KC - 1))))(),
                                            reads=(["strip0", "xln_fm"] + ([PB[bank]] if kc2 else [])),
                                            writes=[PB[bank]], tag=f"a_p{dstrg}{g}_{ho}_{tt}_{kc2}")
                                    ctt = ctab + t0
                                    stt_ = stab + t0
                                    dcol = ho * (R if dstrg == "q_fm" else T1) + tt * 512
                                    pr = ((ho * nt + tt) % 2) * 2
                                    r0, r1_ = rt[pr], rt[pr + 1]
                                    r0n, r1n = f"a_rt{pr}", f"a_rt{pr + 1}"
                                    sch.add("vector", (lambda bank=bank, r0=r0: (lambda e: e.tensor_copy(
                                        r0[:], pb[bank][:])))(),
                                        reads=[PB[bank]], writes=[r0n], tag=f"a_rv{dstrg}{g}_{ho}_{tt}")
                                    sch.add("vector", (lambda r0=r0, r1_=r1_: (lambda e: e.stream_shuffle(
                                        r1_[:], r0[:], SWAPM)))(),
                                        reads=[r0n], writes=[r1n], tag=f"a_rs{dstrg}{g}_{ho}_{tt}")
                                    sch.add("vector", (lambda ctt=ctt, r0=r0: (lambda e: e.tensor_tensor(
                                        r0[:], r0[:], tabsb[:, ctt:ctt + 512], ALU.mult)))(),
                                        reads=[r0n, "tabsb"], writes=[r0n], tag=f"a_r1{dstrg}{g}_{ho}_{tt}")
                                    sch.add("vector", (lambda stt_=stt_, r1_=r1_: (lambda e: e.tensor_tensor(
                                        r1_[:], r1_[:], tabsb[:, stt_:stt_ + 512], ALU.mult)))(),
                                        reads=[r1n, "tabsb"], writes=[r1n], tag=f"a_r2{dstrg}{g}_{ho}_{tt}")
                                    sch.add("vector", (lambda dst=dst, dcol=dcol, r0=r0, r1_=r1_: (lambda e: e.tensor_tensor(
                                        dst[:, dcol:dcol + 512], r0[:], r1_[:], ALU.add)))(),
                                        reads=[r0n, r1n], writes=[dstrg], tag=f"a_r3{dstrg}{g}_{ho}_{tt}")

                        mk_wstrip(strip0, "strip0", aWv, g, (3, 4), wcb2, "a_wc", f"a_wv{g}")
                        for tc in range(NTC):
                            for kc2 in range(KC):
                                sch.add("tensor", (lambda kc2=kc2, tc=tc: (lambda e: e.matmul(
                                    pb[2][:],
                                    xln_fm[:, kc2 * T1 + tc * 128: kc2 * T1 + (tc + 1) * 128],
                                    strip0[:, kc2 * 512:(kc2 + 1) * 512],
                                    start=(kc2 == 0), stop=(kc2 == KC - 1))))(),
                                    reads=(["strip0", "xln_fm"] + ([PB[2]] if kc2 else [])),
                                    writes=[PB[2]], tag=f"a_pv{g}_{tc}_{kc2}")
                            sch.add("vector", (lambda tc=tc: (lambda e: e.tensor_copy(
                                v_tm[:, tc * 512:(tc + 1) * 512], pb[2][:])))(),
                                reads=[PB[2]], writes=["v_tm"], tag=f"a_ve{g}_{tc}")

                        if g == 0:
                            dbg("dbg_q", q_fm[:], "q_fm", [128, 4 * R], BF16)
                            dbg("dbg_k", k_fm[:], "k_fm", [128, 4 * T1], BF16)
                            dbg("dbg_v", v_tm[:], "v_tm", [128, NTC * 512], BF16)
                        def u_stage1(ho, b):
                            h = g * 4 + ho
                            up = b
                            p_sb = p_sbr[up]; p_rg = f"p_sb{up}"
                            for jc in range(8):
                                bank = 3 + (jc % 2)
                                kcol = ho * T1 + b * 512 + jc * 128
                                qcol = ho * R + b * 512
                                sch.add("tensor", (lambda bank=bank, kcol=kcol, qcol=qcol: (lambda e: e.matmul(
                                    pb[bank][:], k_fm[:, kcol:kcol + 128], q_fm[:, qcol:qcol + 512],
                                    start=True, stop=False)))(),
                                    reads=["k_fm", "q_fm"], writes=[PB[bank]], tag=f"a_s{h}_{b}_{jc}")
                                if b == 0 and jc < 4:
                                    mt, mtrg, mcol = maskb0, "maskb0", jc * 512
                                else:
                                    mt, mtrg, mcol = maskm, "maskm", jc * 512
                                sch.add("tensor", (lambda bank=bank, mt=mt, mcol=mcol: (lambda e: e.matmul(
                                    pb[bank][:], ident[:], mt[:, mcol:mcol + 512],
                                    start=False, stop=True)))(),
                                    reads=[mtrg, "ident", PB[bank]], writes=[PB[bank]], tag=f"a_m{h}_{b}_{jc}")
                                sch.add("scalar", (lambda bank=bank, jc=jc, p_sb=p_sb: (lambda e: e.activation(
                                    p_sb[:, jc * 512:(jc + 1) * 512], pb[bank][:], AF.Exp,
                                    bias=CEXP, scale=1.0)))(),
                                    reads=[PB[bank], "consts"], writes=[p_rg], tag=f"a_e{h}_{b}_{jc}")

                        def u_sums(ho, b, su_b):
                            h = g * 4 + ho
                            up = b
                            p_sb = p_sbr[up]; p_rg = f"p_sb{up}"
                            ssum = ssumr[up]; ss_rg = f"a_ss{up}"
                            for jc in range(8):
                                sch.add("tensor", (lambda jc=jc, p_sb=p_sb, su_b=su_b: (lambda e: e.matmul(
                                    pb[su_b][0:1, :], ones_c[:], p_sb[:, jc * 512:(jc + 1) * 512],
                                    start=(jc == 0), stop=(jc == 7))))(),
                                    reads=(["ones_c", p_rg] + ([PB[su_b]] if jc else [])),
                                    writes=[PB[su_b]], tag=f"a_su{h}_{b}_{jc}")
                            sch.add("vector", (lambda ssum=ssum, su_b=su_b: (lambda e: e.tensor_copy(
                                ssum[:], pb[su_b][0:1, :])))(),
                                reads=[PB[su_b]], writes=[ss_rg], tag=f"a_se{h}_{b}")
                            sch.add("scalar", (lambda ssum=ssum, srecip=srecipr[up]: (lambda e: e.activation(
                                srecip[:], ssum[:], AF.Ln, bias=0.0, scale=1.0)))(),
                                reads=[ss_rg, "consts"], writes=[f"a_sr{up}"], tag=f"a_rl{h}_{b}")
                            sch.add("scalar", (lambda srecip=srecipr[up]: (lambda e: e.activation(
                                srecip[:], srecip[:], AF.Exp, bias=0.0, scale=-1.0)))(),
                                reads=[f"a_sr{up}", "consts"], writes=[f"a_sr{up}"], tag=f"a_re{h}_{b}")

                        def u_stage2(ho, b, rb_b, av_b):
                            h = g * 4 + ho
                            up = b
                            p_sb = p_sbr[up]; p_rg = f"p_sb{up}"
                            ssum = ssumr[up]; ss_rg = f"a_ss{up}"
                            srecip = srecipr[up]; sr_rg = f"a_sr{up}"
                            recipB = recipBr[up]; rB_rg = f"a_rB{up}"
                            sch.add("tensor", (lambda srecip=srecip, rb_b=rb_b: (lambda e: e.matmul(
                                pb[rb_b][:], ones_rf[:], srecip[:], start=True, stop=True)))(),
                                reads=[sr_rg, "ones_rf"], writes=[PB[rb_b]], tag=f"a_rb{h}_{b}")
                            sch.add("vector", (lambda recipB=recipB, rb_b=rb_b: (lambda e: e.tensor_copy(
                                recipB[:], pb[rb_b][:])))(),
                                reads=[PB[rb_b]], writes=[rB_rg], tag=f"a_rbe{h}_{b}")
                            for jc in range(8):
                                vcol = (4 * b + jc) * 512 + ho * 128
                                sch.add("tensor", (lambda jc=jc, vcol=vcol, p_sb=p_sb, av_b=av_b: (lambda e: e.matmul(
                                    pb[av_b][:], v_tm[:, vcol:vcol + 128], p_sb[:, jc * 512:(jc + 1) * 512],
                                    start=(jc == 0), stop=(jc == 7))))(),
                                    reads=(["v_tm", p_rg] + ([PB[av_b]] if jc else [])),
                                    writes=[PB[av_b]], tag=f"a_av{h}_{b}_{jc}")
                            acol = h * R + b * 512
                            sch.add("vector", (lambda acol=acol, recipB=recipB, av_b=av_b: (lambda e: e.tensor_tensor(
                                attnout[:, acol:acol + 512], pb[av_b][:], recipB[:], ALU.mult)))(),
                                reads=[PB[av_b], rB_rg], writes=["attnout"], tag=f"a_ao{h}_{b}")
                            if h == 0 and b == 1:
                                dbg("dbg_p", p_sb[:], p_rg, [128, 8 * WIN], BF16)
                                dbg("dbg_sum", ssum[:], ss_rg, [1, 512], F32)
                                dbg("dbg_recipB", recipB[:], rB_rg, [128, 512], F32)

                        for ho in range(4):
                            u_stage1(ho, 0)
                            u_stage1(ho, 1)
                            u_sums(ho, 0, 5)
                            u_sums(ho, 1, 0)
                            u_stage2(ho, 0, 6, 7)
                            u_stage2(ho, 1, 1, 2)
                sch.fence()
                dbg("dbg_attnout", attnout[:], "attnout", [128, NH * R], BF16)

                a3 = ExitStack()
                with a3:
                    xres = [sb(f"a_xr{i}", [128, D], F32, a3) for i in range(NTO)]
                    wcb3 = [sb(f"a_w3c{i}", [128, D], BF16, a3) for i in range(2)]
                    for tch in range(NTO):
                        sch.dma("sync", xres[tch][:], x_in[HL + tch * 128: HL + (tch + 1) * 128, :],
                                writes=[f"a_xr{tch}"], tag=f"a_lr{tch}")
                    for ot in range(4):
                        wbuf = strip0 if ot % 2 == 0 else strip1
                        wrg = "strip0" if ot % 2 == 0 else "strip1"
                        mk_wstrip(wbuf, wrg, aWo, ot, (6, 7), wcb3, "a_w3c", f"a_wo{ot}")
                        for tch in range(NTO):
                            bank = tch % 4
                            for ic in range(KC):
                                sch.add("tensor", (lambda bank=bank, ic=ic, tch=tch, wbuf=wbuf: (lambda e: e.matmul(
                                    pb[bank][:], attnout[:, ic * R + tch * 128: ic * R + (tch + 1) * 128],
                                    wbuf[:, ic * 512:(ic + 1) * 512],
                                    start=(ic == 0), stop=(ic == KC - 1))))(),
                                    reads=(["attnout", wrg] + ([PB[bank]] if ic else [])),
                                    writes=[PB[bank]], tag=f"a_o{ot}_{tch}_{ic}")
                            sch.add("vector", (lambda bank=bank, tch=tch, ot=ot: (lambda e: e.tensor_tensor(
                                xres[tch][:, ot * 512:(ot + 1) * 512], pb[bank][:],
                                xres[tch][:, ot * 512:(ot + 1) * 512], ALU.add)))(),
                                reads=[PB[bank], f"a_xr{tch}"], writes=[f"a_xr{tch}"], tag=f"a_or{ot}_{tch}")
                    for tch in range(NTO):
                        sch.dma("gpsimd", sc_X[tch * 128:(tch + 1) * 128, :], xres[tch][:],
                                reads=[f"a_xr{tch}"], writes=[f"scX{tch}_{q}" for q in range(4)],
                                tag=f"a_sx{tch}")
            sch.fence()

        def ffn_phase(idx, x_src, final):
            fp_ = ExitStack()
            with fp_:
                pre = f"f{idx}"
                gamB = sb(pre + "gamB", [128, D], BF16, fp_)
                betB = sb(pre + "betB", [128, D], BF16, fp_)
                b1t = [sb(pre + f"b1t{i}", [1, 512], BF16, fp_) for i in range(2)]
                b2t = [sb(pre + f"b2t{i}", [1, 512], BF16, fp_) for i in range(2)]
                xc2 = [sb(pre + f"xc{i}", [128, D], F32, fp_) for i in range(2)]
                xb2 = [sb(pre + f"xb{i}", [128, D], BF16, fp_) for i in range(2)]
                t0b = sb(pre + "t0", [128, D], F32, fp_)
                lstat = sb(pre + "ls", [128, 16], F32, fp_)
                lnq_fm = sb(pre + "lnq", [128, KC * 512], BF16, fp_)
                hq_fm = sb(pre + "hq", [128, KCH * 512], BF16, fp_)
                hg = [sb(pre + f"hg{i}", [128, 4 * 512], BF16, fp_) for i in range(2)]
                res = [sb(pre + f"res{i}", [128, D], BF16, fp_) for i in range(4)]
                xs2 = [sb(pre + f"xs{i}", [128, 512], F32, fp_) for i in range(2)]
                f2r = [sb(pre + f"f2{i}", [128, 512], BF16, fp_) for i in range(4)]
                qt2 = [make_qt(pre + "q0", D, fp_), make_qt(pre + "q1", D, fp_)]

                rw_ = ExitStack()
                with rw_:
                    rows = [sb(pre + f"lnr{i}", [1, 512], BF16, rw_) for i in range(2)]
                    for nm, srcv, dst, dstrg in (("g", lnw, gamB, pre + "gamB"),
                                                 ("b", lnb, betB, pre + "betB")):
                        for j in range(4):
                            row = rows[j % 2]; rowrg = pre + f"lnr{j % 2}"
                            sch.dma("gpsimd", row[:], srcv.rearrange("(a b) -> a b", a=1)[0:1, j * 512:(j + 1) * 512],
                                    writes=[rowrg], tag=f"{pre}ld{nm}{j}")
                            sch.add("tensor", (lambda row=row: (lambda e: e.matmul(
                                pb[0][:], ones_r[:], row[0:1, :], start=True, stop=True)))(),
                                reads=[rowrg, "ones_r"], writes=[PB[0]], tag=f"{pre}gb{nm}{j}")
                            sch.add("vector", (lambda dst=dst, j=j: (lambda e: e.tensor_copy(
                                dst[:, j * 512:(j + 1) * 512], pb[0][:])))(),
                                reads=[PB[0]], writes=[dstrg], tag=f"{pre}ge{nm}{j}")
                sch.fence()

                scT_w1 = sc_qW1T.rearrange("(kc p) o -> p kc o", p=128)
                scT_w2 = sc_qW2T.rearrange("(kc p) o -> p kc o", p=128)

                def ln_chain(half, tci, tbanks):
                    tch = half * 4 + tci
                    xt = xc2[tci % 2]; xtrg = pre + f"xc{tci % 2}"
                    xbt = xb2[tci % 2]; xbrg = pre + f"xb{tci % 2}"
                    scol = (tci % 2) * 8
                    lsrg = pre + f"ls{tci % 2}"
                    qtc = qt2[tci % 2]
                    sch.begin_collect()
                    sch.dma("sync", xt[:], x_src[tch * 128:(tch + 1) * 128, :], writes=[xtrg],
                            reads=[f"scX{tch}_{q}" for q in range(4)], tag=f"{pre}lx{tch}")
                    ln_chunk(xt[:, 0:D], xtrg, lstat, lsrg, scol, t0b[:], pre + "t0", f"{pre}l2{tch}",
                             save_to=(ln2stats, "ln2stats", tch * 2))
                    ln_apply(xt[:, 0:D], xtrg, lstat, lsrg, scol, xbt[:, 0:D], xbrg, f"{pre}l2{tch}")
                    ln_chunk(xbt[:, 0:D], xbrg, lstat, lsrg, scol, t0b[:], pre + "t0", f"{pre}l3{tch}")
                    sch.add("vector", (lambda xbt=xbt, xt=xt, scol=scol: (lambda e: e.tensor_scalar(
                        xt[:], xbt[:], lstat[:, scol:scol + 1], lstat[:, scol + 1:scol + 2],
                        ALU.subtract, ALU.mult)))(),
                        reads=[xbrg, lsrg], writes=[xtrg], tag=f"{pre}l3a{tch}")
                    sch.add("vector", (lambda xt=xt: (lambda e: e.tensor_tensor(
                        xt[:], xt[:], gamB[:], ALU.mult)))(),
                        reads=[xtrg, pre + "gamB"], writes=[xtrg], tag=f"{pre}l3b{tch}", fsz=D)
                    sch.add("vector", (lambda xt=xt, xbt=xbt: (lambda e: e.tensor_tensor(
                        xbt[:], xt[:], betB[:], ALU.add)))(),
                        reads=[xtrg, pre + "betB"], writes=[xbrg], tag=f"{pre}l3c{tch}", fsz=D)
                    quant_chunk(xbt[:, 0:D], xbrg, D, qtc, f"{pre}qz{tch}")
                    chain = sch.end_collect()

                    def do_T(tci=tci, tch=tch, qtc=qtc, tbanks=tbanks):
                        def ev2(bsel, gi, n, brg):
                            sch.add("vector", (lambda bsel=bsel, gi=gi, n=n: (lambda e: e.tensor_copy(
                                lnq_fm.rearrange("p (c t) -> p c t", t=512)[:, gi:gi + n, tci * 128:(tci + 1) * 128],
                                pb[bsel][:].bitcast(BF16)[:, 0:n * 128].rearrange("p (c q) -> p c q", q=128))))(),
                                reads=[brg], writes=[pre + f"lnq{tci}"], tag=f"{pre}lqe{tch}_{gi}", fsz=n * 128)
                        transpose_pack(qtc["q"][0], qtc["q"][1], [c * 128 for c in range(KC)], tbanks,
                                       ev2, f"{pre}lqT{tch}_")
                    chain.append(do_T)
                    return chain

                def ln_stage(half, tbanks=(4, 5)):
                    for pi in range(2):
                        chains = [ln_chain(half, pi * 2 + j, tbanks) for j in (0, 1)]
                        for st in zip_longest(*chains):
                            for t in st:
                                if t is not None:
                                    t()

                def w1_stage(half):
                    pend_T = []
                    for ot in range(16):
                        wbuf = strip0 if ot % 2 == 0 else strip1
                        wrg = "strip0" if ot % 2 == 0 else "strip1"
                        for hf in range(2):
                            sch.dma("sync" if ot % 2 == 0 else "gpsimd",
                                    wbuf[:, hf * 4096:(hf + 1) * 4096].rearrange("p (kc o) -> p kc o", o=512),
                                    scT_w1[:, hf * 8:(hf + 1) * 8, ot * 512:(ot + 1) * 512],
                                    reads=[f"scqW1T#{i}" for i in range(4)], writes=[wrg + "ab"[hf]],
                                    tag=f"{pre}w1l{half}_{ot}_{hf}")
                        hgb = hg[ot % 2]; hgrg = pre + f"hg{ot % 2}"
                        bt = b1t[ot % 2]; btrg = pre + f"b1t{ot % 2}"
                        sch.dma("gpsimd", bt[:], b1v.rearrange("(a b) -> a b", a=1)[0:1, ot * 512:(ot + 1) * 512],
                                writes=[btrg], tag=f"{pre}w1bl{half}_{ot}")
                        for tci in range(4):
                            bank = tci
                            for kc2 in range(KC):
                                sch.add("tensor", (lambda bank=bank, kc2=kc2, tci=tci, wbuf=wbuf: (lambda e: e.matmul(
                                    pb[bank][:], lnq_fm[:, kc2 * 512 + tci * 128: kc2 * 512 + (tci + 1) * 128],
                                    wbuf[:, kc2 * 512:(kc2 + 1) * 512],
                                    start=(kc2 == 0), stop=False)))(),
                                    reads=([pre + f"lnq{tci}", wrg + "ab"[kc2 // 8]] + ([PB[bank]] if kc2 else [])),
                                    writes=[PB[bank]], tag=f"{pre}w1m{half}_{ot}_{tci}_{kc2}")
                            sch.add("tensor", (lambda bank=bank, bt=bt: (lambda e: e.matmul(
                                pb[bank][:], ones_r[:], bt[:],
                                start=False, stop=True)))(),
                                reads=[btrg, "ones_r", PB[bank]], writes=[PB[bank]], tag=f"{pre}w1b{half}_{ot}_{tci}")
                            fb = f2r[tci % 2]; fbrg = pre + f"f2{tci % 2}"
                            sch.add("vector", (lambda bank=bank, fb=fb: (lambda e: e.tensor_copy(
                                fb[:], pb[bank][:])))(),
                                reads=[PB[bank]], writes=[fbrg], tag=f"{pre}w1e{half}_{ot}_{tci}")
                            sch.add("scalar", (lambda hgb=hgb, tci=tci, fb=fb: (lambda e: e.activation(
                                hgb[:, tci * 512:(tci + 1) * 512], fb[:], AF.Gelu_apprx_tanh,
                                bias=0.0, scale=1.0)))(),
                                reads=[fbrg, "consts"], writes=[hgrg], tag=f"{pre}w1g{half}_{ot}_{tci}")
                        qtc = qt2[ot % 2]
                        quant_chunk(hgb[:, 0:2048], hgrg, 2048, qtc, f"{pre}hz{half}_{ot}")

                        def hq_transposes(ot=ot, qtc=qtc):
                            for grp2 in range(2):
                                bsel = 4 + grp2
                                for oc_ in range(2):
                                    for tci_ in range(4):
                                        col = tci_ * 512 + (grp2 * 2 + oc_) * 128
                                        slot = oc_ * 4 + tci_
                                        sch.add("tensor", (lambda bsel=bsel, slot=slot, col=col, qtc=qtc: (lambda e: e.transpose(
                                            pb[bsel][:].bitcast(BF16)[:, slot * 128:(slot + 1) * 128],
                                            qtc["q"][0][:, col:col + 128], ident[:])))(),
                                            reads=[qtc["q"][1], "ident"], writes=[PB[bsel]], tag=f"{pre}hT{half}_{ot}_{grp2}_{oc_}_{tci_}")
                                    dest = hq_fm[:, (ot * 4 + grp2 * 2) * 512: (ot * 4 + grp2 * 2 + 2) * 512]
                                    sch.add("vector", (lambda bsel=bsel, dest=dest: (lambda e: e.tensor_copy(
                                        dest.rearrange("p (a b q) -> p a b q", b=4, q=128),
                                        pb[bsel][:].bitcast(BF16)[:, 0:1024].rearrange("p (a b q) -> p a b q", b=4, q=128))))(),
                                        reads=[PB[bsel]], writes=[pre + "hq"], tag=f"{pre}hqe{half}_{ot}_{grp2}")
                        pend_T.append(hq_transposes)
                        if len(pend_T) > 1:
                            pend_T.pop(0)()
                    while pend_T:
                        pend_T.pop(0)()

                def w2_stage(half, hook=None):
                    tbase = half * 4
                    for tci in range(4):
                        tch = tbase + tci
                        xt = xc2[tci % 2]; xtrg = pre + f"xc{tci % 2}"
                        sch.dma("sync", xt[:], x_src[tch * 128:(tch + 1) * 128, :], writes=[xtrg],
                            reads=[f"scX{tch}_{q}" for q in range(4)], tag=f"{pre}rx{tch}")
                        sch.add("vector", (lambda xt=xt, tci=tci, tch=tch: (lambda e: e.tensor_scalar(
                            res[tci][:], xt[:], ln2stats[:, tch * 2:tch * 2 + 1], ln2stats[:, tch * 2 + 1:tch * 2 + 2],
                            ALU.subtract, ALU.mult)))(),
                            reads=[xtrg, "ln2stats"], writes=[pre + f"res{tci}"], tag=f"{pre}rr{tch}", fsz=D)

                    for o2 in range(4):
                        bt2 = b2t[o2 % 2]; bt2rg = pre + f"b2t{o2 % 2}"
                        sch.dma("gpsimd", bt2[:], b2v.rearrange("(a b) -> a b", a=1)[0:1, o2 * 512:(o2 + 1) * 512],
                                writes=[bt2rg], tag=f"{pre}w2bl{half}_{o2}")
                        for kg in range(4):
                            wbuf = strip0 if kg % 2 == 0 else strip1
                            wrg = "strip0" if kg % 2 == 0 else "strip1"
                            for hf in range(2):
                                sch.dma("sync" if kg % 2 == 0 else "gpsimd",
                                        wbuf[:, hf * 4096:(hf + 1) * 4096].rearrange("p (kc o) -> p kc o", o=512),
                                        scT_w2[:, kg * 16 + hf * 8: kg * 16 + (hf + 1) * 8, o2 * 512:(o2 + 1) * 512],
                                        reads=[f"scqW2T#{i}" for i in range(4)], writes=[wrg + "ab"[hf]],
                                        tag=f"{pre}w2l{half}_{o2}_{kg}_{hf}")
                            for tci in range(4):
                                bank = 4 + tci
                                for kk in range(16):
                                    kc2 = kg * 16 + kk
                                    sch.add("tensor", (lambda bank=bank, kc2=kc2, kk=kk, tci=tci, wbuf=wbuf: (lambda e: e.matmul(
                                        pb[bank][:], hq_fm[:, kc2 * 512 + tci * 128: kc2 * 512 + (tci + 1) * 128],
                                        wbuf[:, kk * 512:(kk + 1) * 512],
                                        start=(kc2 == 0), stop=False)))(),
                                        reads=([pre + "hq", wrg + "ab"[kk // 8]] + ([PB[bank]] if kc2 else [])),
                                        writes=[PB[bank]], tag=f"{pre}w2m{half}_{o2}_{tci}_{kc2}")
                        for tci in range(4):
                            bank = 4 + tci
                            sch.add("tensor", (lambda bank=bank, bt2=bt2: (lambda e: e.matmul(
                                pb[bank][:], ones_r[:], bt2[:],
                                start=False, stop=True)))(),
                                reads=[bt2rg, "ones_r", PB[bank]], writes=[PB[bank]], tag=f"{pre}w2b{half}_{o2}_{tci}")
                        for tci in range(4):
                            bank = 4 + tci
                            fb = f2r[tci]; fbrg = pre + f"f2{tci}"
                            sch.add("vector", (lambda bank=bank, tci=tci, o2=o2, fb=fb: (lambda e: e.scalar_tensor_tensor(
                                fb[:], pb[bank][:], 1.0, res[tci][:, o2 * 512:(o2 + 1) * 512], ALU.mult, ALU.add)))(),
                                reads=[PB[bank], pre + f"res{tci}"], writes=[fbrg], tag=f"{pre}w2f{half}_{o2}_{tci}")
                        for tci in range(4):
                            tch = tbase + tci
                            fb = f2r[tci]; fbrg = pre + f"f2{tci}"
                            xsl = xs2[tci % 2]; xslrg = pre + f"xs{tci % 2}"
                            sch.dma("sync", xsl[:], x_src[tch * 128:(tch + 1) * 128, o2 * 512:(o2 + 1) * 512],
                                    reads=[f"scX{tch}_{o2}"], writes=[xslrg], tag=f"{pre}w2x{half}_{o2}_{tci}")
                            sch.add("vector", (lambda xsl=xsl, fb=fb: (lambda e: e.tensor_tensor(
                                xsl[:], fb[:], xsl[:], ALU.add)))(),
                                reads=[fbrg, xslrg], writes=[xslrg], tag=f"{pre}w2a{half}_{o2}_{tci}")
                            dst = out if final else sc_X
                            sch.dma("gpsimd", dst[tch * 128:(tch + 1) * 128, o2 * 512:(o2 + 1) * 512], xsl[:],
                                    reads=[xslrg], writes=([f"scX{tch}_{o2}"] if not final else []),
                                    tag=f"{pre}w2s{half}_{o2}_{tci}")
                        if hook is not None:
                            hook(o2)

                def ln1_hook(o2):
                    if o2 in (0, 1):
                        chains = [ln_chain(1, o2 * 2 + j, (0, 1)) for j in (0, 1)]
                        for st in zip_longest(*chains):
                            for t in st:
                                if t is not None:
                                    t()

                ln_stage(0)
                w1_stage(0)
                w2_stage(0, hook=ln1_hook)
                w1_stage(1)
                w2_stage(1)
            sch.fence()

        if "ffn1" in phases:
            ffn_phase(1, sc_X, final=False)

        if "attnb" in phases:
            bp = ExitStack()
            with bp:
                xln3_fm = sb("b_xln", [128, KC * R], BF16, bp)
                v3_fm = sb("b_v3", [128, KC * R], BF16, bp)
                xc3 = [sb(f"b_xc{i}", [128, D], F32, bp) for i in range(2)]
                xb3 = [sb(f"b_xb{i}", [128, D], BF16, bp) for i in range(2)]
                lstat3 = sb("b_ls", [128, 16], F32, bp)
                sqs3 = sb("b_sq", [128, D], F32, bp)
                xsl3 = [sb(f"b_xs{i}", [128, 512], F32, bp) for i in range(2)]

                def b_chain(tch):
                    xt = xc3[tch % 2]; xtrg = f"b_xc{tch % 2}"
                    xbt = xb3[tch % 2]; xbrg = f"b_xb{tch % 2}"
                    scol = (tch % 2) * 8
                    sch.begin_collect()
                    sch.dma("sync", xt[:], sc_X[tch * 128:(tch + 1) * 128, :], writes=[xtrg],
                            reads=[f"scX{tch}_{q}" for q in range(4)], tag=f"b_lx{tch}")
                    ln_chunk(xt[:, 0:D], xtrg, lstat3, f"b_ls{tch % 2}", scol, sqs3[:], "b_sq", f"b_l{tch}")
                    ln_apply(xt[:, 0:D], xtrg, lstat3, f"b_ls{tch % 2}", scol, xbt[:, 0:D], xbrg, f"b_l{tch}")
                    chain = sch.end_collect()

                    def do_T(tch=tch, xbt=xbt, xbrg=xbrg):
                        def evb(bsel, gi, n, brg):
                            sch.add("vector", (lambda bsel=bsel, gi=gi, n=n: (lambda e: e.tensor_copy(
                                xln3_fm.rearrange("p (c t) -> p c t", t=R)[:, gi:gi + n, tch * 128:(tch + 1) * 128],
                                pb[bsel][:].bitcast(BF16)[:, 0:n * 128].rearrange("p (c q) -> p c q", q=128))))(),
                                reads=[brg], writes=["b_xln"], tag=f"b_xe{tch}_{gi}")
                        transpose_pack(xbt, xbrg, [c * 128 for c in range(KC)], [4, 5], evb, f"b_xT{tch}_")
                    chain.append(do_T)
                    return chain

                for pi in range(0, NTO, 2):
                    chains = [b_chain(pi), b_chain(pi + 1)]
                    for st in zip_longest(*chains):
                        for t in st:
                            if t is not None:
                                t()

                wcbb = [sb(f"b_wc{i}", [128, D], BF16, bp) for i in range(2)]
                for g in range(4):
                    wbuf = strip0 if g % 2 == 0 else strip1
                    wrg = "strip0" if g % 2 == 0 else "strip1"
                    mk_wstrip(wbuf, wrg, bWqkv[2 * D:3 * D, :], g, (6, 7), wcbb, "b_wc", f"b_wv{g}")
                    for oo in range(4):
                        for tt in range(2):
                            bank = oo
                            for kc2 in range(KC):
                                sch.add("tensor", (lambda bank=bank, kc2=kc2, oo=oo, tt=tt, wbuf=wbuf: (lambda e: e.matmul(
                                    pb[bank][:], wbuf[:, kc2 * 512 + oo * 128: kc2 * 512 + (oo + 1) * 128],
                                    xln3_fm[:, kc2 * R + tt * 512: kc2 * R + tt * 512 + 512],
                                    start=(kc2 == 0), stop=(kc2 == KC - 1))))(),
                                    reads=([wrg, "b_xln"] + ([PB[bank]] if kc2 else [])),
                                    writes=[PB[bank]], tag=f"b_v{g}_{oo}_{tt}_{kc2}")
                            oc = g * 4 + oo
                            sch.add("vector", (lambda bank=bank, oc=oc, tt=tt: (lambda e: e.tensor_copy(
                                v3_fm[:, oc * R + tt * 512: oc * R + tt * 512 + 512], pb[bank][:])))(),
                                reads=[PB[bank]], writes=["b_v3"], tag=f"b_ve{g}_{oo}_{tt}")

                for ot in range(4):
                    wbuf = strip0 if ot % 2 == 0 else strip1
                    wrg = "strip0" if ot % 2 == 0 else "strip1"
                    mk_wstrip(wbuf, wrg, bWo, ot, (0, 1), wcbb, "b_wc", f"b_wo{ot}")
                    for tch in range(NTO):
                        bank = 4 + tch % 4
                        for ic in range(KC):
                            sch.add("tensor", (lambda bank=bank, ic=ic, tch=tch, wbuf=wbuf: (lambda e: e.matmul(
                                pb[bank][:], v3_fm[:, ic * R + tch * 128: ic * R + (tch + 1) * 128],
                                wbuf[:, ic * 512:(ic + 1) * 512],
                                start=(ic == 0), stop=(ic == KC - 1))))(),
                                reads=(["b_v3", wrg] + ([PB[bank]] if ic else [])),
                                writes=[PB[bank]], tag=f"b_o{ot}_{tch}_{ic}")
                        xsl = xsl3[tch % 2]; xslrg = f"b_xs{tch % 2}"
                        sch.dma("sync", xsl[:], sc_X[tch * 128:(tch + 1) * 128, ot * 512:(ot + 1) * 512],
                                reads=[f"scX{tch}_{ot}"], writes=[xslrg], tag=f"b_lxs{ot}_{tch}")
                        sch.add("vector", (lambda bank=bank, xsl=xsl: (lambda e: e.tensor_tensor(
                            xsl[:], pb[bank][:], xsl[:], ALU.add)))(),
                            reads=[PB[bank], xslrg], writes=[xslrg], tag=f"b_or{ot}_{tch}")
                        sch.dma("gpsimd", sc_X[tch * 128:(tch + 1) * 128, ot * 512:(ot + 1) * 512], xsl[:],
                                reads=[xslrg], writes=[f"scX{tch}_{ot}"], tag=f"b_sxs{ot}_{tch}")
            sch.fence()

        if "ffn2" in phases:
            ffn_phase(2, sc_X, final=True)
        else:
            tmpo = sb("tmpo", [128, D], F32)
            for tch in range(NTO):
                sch.dma("sync", tmpo[:], sc_X[tch * 128:(tch + 1) * 128, :], writes=["tmpo"], tag=f"d_l{tch}")
                sch.dma("sync", out[tch * 128:(tch + 1) * 128, :], tmpo[:], reads=["tmpo"], tag=f"d_s{tch}")

        sch.resolve()
        global LAST_SCHED
        LAST_SCHED = sch
        semkeys = set()
        for o in sch.ops:
            if o.done:
                semkeys.add(o.done[0])
            for k, _ in o.waits:
                semkeys.add(k)
        sems = {k: ctx.enter_context(nc.semaphore("_".join(str(p) for p in k))) for k in sorted(semkeys, key=str)}
        blk = ctx.enter_context(nc.Block())
        sch.emit(nc, blk, sems)

    return nc


def make_in_maps(inputs):
    x = np.ascontiguousarray(np.asarray(inputs["L0_attn_a_x"], np.float32).reshape(SEQ, D))
    ident, pswap = host_ident()
    consts = host_consts()
    base = dict(
        aWq=np.ascontiguousarray(inputs["L0_attn_a_Wq"], dtype=np.float32),
        aWk=np.ascontiguousarray(inputs["L0_attn_a_Wk"], dtype=np.float32),
        aWv=np.ascontiguousarray(inputs["L0_attn_a_Wv"], dtype=np.float32),
        aWo=np.ascontiguousarray(inputs["L0_attn_a_Wo"], dtype=np.float32),
        bWqkv=np.ascontiguousarray(inputs["L0_attn_b_weight_qkv"], dtype=np.float32),
        bWo=np.ascontiguousarray(inputs["L0_attn_b_weight_out"], dtype=np.float32),
        W1=np.ascontiguousarray(inputs["L0_ffn_W1"], dtype=np.float32),
        W2=np.ascontiguousarray(inputs["L0_ffn_W2"], dtype=np.float32),
        b1v=np.ascontiguousarray(inputs["L0_ffn_b1"], dtype=np.float32),
        b2v=np.ascontiguousarray(inputs["L0_ffn_b2"], dtype=np.float32),
        lnw=np.ascontiguousarray(inputs["L0_ffn_ln_w"], dtype=np.float32),
        lnb=np.ascontiguousarray(inputs["L0_ffn_ln_b"], dtype=np.float32),
        identI=ident, pswapI=pswap, constsI=consts,
    )
    maps = []
    for c in range(NCORES):
        p0 = c * R
        if c == 0:
            xs = np.concatenate([np.zeros((HL, D), np.float32), x[:R]], axis=0)
        else:
            xs = x[p0 - HL: p0 + R]
        mm_, b0 = host_masks(c)
        m = dict(base)
        m.update(x_in=np.ascontiguousarray(xs), tabs=host_tables(c), maskM=mm_, maskB0=b0)
        maps.append(m)
    return maps


def kernel(**inputs):
    maps = make_in_maps(inputs)
    nc = build_program()
    res = run_bass_kernel_spmd(nc, maps, core_ids=list(range(NCORES)))
    outs = [np.asarray(r["out"], np.float32) for r in res.results]
    return np.concatenate(outs, axis=0).reshape(1, SEQ, D)


# version 30
# speedup vs baseline: 1.0386x; 1.0091x over previous
import numpy as np
import ml_dtypes
from contextlib import ExitStack
from itertools import zip_longest

import concourse.bass as bass
import concourse.mybir as mybir
from concourse.bass_utils import run_bass_kernel_spmd

F32 = mybir.dt.float32
BF16 = mybir.dt.bfloat16
I32 = mybir.dt.int32
I8 = mybir.dt.int8
AF = mybir.ActivationFunctionType
ALU = mybir.AluOpType
AX = mybir.AxisListType
BF = ml_dtypes.bfloat16

SEQ, D, NH, HD, WIN = 8192, 2048, 16, 128, 512
NCORES = 8
R = SEQ // NCORES
HL = WIN
T1 = R + HL
DH = 8192
KC = D // 128
KCH = DH // 128
NTC = T1 // 128
NTO = R // 128
EPS = 1e-5
CEXP = -12.0
MASKV = -30000.0
CONSTS = [0.0, 1.0, CEXP, -0.5, -1.0, EPS]

ENGINES = ("tensor", "vector", "scalar", "gpsimd", "sync")
NDMASEM = 8
LAST_SCHED = None


class _Op:
    def __init__(self, eng, fn, reads, writes, dma, tag, idx, fence, fsz=0):
        self.eng, self.fn = eng, fn
        self.reads, self.writes = tuple(reads), tuple(writes)
        self.dma, self.tag, self.idx = dma, tag, idx
        self.fsz = fsz
        self.fence = fence
        self.needs_inc = False
        self.done = None
        self.deps = None
        self.waits = None


class Sched:
    def __init__(self):
        self.ops = []
        self._fence_deps = ()
        self._col = None

    def add(self, eng, fn, reads=(), writes=(), dma=False, tag="", fsz=0):
        assert eng in ENGINES, eng
        if self._col is not None:
            self._col.append(lambda: self.add(eng, fn, reads, writes, dma, tag, fsz))
            return None
        o = _Op(eng, fn, reads, writes, dma, tag, len(self.ops), self._fence_deps, fsz)
        self.ops.append(o)
        return o

    def begin_collect(self):
        assert self._col is None
        self._col = []

    def end_collect(self):
        c, self._col = self._col, None
        return c

    @staticmethod
    def _ordered(d, o):
        return (d.eng == o.eng and not d.dma and not o.dma and
                (d.eng == "tensor" or (d.fsz >= 2048 and d.eng in ("vector", "scalar"))))

    def dma(self, eng, out, in_, reads=(), writes=(), tag="", **kw):
        fn = (lambda out=out, in_=in_, kw=kw: (lambda e: e.dma_start(out=out, in_=in_, **kw)))()
        return self.add(eng, fn, reads=reads, writes=writes, dma=True, tag=tag)

    def fence(self):
        last = {}
        for o in self.ops:
            last[(o.eng, o.dma)] = o
        self._fence_deps = tuple(last.values())

    def resolve(self):
        last_w = {}
        readers = {}
        for o in self.ops:
            deps = set(o.fence)
            for r in o.reads:
                w = last_w.get(r)
                if w is not None:
                    deps.add(w)
            for w_ in o.writes:
                lw = last_w.get(w_)
                if lw is not None:
                    deps.add(lw)
                deps.update(readers.get(w_, ()))
            deps.discard(o)
            o.deps = deps
            for r in o.reads:
                readers.setdefault(r, []).append(o)
            for w_ in o.writes:
                last_w[w_] = o
                readers[w_] = []
        for o in self.ops:
            for d in o.deps:
                if not self._ordered(d, o):
                    d.needs_inc = True
        cnt = {}
        dma_idx = {}
        for o in self.ops:
            if o.dma:
                i = dma_idx.get(o.eng, 0)
                dma_idx[o.eng] = i + 1
                key = ("dma", o.eng, i % NDMASEM)
                cnt[key] = 16 * (i // NDMASEM + 1)
                o.done = (key, cnt[key])
                o.dma_semkey = key
            elif o.needs_inc:
                key = ("eng", o.eng)
                cnt[key] = cnt.get(key, 0) + 1
                o.done = (key, cnt[key])
        self.final_counts = dict(cnt)
        marks = {e: {} for e in ENGINES}
        for o in self.ops:
            wl = {}
            for d in o.deps:
                if self._ordered(d, o):
                    continue
                assert d.done is not None, (d.tag, o.tag)
                k, v = d.done
                if wl.get(k, 0) < v:
                    wl[k] = v
            ww = marks[o.eng]
            o.waits = []
            for k, v in sorted(wl.items(), key=lambda kv: str(kv[0])):
                if ww.get(k, 0) < v:
                    o.waits.append((k, v))
                    ww[k] = v

    def emit(self, nc, blk, sems):
        by_eng = {}
        for o in self.ops:
            by_eng.setdefault(o.eng, []).append(o)
        for eng in ENGINES:
            if eng not in by_eng:
                continue
            ops = by_eng[eng]

            def body(e, ops=ops, eng=eng):
                for o in ops:
                    for k, v in o.waits:
                        e.wait_ge(sems[k], v)
                    inst = o.fn(e)
                    if o.dma:
                        inst.then_inc(sems[o.dma_semkey], 16)
                    elif o.needs_inc:
                        inst.then_inc(sems[("eng", eng)], 1)
                if eng == "sync":
                    for k, c in sorted(self.final_counts.items(), key=lambda kv: str(kv[0])):
                        if k[0] == "dma":
                            e.wait_ge(sems[k], c)
            getattr(blk, eng)(body)


def host_tables(core_id):
    p0 = core_id * R
    pos = np.arange(p0 - HL, p0 + R, dtype=np.float64)
    pos = np.where(pos < 0, 0, pos).astype(np.float32)
    inv = (1.0 / (np.float32(10000.0) ** (np.arange(0, HD, 2, dtype=np.float32) / np.float32(HD)))).astype(np.float32)
    fr = (pos[None, :] * inv[:, None]).astype(np.float32)
    cos = np.cos(fr).astype(np.float32)
    sin = np.sin(fr).astype(np.float32)
    cosF = np.empty((HD, T1), np.float32)
    sinF = np.empty((HD, T1), np.float32)
    cosF[0::2] = cos; cosF[1::2] = cos
    sinF[0::2] = -sin; sinF[1::2] = sin
    s = np.float32(HD ** (-0.5))
    return np.concatenate([cosF, sinF, cosF * s, sinF * s], axis=1).astype(BF)


def host_masks(core_id):
    j = np.arange(128)[:, None]
    i = np.arange(WIN)[None, :]
    tiles = []
    for jc in range(8):
        jl = jc * 128 + j
        valid = (jl > i) & (jl <= i + WIN)
        tiles.append(np.where(valid, 0.0, MASKV).astype(np.float32))
    main = np.concatenate(tiles, axis=1)
    if core_id == 0:
        b0 = np.full((128, 4 * WIN), MASKV, np.float32)
    else:
        b0 = np.concatenate(tiles[:4], axis=1)
    return main.astype(BF), b0.astype(BF)


def host_ident():
    ident = np.eye(128, dtype=np.float32).astype(BF)
    sw = np.arange(128)
    sw = (sw // 2) * 2 + (1 - sw % 2)
    pswap = np.eye(128, dtype=np.float32)[sw].astype(BF)
    return ident, pswap


def host_consts():
    c = np.zeros((128, 8), np.float32)
    c[:, :len(CONSTS)] = np.array(CONSTS, np.float32)[None, :]
    return c


def build_program(phases=("wprep", "attn", "ffn1", "attnb", "ffn2"), debug=()):
    nc = bass.Bass("TRN2", target_bir_lowering=False)
    sch = Sched()

    def din(name, shape, dtype=F32):
        return nc.dram_tensor(name, list(shape), dtype, kind="ExternalInput").ap()

    x_in = din("x_in", [T1, D])
    aWq = din("aWq", [D, D]); aWk = din("aWk", [D, D])
    aWv = din("aWv", [D, D]); aWo = din("aWo", [D, D])
    bWqkv = din("bWqkv", [3 * D, D]); bWo = din("bWo", [D, D])
    W1 = din("W1", [DH, D]); W2 = din("W2", [D, DH])
    b1v = din("b1v", [DH]); b2v = din("b2v", [D])
    lnw = din("lnw", [D]); lnb = din("lnb", [D])
    tabs = din("tabs", [128, 4 * T1], BF16)
    maskM = din("maskM", [128, 8 * WIN], BF16)
    maskB0 = din("maskB0", [128, 4 * WIN], BF16)
    identI = din("identI", [128, 128], BF16)
    pswapI = din("pswapI", [128, 128], BF16)
    constsI = din("constsI", [128, 8])

    out = nc.dram_tensor("out", [R, D], F32, kind="ExternalOutput").ap()

    dbg_tensors = {}

    def dbg(name, ap, rg, shape, dtype):
        if name in debug:
            t = nc.dram_tensor(name, list(shape), dtype, kind="ExternalOutput").ap()
            dbg_tensors[name] = t
            sch.dma("sync", t, ap, reads=[rg], tag="dump_" + name)

    def scr(name, shape, dtype=BF16):
        kind = "ExternalOutput" if name in debug else "Internal"
        return nc.dram_tensor(name, list(shape), dtype, kind=kind).ap()

    sc_qW1T = scr("sc_qW1T", [D, DH]); sc_qW2T = scr("sc_qW2T", [DH, D])
    sc_X = scr("sc_X", [R, D], F32)

    ctx = ExitStack()
    with ctx:
        def sb(name, shape, dtype=BF16, stack=None):
            return (stack or ctx).enter_context(nc.sbuf_tensor(name, shape, dtype))

        pb = [ctx.enter_context(nc.psum_tensor(f"pb{i}", [128, 512], F32)) for i in range(8)]
        PB = [f"pb{i}" for i in range(8)]
        strip0 = sb("strip0", [128, 16 * 512], BF16)
        strip1 = sb("strip1", [128, 16 * 512], BF16)
        sconsts = sb("sconsts", [128, 8], F32)
        ident = sb("ident", [128, 128], BF16)
        pswap = sb("pswap", [128, 128], BF16)
        ones_r = sb("ones_r", [1, 128], BF16)
        ones_rf = sb("ones_rf", [1, 128], F32)
        ones_c = sb("ones_c", [128, 1], BF16)
        ln2stats = sb("ln2stats", [128, 2 * NTO], F32)

        for ci, cv in enumerate(CONSTS):
            nc.const_aps.register(F32, cv, sconsts[:, ci:ci + 1])

        sch.dma("sync", sconsts[:], constsI, writes=["consts"], tag="ld_consts")
        sch.dma("sync", ident[:], identI, writes=["ident"], tag="ld_ident")
        sch.dma("sync", pswap[:], pswapI, writes=["pswap"], tag="ld_pswap")
        sch.add("vector", lambda e: e.memset(ones_r[:], 1.0), writes=["ones_r"])
        sch.add("vector", lambda e: e.memset(ones_rf[:], 1.0), writes=["ones_rf"])
        sch.add("vector", lambda e: e.memset(ones_c[:], 1.0), writes=["ones_c"])

        def quant_chunk(src_ap, src_rg, F, qt, tag):
            G = F // 32
            (amax, amaxr), (ei, eir), (rcp, rcpr), (scl, sclr), (i8t, i8r), (qo, qor) = (
                qt["amax"], qt["ei"], qt["recip"], qt["scale"], qt["i8"], qt["q"])
            sch.add("vector", lambda e: e.tensor_reduce(
                amax[:, 0:G].unsqueeze(2), src_ap.rearrange("p (g b) -> p g b", b=32), AX.X, ALU.max,
                apply_absolute_value=True), reads=[src_rg], writes=[amaxr], tag=tag + "am")
            sch.add("vector", lambda e: e.tensor_single_scalar(
                ei[:, 0:G], amax[:, 0:G].bitcast(I32), 23, ALU.logical_shift_right),
                reads=[amaxr], writes=[eir], tag=tag + "e")
            sch.add("vector", lambda e: e.tensor_scalar(
                rcp[:, 0:G].bitcast(I32), ei[:, 0:G], -1, 260, ALU.mult, ALU.add),
                reads=[eir], writes=[rcpr], tag=tag + "r1")
            sch.add("vector", lambda e: e.tensor_single_scalar(
                rcp[:, 0:G].bitcast(I32), rcp[:, 0:G].bitcast(I32), 23, ALU.logical_shift_left),
                reads=[rcpr], writes=[rcpr], tag=tag + "r2")
            sch.add("vector", lambda e: e.tensor_scalar(
                scl[:, 0:G].bitcast(I32), ei[:, 0:G], 1, -6, ALU.mult, ALU.add),
                reads=[eir], writes=[sclr], tag=tag + "s1")
            sch.add("vector", lambda e: e.tensor_single_scalar(
                scl[:, 0:G].bitcast(I32), scl[:, 0:G].bitcast(I32), 23, ALU.logical_shift_left),
                reads=[sclr], writes=[sclr], tag=tag + "s2")
            rB = rcp[:, 0:G].unsqueeze(2).broadcast_to([128, G, 32])
            sB = scl[:, 0:G].unsqueeze(2).broadcast_to([128, G, 32])
            sch.add("vector", lambda e: e.tensor_tensor(
                i8t[:, 0:F].rearrange("p (g b) -> p g b", b=32),
                src_ap.rearrange("p (g b) -> p g b", b=32), rB, ALU.mult),
                reads=[src_rg, rcpr], writes=[i8r], tag=tag + "i8", fsz=F)
            sch.add("vector", lambda e: e.tensor_tensor(
                qo[:, 0:F].rearrange("p (g b) -> p g b", b=32),
                i8t[:, 0:F].rearrange("p (g b) -> p g b", b=32), sB, ALU.mult),
                reads=[i8r, sclr], writes=[qor], tag=tag + "q", fsz=F)

        def make_qt(pre, Fmax, stack, share_i8=None):
            G = Fmax // 32
            return dict(
                amax=(sb(pre + "amax", [128, G], F32, stack), pre + "amax"),
                ei=(sb(pre + "ei", [128, G], I32, stack), pre + "ei"),
                recip=(sb(pre + "recip", [128, G], F32, stack), pre + "recip"),
                scale=(sb(pre + "scale", [128, G], F32, stack), pre + "scale"),
                i8=(share_i8 if share_i8 is not None else (sb(pre + "i8", [128, Fmax], I8, stack), pre + "i8")),
                q=(sb(pre + "q", [128, Fmax], BF16, stack), pre + "q"),
            )

        def transpose_pack(src_ap, src_rg, chunk_cols, banks, evac, tag, perm=None):
            pmat, pmn = (perm, "pswap") if perm is not None else (ident, "ident")
            nb = len(banks)
            for gi in range(0, len(chunk_cols), 8):
                n = min(8, len(chunk_cols) - gi)
                bsel = banks[(gi // 8) % nb]
                brg = PB[bsel]
                for j in range(n):
                    col = chunk_cols[gi + j]
                    sch.add("tensor", (lambda bsel=bsel, j=j, col=col, pmat=pmat: (lambda e: e.transpose(
                        pb[bsel][:].bitcast(BF16)[:, j * 128:(j + 1) * 128],
                        src_ap[:, col:col + 128], pmat[:])))(),
                        reads=[src_rg, pmn], writes=[brg], tag=tag + f"T{gi + j}")
                evac(bsel, gi, n, brg)

        def ln_chunk(xap, xrg, stats, stats_rg, scol, sqscr, sqrg, tag, save_to=None):
            rN = 1.0 / D
            sch.add("vector", lambda e: e.tensor_reduce(
                stats[:, scol + 2:scol + 3], xap, AX.X, ALU.add),
                reads=[xrg], writes=[stats_rg], tag=tag + "sm")
            sch.add("scalar", lambda e: e.activation(sqscr, xap, AF.Square, bias=0.0, scale=1.0,
                                                     accum_out=stats[:, scol + 3:scol + 4]),
                    reads=[xrg, "consts"], writes=[stats_rg, sqrg], tag=tag + "sq")
            sch.add("vector", lambda e: e.tensor_single_scalar(
                stats[:, scol:scol + 1], stats[:, scol + 2:scol + 3], rN, ALU.mult),
                reads=[stats_rg], writes=[stats_rg], tag=tag + "mn")
            sch.add("vector", lambda e: e.tensor_tensor(
                stats[:, scol + 4:scol + 5], stats[:, scol:scol + 1], stats[:, scol:scol + 1], ALU.mult),
                reads=[stats_rg], writes=[stats_rg], tag=tag + "m2")
            sch.add("vector", lambda e: e.tensor_single_scalar(
                stats[:, scol + 5:scol + 6], stats[:, scol + 3:scol + 4], rN, ALU.mult),
                reads=[stats_rg], writes=[stats_rg], tag=tag + "v1")
            sch.add("vector", lambda e: e.tensor_tensor(
                stats[:, scol + 1:scol + 2], stats[:, scol + 5:scol + 6], stats[:, scol + 4:scol + 5],
                ALU.subtract),
                reads=[stats_rg], writes=[stats_rg], tag=tag + "vr")
            if save_to is not None:
                sv, svrg, svcol = save_to
                sch.add("vector", lambda e: e.tensor_copy(sv[:, svcol:svcol + 1], stats[:, scol:scol + 1]),
                        reads=[stats_rg], writes=[svrg], tag=tag + "sv")
            sch.add("scalar", lambda e: e.activation(stats[:, scol + 1:scol + 2], stats[:, scol + 1:scol + 2],
                                                     AF.Ln, bias=EPS, scale=1.0),
                    reads=[stats_rg, "consts"], writes=[stats_rg], tag=tag + "ln")
            sch.add("scalar", lambda e: e.activation(stats[:, scol + 1:scol + 2], stats[:, scol + 1:scol + 2],
                                                     AF.Exp, bias=0.0, scale=-0.5),
                    reads=[stats_rg, "consts"], writes=[stats_rg], tag=tag + "ex")
            if save_to is not None:
                sv, svrg, svcol = save_to
                sch.add("vector", lambda e: e.tensor_copy(sv[:, svcol + 1:svcol + 2], stats[:, scol + 1:scol + 2]),
                        reads=[stats_rg], writes=[svrg], tag=tag + "sv2")

        def ln_apply(xap, xrg, stats, stats_rg, scol, outap, outrg, tag):
            sch.add("vector", lambda e: e.tensor_scalar(
                outap, xap, stats[:, scol:scol + 1], stats[:, scol + 1:scol + 2],
                ALU.subtract, ALU.mult), reads=[xrg, stats_rg], writes=[outrg], tag=tag + "ap", fsz=D)

        def mk_wstrip(wbuf, wrg, w_dram, grow, tbanks, wcb, wcbpre, tag):
            for rc in range(4):
                wt = wcb[rc % 2]; wtrg = wcbpre + str(rc % 2)
                sch.dma("gpsimd", wt[:],
                        w_dram[grow * 512 + rc * 128: grow * 512 + (rc + 1) * 128, :],
                        writes=[wtrg], tag=f"{tag}ld{rc}")
                for gi in range(2):
                    bsel = tbanks[gi]
                    for j in range(8):
                        ic = gi * 8 + j
                        sch.add("tensor", (lambda bsel=bsel, j=j, ic=ic, wt=wt: (lambda e: e.transpose(
                            pb[bsel][:].bitcast(BF16)[:, j * 128:(j + 1) * 128],
                            wt[:, ic * 128:(ic + 1) * 128], ident[:])))(),
                            reads=[wtrg, "ident"], writes=[PB[bsel]], tag=f"{tag}T{rc}_{ic}")
                    sch.add("vector", (lambda bsel=bsel, gi=gi, rc=rc, wbuf=wbuf: (lambda e: e.tensor_copy(
                        wbuf[:].rearrange("p (kc o) -> p kc o", o=512)[:, gi * 8:gi * 8 + 8, rc * 128:(rc + 1) * 128],
                        pb[bsel][:].bitcast(BF16)[:, 0:1024].rearrange("p (c q) -> p c q", q=128))))(),
                        reads=[PB[bsel]], writes=[wrg], tag=f"{tag}e{rc}_{gi}", fsz=1024)

        if "wprep" in phases:
            wp = ExitStack()
            with wp:
                NWR = 4
                wbig = [sb(f"wbig{i}", [128, 2048], F32, wp) for i in range(NWR)]
                qtw = [make_qt(f"wq{i}_", 2048, wp) for i in range(NWR)]
                wtev = [sb(f"wtev{i}", [128, 1024], BF16, wp) for i in range(2 * NWR)]
                _wn = [0]

                def prep_weight(src_ap, dest_scT, o_chunks, k_cols, do_quant, tag, screg, dest_scT2=None,
                                screg2=None):
                    npc = k_cols // 2048
                    dview = dest_scT.rearrange("(kc p) o -> p kc o", p=128)
                    dview2 = dest_scT2.rearrange("(kc p) o -> p kc o", p=128) if dest_scT2 is not None else None
                    def mk_evac(oc, p, dv, sfx, sreg, par, tag):
                        def evac(bsel, gi, n, brg):
                            ev = wtev[par + (NWR if sfx == "p" else 0)]
                            evrg = f"wtev{par + (NWR if sfx == 'p' else 0)}"
                            sch.add("vector", (lambda ev=ev, bsel=bsel, n=n: (lambda e: e.tensor_copy(
                                ev[:, 0:n * 128], pb[bsel][:].bitcast(BF16)[:, 0:n * 128])))(),
                                reads=[brg], writes=[evrg], tag=f"{tag}tev{sfx}{oc}_{p}_{gi}", fsz=n * 128)
                            sch.dma("gpsimd",
                                    dv[:, p * 16 + gi:p * 16 + gi + n, oc * 128:(oc + 1) * 128],
                                    ev[:, 0:n * 128].rearrange("p (c q) -> p c q", q=128),
                                    reads=[evrg], writes=[f"{sreg}#{oc % 4}"], tag=f"{tag}st{sfx}{oc}_{p}_{gi}")
                        return evac

                    def piece_chain(oc, p):
                        par = _wn[0] % NWR
                        _wn[0] += 1
                        w = wbig[par]; wrg = f"wbig{par}"
                        qtc = qtw[par]
                        sch.begin_collect()
                        sch.dma("sync", w[:], src_ap[oc * 128:(oc + 1) * 128, p * 2048:(p + 1) * 2048],
                                writes=[wrg], tag=f"{tag}ld{oc}_{p}")
                        if do_quant:
                            quant_chunk(w[:], wrg, 2048, qtc, f"{tag}qz{oc}_{p}")
                        else:
                            sch.add("vector", (lambda w=w, qtc=qtc: (lambda e: e.tensor_copy(
                                qtc["q"][0][:, 0:2048], w[:])))(),
                                reads=[wrg], writes=[qtc["q"][1]], tag=f"{tag}cast{oc}_{p}", fsz=2048)
                        chain = sch.end_collect()

                        def do_T(oc=oc, p=p, qtc=qtc, par=par):
                            cols = [c * 128 for c in range(16)]
                            transpose_pack(qtc["q"][0], qtc["q"][1], cols, [4, 5],
                                           mk_evac(oc, p, dview, "n", screg, par, tag), f"{tag}t{oc}_{p}_")
                        chain.append(do_T)
                        return chain

                    pieces = [(oc, p) for oc in range(o_chunks) for p in range(npc)]
                    for i in range(0, len(pieces), NWR):
                        chains = [piece_chain(*pc) for pc in pieces[i:i + NWR]]
                        for st in zip_longest(*chains):
                            for t in st:
                                if t is not None:
                                    t()

                prep_weight(W1, sc_qW1T, KCH, D, True, "pw1", "scqW1T")
                prep_weight(W2, sc_qW2T, KC, DH, True, "pw2", "scqW2T")
            sch.fence()

        if "attn" in phases:
            ap_ = ExitStack()
            with ap_:
                xln_fm = sb("xln_fm", [128, KC * T1], BF16, ap_)
                attnout = sb("attnout", [128, NH * R], BF16, ap_)

                a1 = ExitStack()
                with a1:
                    xc = [sb(f"a_xc{i}", [128, D], F32, a1) for i in range(2)]
                    xlntm = [sb(f"a_xt{i}", [128, D], BF16, a1) for i in range(2)]
                    lstats = sb("a_ls", [128, 16], F32, a1)
                    sqs1 = sb("a_sq", [128, D], F32, a1)

                    def a1_chain(tc):
                        xt = xc[tc % 2]; xtrg = f"a_xc{tc % 2}"
                        lt = xlntm[tc % 2]; ltrg = f"a_xt{tc % 2}"
                        scol = (tc % 2) * 8
                        sch.begin_collect()
                        sch.dma("sync", xt[:], x_in[tc * 128:(tc + 1) * 128, :], writes=[xtrg], tag=f"a_lx{tc}")
                        ln_chunk(xt[:, 0:D], xtrg, lstats, f"a_ls{tc % 2}", scol, sqs1[:], "a_sq", f"a_l{tc}")
                        ln_apply(xt[:, 0:D], xtrg, lstats, f"a_ls{tc % 2}", scol, lt[:, 0:D], ltrg, f"a_l{tc}")
                        chain = sch.end_collect()

                        def do_T(tc=tc, lt=lt, ltrg=ltrg):
                            def evx(bsel, gi, n, brg):
                                sch.add("vector", (lambda bsel=bsel, gi=gi, n=n: (lambda e: e.tensor_copy(
                                    xln_fm.rearrange("p (c t) -> p c t", t=T1)[:, gi:gi + n, tc * 128:(tc + 1) * 128],
                                    pb[bsel][:].bitcast(BF16)[:, 0:n * 128].rearrange("p (c q) -> p c q", q=128))))(),
                                    reads=[brg], writes=["xln_fm"], tag=f"a_xe{tc}_{gi}")
                            transpose_pack(lt, ltrg, [c * 128 for c in range(KC)], [4, 5], evx, f"a_xT{tc}_")
                        chain.append(do_T)
                        return chain

                    for pi in range(0, NTC, 2):
                        chains = [a1_chain(pi), a1_chain(pi + 1)]
                        for st in zip_longest(*chains):
                            for t in st:
                                if t is not None:
                                    t()
                sch.fence()
                dbg("dbg_xln", xln_fm[:], "xln_fm", [128, KC * T1], BF16)

                pass

                a2 = ExitStack()
                with a2:
                    tabsb = sb("tabsb", [128, 4 * T1], BF16, a2)
                    maskm = sb("maskm", [128, 8 * WIN], BF16, a2)
                    maskb0 = sb("maskb0", [128, 4 * WIN], BF16, a2)
                    sch.dma("sync", tabsb[:], tabs, writes=["tabsb"], tag="ld_tabs")
                    sch.dma("sync", maskm[:], maskM, writes=["maskm"], tag="ld_maskm")
                    sch.dma("sync", maskb0[:], maskB0, writes=["maskb0"], tag="ld_maskb0")
                    q_fm = sb("q_fm", [128, 4 * R], BF16, a2)
                    k_fm = sb("k_fm", [128, 4 * T1], BF16, a2)
                    v_tm = sb("v_tm", [128, NTC * 512], BF16, a2)
                    p_sbr = [sb(f"p_sb{i}", [128, 8 * WIN], BF16, a2) for i in range(2)]
                    rt = [sb(f"a_rt{i}", [128, 512], BF16, a2) for i in range(4)]
                    ssumr = [sb(f"a_ss{i}", [1, 512], F32, a2) for i in range(2)]
                    srecipr = [sb(f"a_sr{i}", [1, 512], F32, a2) for i in range(2)]
                    recipBr = [sb(f"a_rB{i}", [128, 512], F32, a2) for i in range(2)]
                    wcb2 = [strip1[:, 0:D], strip1[:, D:2 * D]]

                    SWAPM = []
                    for i in range(0, 32, 2):
                        SWAPM += [i + 1, i]
                    for g in range(4):
                        for (wsrc, dst, dstrg, nt, toff, ctab, stab) in (
                            (aWq, q_fm, "q_fm", 2, HL, 2 * T1, 3 * T1),
                            (aWk, k_fm, "k_fm", 3, 0, 0, T1),
                        ):
                            mk_wstrip(strip0, "strip0", wsrc, g, (3, 4), wcb2, "a_wc", f"a_w{dstrg}{g}")
                            for ho in range(4):
                                for tt in range(nt):
                                    t0 = toff + tt * 512
                                    bank = (ho * nt + tt) % 2
                                    for kc2 in range(KC):
                                        sch.add("tensor", (lambda bank=bank, kc2=kc2, ho=ho, t0=t0: (lambda e: e.matmul(
                                            pb[bank][:],
                                            strip0[:, kc2 * 512 + ho * 128: kc2 * 512 + (ho + 1) * 128],
                                            xln_fm[:, kc2 * T1 + t0: kc2 * T1 + t0 + 512],
                                            start=(kc2 == 0), stop=(kc2 == KC - 1))))(),
                                            reads=(["strip0", "xln_fm"] + ([PB[bank]] if kc2 else [])),
                                            writes=[PB[bank]], tag=f"a_p{dstrg}{g}_{ho}_{tt}_{kc2}")
                                    ctt = ctab + t0
                                    stt_ = stab + t0
                                    dcol = ho * (R if dstrg == "q_fm" else T1) + tt * 512
                                    pr = ((ho * nt + tt) % 2) * 2
                                    r0, r1_ = rt[pr], rt[pr + 1]
                                    r0n, r1n = f"a_rt{pr}", f"a_rt{pr + 1}"
                                    sch.add("vector", (lambda bank=bank, r0=r0: (lambda e: e.tensor_copy(
                                        r0[:], pb[bank][:])))(),
                                        reads=[PB[bank]], writes=[r0n], tag=f"a_rv{dstrg}{g}_{ho}_{tt}")
                                    sch.add("vector", (lambda r0=r0, r1_=r1_: (lambda e: e.stream_shuffle(
                                        r1_[:], r0[:], SWAPM)))(),
                                        reads=[r0n], writes=[r1n], tag=f"a_rs{dstrg}{g}_{ho}_{tt}")
                                    sch.add("vector", (lambda ctt=ctt, r0=r0: (lambda e: e.tensor_tensor(
                                        r0[:], r0[:], tabsb[:, ctt:ctt + 512], ALU.mult)))(),
                                        reads=[r0n, "tabsb"], writes=[r0n], tag=f"a_r1{dstrg}{g}_{ho}_{tt}")
                                    sch.add("vector", (lambda stt_=stt_, r1_=r1_: (lambda e: e.tensor_tensor(
                                        r1_[:], r1_[:], tabsb[:, stt_:stt_ + 512], ALU.mult)))(),
                                        reads=[r1n, "tabsb"], writes=[r1n], tag=f"a_r2{dstrg}{g}_{ho}_{tt}")
                                    sch.add("vector", (lambda dst=dst, dcol=dcol, r0=r0, r1_=r1_: (lambda e: e.tensor_tensor(
                                        dst[:, dcol:dcol + 512], r0[:], r1_[:], ALU.add)))(),
                                        reads=[r0n, r1n], writes=[dstrg], tag=f"a_r3{dstrg}{g}_{ho}_{tt}")

                        mk_wstrip(strip0, "strip0", aWv, g, (3, 4), wcb2, "a_wc", f"a_wv{g}")
                        for tc in range(NTC):
                            for kc2 in range(KC):
                                sch.add("tensor", (lambda kc2=kc2, tc=tc: (lambda e: e.matmul(
                                    pb[2][:],
                                    xln_fm[:, kc2 * T1 + tc * 128: kc2 * T1 + (tc + 1) * 128],
                                    strip0[:, kc2 * 512:(kc2 + 1) * 512],
                                    start=(kc2 == 0), stop=(kc2 == KC - 1))))(),
                                    reads=(["strip0", "xln_fm"] + ([PB[2]] if kc2 else [])),
                                    writes=[PB[2]], tag=f"a_pv{g}_{tc}_{kc2}")
                            sch.add("vector", (lambda tc=tc: (lambda e: e.tensor_copy(
                                v_tm[:, tc * 512:(tc + 1) * 512], pb[2][:])))(),
                                reads=[PB[2]], writes=["v_tm"], tag=f"a_ve{g}_{tc}")

                        if g == 0:
                            dbg("dbg_q", q_fm[:], "q_fm", [128, 4 * R], BF16)
                            dbg("dbg_k", k_fm[:], "k_fm", [128, 4 * T1], BF16)
                            dbg("dbg_v", v_tm[:], "v_tm", [128, NTC * 512], BF16)
                        def u_stage1(ho, b):
                            h = g * 4 + ho
                            up = b
                            p_sb = p_sbr[up]; p_rg = f"p_sb{up}"
                            for jc in range(8):
                                bank = 3 + (jc % 2)
                                kcol = ho * T1 + b * 512 + jc * 128
                                qcol = ho * R + b * 512
                                sch.add("tensor", (lambda bank=bank, kcol=kcol, qcol=qcol: (lambda e: e.matmul(
                                    pb[bank][:], k_fm[:, kcol:kcol + 128], q_fm[:, qcol:qcol + 512],
                                    start=True, stop=False)))(),
                                    reads=["k_fm", "q_fm"], writes=[PB[bank]], tag=f"a_s{h}_{b}_{jc}")
                                if b == 0 and jc < 4:
                                    mt, mtrg, mcol = maskb0, "maskb0", jc * 512
                                else:
                                    mt, mtrg, mcol = maskm, "maskm", jc * 512
                                sch.add("tensor", (lambda bank=bank, mt=mt, mcol=mcol: (lambda e: e.matmul(
                                    pb[bank][:], ident[:], mt[:, mcol:mcol + 512],
                                    start=False, stop=True)))(),
                                    reads=[mtrg, "ident", PB[bank]], writes=[PB[bank]], tag=f"a_m{h}_{b}_{jc}")
                                sch.add("scalar", (lambda bank=bank, jc=jc, p_sb=p_sb: (lambda e: e.activation(
                                    p_sb[:, jc * 512:(jc + 1) * 512], pb[bank][:], AF.Exp,
                                    bias=CEXP, scale=1.0)))(),
                                    reads=[PB[bank], "consts"], writes=[p_rg], tag=f"a_e{h}_{b}_{jc}")

                        def u_sums(ho, b, su_b):
                            h = g * 4 + ho
                            up = b
                            p_sb = p_sbr[up]; p_rg = f"p_sb{up}"
                            ssum = ssumr[up]; ss_rg = f"a_ss{up}"
                            for jc in range(8):
                                sch.add("tensor", (lambda jc=jc, p_sb=p_sb, su_b=su_b: (lambda e: e.matmul(
                                    pb[su_b][0:1, :], ones_c[:], p_sb[:, jc * 512:(jc + 1) * 512],
                                    start=(jc == 0), stop=(jc == 7))))(),
                                    reads=(["ones_c", p_rg] + ([PB[su_b]] if jc else [])),
                                    writes=[PB[su_b]], tag=f"a_su{h}_{b}_{jc}")
                            sch.add("vector", (lambda ssum=ssum, su_b=su_b: (lambda e: e.tensor_copy(
                                ssum[:], pb[su_b][0:1, :])))(),
                                reads=[PB[su_b]], writes=[ss_rg], tag=f"a_se{h}_{b}")
                            sch.add("scalar", (lambda ssum=ssum, srecip=srecipr[up]: (lambda e: e.activation(
                                srecip[:], ssum[:], AF.Ln, bias=0.0, scale=1.0)))(),
                                reads=[ss_rg, "consts"], writes=[f"a_sr{up}"], tag=f"a_rl{h}_{b}")
                            sch.add("scalar", (lambda srecip=srecipr[up]: (lambda e: e.activation(
                                srecip[:], srecip[:], AF.Exp, bias=0.0, scale=-1.0)))(),
                                reads=[f"a_sr{up}", "consts"], writes=[f"a_sr{up}"], tag=f"a_re{h}_{b}")

                        def u_stage2(ho, b, rb_b, av_b):
                            h = g * 4 + ho
                            up = b
                            p_sb = p_sbr[up]; p_rg = f"p_sb{up}"
                            ssum = ssumr[up]; ss_rg = f"a_ss{up}"
                            srecip = srecipr[up]; sr_rg = f"a_sr{up}"
                            recipB = recipBr[up]; rB_rg = f"a_rB{up}"
                            sch.add("tensor", (lambda srecip=srecip, rb_b=rb_b: (lambda e: e.matmul(
                                pb[rb_b][:], ones_rf[:], srecip[:], start=True, stop=True)))(),
                                reads=[sr_rg, "ones_rf"], writes=[PB[rb_b]], tag=f"a_rb{h}_{b}")
                            sch.add("vector", (lambda recipB=recipB, rb_b=rb_b: (lambda e: e.tensor_copy(
                                recipB[:], pb[rb_b][:])))(),
                                reads=[PB[rb_b]], writes=[rB_rg], tag=f"a_rbe{h}_{b}")
                            for jc in range(8):
                                vcol = (4 * b + jc) * 512 + ho * 128
                                sch.add("tensor", (lambda jc=jc, vcol=vcol, p_sb=p_sb, av_b=av_b: (lambda e: e.matmul(
                                    pb[av_b][:], v_tm[:, vcol:vcol + 128], p_sb[:, jc * 512:(jc + 1) * 512],
                                    start=(jc == 0), stop=(jc == 7))))(),
                                    reads=(["v_tm", p_rg] + ([PB[av_b]] if jc else [])),
                                    writes=[PB[av_b]], tag=f"a_av{h}_{b}_{jc}")
                            acol = h * R + b * 512
                            sch.add("vector", (lambda acol=acol, recipB=recipB, av_b=av_b: (lambda e: e.tensor_tensor(
                                attnout[:, acol:acol + 512], pb[av_b][:], recipB[:], ALU.mult)))(),
                                reads=[PB[av_b], rB_rg], writes=["attnout"], tag=f"a_ao{h}_{b}")
                            if h == 0 and b == 1:
                                dbg("dbg_p", p_sb[:], p_rg, [128, 8 * WIN], BF16)
                                dbg("dbg_sum", ssum[:], ss_rg, [1, 512], F32)
                                dbg("dbg_recipB", recipB[:], rB_rg, [128, 512], F32)

                        for ho in range(4):
                            u_stage1(ho, 0)
                            u_stage1(ho, 1)
                            u_sums(ho, 0, 5)
                            u_sums(ho, 1, 0)
                            u_stage2(ho, 0, 6, 7)
                            u_stage2(ho, 1, 1, 2)
                sch.fence()
                dbg("dbg_attnout", attnout[:], "attnout", [128, NH * R], BF16)

                a3 = ExitStack()
                with a3:
                    xres = [sb(f"a_xr{i}", [128, D], F32, a3) for i in range(NTO)]
                    wcb3 = [sb(f"a_w3c{i}", [128, D], BF16, a3) for i in range(2)]
                    for tch in range(NTO):
                        sch.dma("sync", xres[tch][:], x_in[HL + tch * 128: HL + (tch + 1) * 128, :],
                                writes=[f"a_xr{tch}"], tag=f"a_lr{tch}")
                    for ot in range(4):
                        wbuf = strip0 if ot % 2 == 0 else strip1
                        wrg = "strip0" if ot % 2 == 0 else "strip1"
                        mk_wstrip(wbuf, wrg, aWo, ot, (6, 7), wcb3, "a_w3c", f"a_wo{ot}")
                        for tch in range(NTO):
                            bank = tch % 4
                            for ic in range(KC):
                                sch.add("tensor", (lambda bank=bank, ic=ic, tch=tch, wbuf=wbuf: (lambda e: e.matmul(
                                    pb[bank][:], attnout[:, ic * R + tch * 128: ic * R + (tch + 1) * 128],
                                    wbuf[:, ic * 512:(ic + 1) * 512],
                                    start=(ic == 0), stop=(ic == KC - 1))))(),
                                    reads=(["attnout", wrg] + ([PB[bank]] if ic else [])),
                                    writes=[PB[bank]], tag=f"a_o{ot}_{tch}_{ic}")
                            sch.add("vector", (lambda bank=bank, tch=tch, ot=ot: (lambda e: e.tensor_tensor(
                                xres[tch][:, ot * 512:(ot + 1) * 512], pb[bank][:],
                                xres[tch][:, ot * 512:(ot + 1) * 512], ALU.add)))(),
                                reads=[PB[bank], f"a_xr{tch}"], writes=[f"a_xr{tch}"], tag=f"a_or{ot}_{tch}")
                    for tch in range(NTO):
                        sch.dma("gpsimd", sc_X[tch * 128:(tch + 1) * 128, :], xres[tch][:],
                                reads=[f"a_xr{tch}"], writes=[f"scX{tch}_{q}" for q in range(4)],
                                tag=f"a_sx{tch}")
            sch.fence()

        def ffn_phase(idx, x_src, final):
            fp_ = ExitStack()
            with fp_:
                pre = f"f{idx}"
                gamB = sb(pre + "gamB", [128, D], BF16, fp_)
                betB = sb(pre + "betB", [128, D], BF16, fp_)
                b1t = [sb(pre + f"b1t{i}", [1, 512], BF16, fp_) for i in range(2)]
                b2t = [sb(pre + f"b2t{i}", [1, 512], BF16, fp_) for i in range(2)]
                xc2 = [sb(pre + f"xc{i}", [128, D], F32, fp_) for i in range(2)]
                xb2 = [sb(pre + f"xb{i}", [128, D], BF16, fp_) for i in range(2)]
                t0b = sb(pre + "t0", [128, D], F32, fp_)
                lstat = sb(pre + "ls", [128, 16], F32, fp_)
                lnq_fm = sb(pre + "lnq", [128, KC * 512], BF16, fp_)
                hq_fm = sb(pre + "hq", [128, KCH * 512], BF16, fp_)
                hg = [sb(pre + f"hg{i}", [128, 4 * 512], BF16, fp_) for i in range(2)]
                res = [sb(pre + f"res{i}", [128, D], BF16, fp_) for i in range(4)]
                xs2 = [sb(pre + f"xs{i}", [128, 512], F32, fp_) for i in range(2)]
                f2r = [sb(pre + f"f2{i}", [128, 512], BF16, fp_) for i in range(4)]
                qt2 = [make_qt(pre + "q0", D, fp_), make_qt(pre + "q1", D, fp_)]

                rw_ = ExitStack()
                with rw_:
                    rows = [sb(pre + f"lnr{i}", [1, 512], BF16, rw_) for i in range(2)]
                    for nm, srcv, dst, dstrg in (("g", lnw, gamB, pre + "gamB"),
                                                 ("b", lnb, betB, pre + "betB")):
                        for j in range(4):
                            row = rows[j % 2]; rowrg = pre + f"lnr{j % 2}"
                            sch.dma("gpsimd", row[:], srcv.rearrange("(a b) -> a b", a=1)[0:1, j * 512:(j + 1) * 512],
                                    writes=[rowrg], tag=f"{pre}ld{nm}{j}")
                            sch.add("tensor", (lambda row=row: (lambda e: e.matmul(
                                pb[0][:], ones_r[:], row[0:1, :], start=True, stop=True)))(),
                                reads=[rowrg, "ones_r"], writes=[PB[0]], tag=f"{pre}gb{nm}{j}")
                            sch.add("vector", (lambda dst=dst, j=j: (lambda e: e.tensor_copy(
                                dst[:, j * 512:(j + 1) * 512], pb[0][:])))(),
                                reads=[PB[0]], writes=[dstrg], tag=f"{pre}ge{nm}{j}")
                sch.fence()

                scT_w1 = sc_qW1T.rearrange("(kc p) o -> p kc o", p=128)
                scT_w2 = sc_qW2T.rearrange("(kc p) o -> p kc o", p=128)

                def ln_chain(half, tci, tbanks):
                    tch = half * 4 + tci
                    xt = xc2[tci % 2]; xtrg = pre + f"xc{tci % 2}"
                    xbt = xb2[tci % 2]; xbrg = pre + f"xb{tci % 2}"
                    scol = (tci % 2) * 8
                    lsrg = pre + f"ls{tci % 2}"
                    qtc = qt2[tci % 2]
                    sch.begin_collect()
                    sch.dma("sync", xt[:], x_src[tch * 128:(tch + 1) * 128, :], writes=[xtrg],
                            reads=[f"scX{tch}_{q}" for q in range(4)], tag=f"{pre}lx{tch}")
                    ln_chunk(xt[:, 0:D], xtrg, lstat, lsrg, scol, t0b[:], pre + "t0", f"{pre}l2{tch}",
                             save_to=(ln2stats, "ln2stats", tch * 2))
                    ln_apply(xt[:, 0:D], xtrg, lstat, lsrg, scol, xbt[:, 0:D], xbrg, f"{pre}l2{tch}")
                    ln_chunk(xbt[:, 0:D], xbrg, lstat, lsrg, scol, t0b[:], pre + "t0", f"{pre}l3{tch}")
                    sch.add("vector", (lambda xbt=xbt, xt=xt, scol=scol: (lambda e: e.tensor_scalar(
                        xt[:], xbt[:], lstat[:, scol:scol + 1], lstat[:, scol + 1:scol + 2],
                        ALU.subtract, ALU.mult)))(),
                        reads=[xbrg, lsrg], writes=[xtrg], tag=f"{pre}l3a{tch}")
                    sch.add("vector", (lambda xt=xt: (lambda e: e.tensor_tensor(
                        xt[:], xt[:], gamB[:], ALU.mult)))(),
                        reads=[xtrg, pre + "gamB"], writes=[xtrg], tag=f"{pre}l3b{tch}", fsz=D)
                    sch.add("vector", (lambda xt=xt, xbt=xbt: (lambda e: e.tensor_tensor(
                        xbt[:], xt[:], betB[:], ALU.add)))(),
                        reads=[xtrg, pre + "betB"], writes=[xbrg], tag=f"{pre}l3c{tch}", fsz=D)
                    quant_chunk(xbt[:, 0:D], xbrg, D, qtc, f"{pre}qz{tch}")
                    chain = sch.end_collect()

                    def do_T(tci=tci, tch=tch, qtc=qtc, tbanks=tbanks):
                        def ev2(bsel, gi, n, brg):
                            sch.add("vector", (lambda bsel=bsel, gi=gi, n=n: (lambda e: e.tensor_copy(
                                lnq_fm.rearrange("p (c t) -> p c t", t=512)[:, gi:gi + n, tci * 128:(tci + 1) * 128],
                                pb[bsel][:].bitcast(BF16)[:, 0:n * 128].rearrange("p (c q) -> p c q", q=128))))(),
                                reads=[brg], writes=[pre + f"lnq{tci}"], tag=f"{pre}lqe{tch}_{gi}", fsz=n * 128)
                        transpose_pack(qtc["q"][0], qtc["q"][1], [c * 128 for c in range(KC)], tbanks,
                                       ev2, f"{pre}lqT{tch}_")
                    chain.append(do_T)
                    return chain

                def ln_stage(half, tbanks=(4, 5)):
                    for pi in range(2):
                        chains = [ln_chain(half, pi * 2 + j, tbanks) for j in (0, 1)]
                        for st in zip_longest(*chains):
                            for t in st:
                                if t is not None:
                                    t()

                def w1_stage(half):
                    pend_T = []
                    for ot in range(16):
                        wbuf = strip0 if ot % 2 == 0 else strip1
                        wrg = "strip0" if ot % 2 == 0 else "strip1"
                        for hf in range(2):
                            sch.dma("sync" if ot % 2 == 0 else "gpsimd",
                                    wbuf[:, hf * 4096:(hf + 1) * 4096].rearrange("p (kc o) -> p kc o", o=512),
                                    scT_w1[:, hf * 8:(hf + 1) * 8, ot * 512:(ot + 1) * 512],
                                    reads=[f"scqW1T#{i}" for i in range(4)], writes=[wrg + "ab"[hf]],
                                    tag=f"{pre}w1l{half}_{ot}_{hf}")
                        hgb = hg[ot % 2]; hgrg = pre + f"hg{ot % 2}"
                        bt = b1t[ot % 2]; btrg = pre + f"b1t{ot % 2}"
                        sch.dma("gpsimd", bt[:], b1v.rearrange("(a b) -> a b", a=1)[0:1, ot * 512:(ot + 1) * 512],
                                writes=[btrg], tag=f"{pre}w1bl{half}_{ot}")
                        for tci in range(4):
                            bank = tci
                            for kc2 in range(KC):
                                sch.add("tensor", (lambda bank=bank, kc2=kc2, tci=tci, wbuf=wbuf: (lambda e: e.matmul(
                                    pb[bank][:], lnq_fm[:, kc2 * 512 + tci * 128: kc2 * 512 + (tci + 1) * 128],
                                    wbuf[:, kc2 * 512:(kc2 + 1) * 512],
                                    start=(kc2 == 0), stop=False)))(),
                                    reads=([pre + f"lnq{tci}", wrg + "ab"[kc2 // 8]] + ([PB[bank]] if kc2 else [])),
                                    writes=[PB[bank]], tag=f"{pre}w1m{half}_{ot}_{tci}_{kc2}")
                            sch.add("tensor", (lambda bank=bank, bt=bt: (lambda e: e.matmul(
                                pb[bank][:], ones_r[:], bt[:],
                                start=False, stop=True)))(),
                                reads=[btrg, "ones_r", PB[bank]], writes=[PB[bank]], tag=f"{pre}w1b{half}_{ot}_{tci}")
                            fb = f2r[tci % 2]; fbrg = pre + f"f2{tci % 2}"
                            sch.add("vector", (lambda bank=bank, fb=fb: (lambda e: e.tensor_copy(
                                fb[:], pb[bank][:])))(),
                                reads=[PB[bank]], writes=[fbrg], tag=f"{pre}w1e{half}_{ot}_{tci}")
                            sch.add("scalar", (lambda hgb=hgb, tci=tci, fb=fb: (lambda e: e.activation(
                                hgb[:, tci * 512:(tci + 1) * 512], fb[:], AF.Gelu_apprx_tanh,
                                bias=0.0, scale=1.0)))(),
                                reads=[fbrg, "consts"], writes=[hgrg], tag=f"{pre}w1g{half}_{ot}_{tci}")
                        qtc = qt2[ot % 2]
                        quant_chunk(hgb[:, 0:2048], hgrg, 2048, qtc, f"{pre}hz{half}_{ot}")

                        def hq_transposes(ot=ot, qtc=qtc):
                            for grp2 in range(2):
                                bsel = 4 + grp2
                                for oc_ in range(2):
                                    for tci_ in range(4):
                                        col = tci_ * 512 + (grp2 * 2 + oc_) * 128
                                        slot = oc_ * 4 + tci_
                                        sch.add("tensor", (lambda bsel=bsel, slot=slot, col=col, qtc=qtc: (lambda e: e.transpose(
                                            pb[bsel][:].bitcast(BF16)[:, slot * 128:(slot + 1) * 128],
                                            qtc["q"][0][:, col:col + 128], ident[:])))(),
                                            reads=[qtc["q"][1], "ident"], writes=[PB[bsel]], tag=f"{pre}hT{half}_{ot}_{grp2}_{oc_}_{tci_}")
                                    dest = hq_fm[:, (ot * 4 + grp2 * 2) * 512: (ot * 4 + grp2 * 2 + 2) * 512]
                                    sch.add("vector", (lambda bsel=bsel, dest=dest: (lambda e: e.tensor_copy(
                                        dest.rearrange("p (a b q) -> p a b q", b=4, q=128),
                                        pb[bsel][:].bitcast(BF16)[:, 0:1024].rearrange("p (a b q) -> p a b q", b=4, q=128))))(),
                                        reads=[PB[bsel]], writes=[pre + "hq"], tag=f"{pre}hqe{half}_{ot}_{grp2}")
                        pend_T.append(hq_transposes)
                        if len(pend_T) > 1:
                            pend_T.pop(0)()
                    while pend_T:
                        pend_T.pop(0)()

                def w2_stage(half, hook=None):
                    tbase = half * 4
                    for tci in range(4):
                        tch = tbase + tci
                        xt = xc2[tci % 2]; xtrg = pre + f"xc{tci % 2}"
                        sch.dma("sync", xt[:], x_src[tch * 128:(tch + 1) * 128, :], writes=[xtrg],
                            reads=[f"scX{tch}_{q}" for q in range(4)], tag=f"{pre}rx{tch}")
                        sch.add("vector", (lambda xt=xt, tci=tci, tch=tch: (lambda e: e.tensor_scalar(
                            res[tci][:], xt[:], ln2stats[:, tch * 2:tch * 2 + 1], ln2stats[:, tch * 2 + 1:tch * 2 + 2],
                            ALU.subtract, ALU.mult)))(),
                            reads=[xtrg, "ln2stats"], writes=[pre + f"res{tci}"], tag=f"{pre}rr{tch}", fsz=D)

                    for o2 in range(4):
                        bt2 = b2t[o2 % 2]; bt2rg = pre + f"b2t{o2 % 2}"
                        sch.dma("gpsimd", bt2[:], b2v.rearrange("(a b) -> a b", a=1)[0:1, o2 * 512:(o2 + 1) * 512],
                                writes=[bt2rg], tag=f"{pre}w2bl{half}_{o2}")
                        for kg in range(4):
                            wbuf = strip0 if kg % 2 == 0 else strip1
                            wrg = "strip0" if kg % 2 == 0 else "strip1"
                            for hf in range(2):
                                sch.dma("sync" if kg % 2 == 0 else "gpsimd",
                                        wbuf[:, hf * 4096:(hf + 1) * 4096].rearrange("p (kc o) -> p kc o", o=512),
                                        scT_w2[:, kg * 16 + hf * 8: kg * 16 + (hf + 1) * 8, o2 * 512:(o2 + 1) * 512],
                                        reads=[f"scqW2T#{i}" for i in range(4)], writes=[wrg + "ab"[hf]],
                                        tag=f"{pre}w2l{half}_{o2}_{kg}_{hf}")
                            for tci in range(4):
                                bank = 4 + tci
                                for kk in range(16):
                                    kc2 = kg * 16 + kk
                                    sch.add("tensor", (lambda bank=bank, kc2=kc2, kk=kk, tci=tci, wbuf=wbuf: (lambda e: e.matmul(
                                        pb[bank][:], hq_fm[:, kc2 * 512 + tci * 128: kc2 * 512 + (tci + 1) * 128],
                                        wbuf[:, kk * 512:(kk + 1) * 512],
                                        start=(kc2 == 0), stop=False)))(),
                                        reads=([pre + "hq", wrg + "ab"[kk // 8]] + ([PB[bank]] if kc2 else [])),
                                        writes=[PB[bank]], tag=f"{pre}w2m{half}_{o2}_{tci}_{kc2}")
                        for tci in range(4):
                            bank = 4 + tci
                            sch.add("tensor", (lambda bank=bank, bt2=bt2: (lambda e: e.matmul(
                                pb[bank][:], ones_r[:], bt2[:],
                                start=False, stop=True)))(),
                                reads=[bt2rg, "ones_r", PB[bank]], writes=[PB[bank]], tag=f"{pre}w2b{half}_{o2}_{tci}")
                        for tci in range(4):
                            bank = 4 + tci
                            fb = f2r[tci]; fbrg = pre + f"f2{tci}"
                            sch.add("vector", (lambda bank=bank, tci=tci, o2=o2, fb=fb: (lambda e: e.scalar_tensor_tensor(
                                fb[:], pb[bank][:], 1.0, res[tci][:, o2 * 512:(o2 + 1) * 512], ALU.mult, ALU.add)))(),
                                reads=[PB[bank], pre + f"res{tci}"], writes=[fbrg], tag=f"{pre}w2f{half}_{o2}_{tci}")
                        for tci in range(4):
                            tch = tbase + tci
                            fb = f2r[tci]; fbrg = pre + f"f2{tci}"
                            xsl = xs2[tci % 2]; xslrg = pre + f"xs{tci % 2}"
                            sch.dma("sync", xsl[:], x_src[tch * 128:(tch + 1) * 128, o2 * 512:(o2 + 1) * 512],
                                    reads=[f"scX{tch}_{o2}"], writes=[xslrg], tag=f"{pre}w2x{half}_{o2}_{tci}")
                            sch.add("vector", (lambda xsl=xsl, fb=fb: (lambda e: e.tensor_tensor(
                                xsl[:], fb[:], xsl[:], ALU.add)))(),
                                reads=[fbrg, xslrg], writes=[xslrg], tag=f"{pre}w2a{half}_{o2}_{tci}")
                            dst = out if final else sc_X
                            sch.dma("gpsimd", dst[tch * 128:(tch + 1) * 128, o2 * 512:(o2 + 1) * 512], xsl[:],
                                    reads=[xslrg], writes=([f"scX{tch}_{o2}"] if not final else []),
                                    tag=f"{pre}w2s{half}_{o2}_{tci}")
                        if hook is not None:
                            hook(o2)

                def ln1_hook(o2):
                    if o2 in (0, 1):
                        chains = [ln_chain(1, o2 * 2 + j, (0, 1)) for j in (0, 1)]
                        for st in zip_longest(*chains):
                            for t in st:
                                if t is not None:
                                    t()

                ln_stage(0)
                w1_stage(0)
                w2_stage(0, hook=ln1_hook)
                w1_stage(1)
                w2_stage(1)
            sch.fence()

        if "ffn1" in phases:
            ffn_phase(1, sc_X, final=False)

        if "attnb" in phases:
            bp = ExitStack()
            with bp:
                xln3_fm = sb("b_xln", [128, KC * R], BF16, bp)
                v3_fm = sb("b_v3", [128, KC * R], BF16, bp)
                xc3 = [sb(f"b_xc{i}", [128, D], F32, bp) for i in range(2)]
                xb3 = [sb(f"b_xb{i}", [128, D], BF16, bp) for i in range(2)]
                lstat3 = sb("b_ls", [128, 16], F32, bp)
                sqs3 = sb("b_sq", [128, D], F32, bp)
                xsl3 = [sb(f"b_xs{i}", [128, 512], F32, bp) for i in range(2)]

                def b_chain(tch):
                    xt = xc3[tch % 2]; xtrg = f"b_xc{tch % 2}"
                    xbt = xb3[tch % 2]; xbrg = f"b_xb{tch % 2}"
                    scol = (tch % 2) * 8
                    sch.begin_collect()
                    sch.dma("sync", xt[:], sc_X[tch * 128:(tch + 1) * 128, :], writes=[xtrg],
                            reads=[f"scX{tch}_{q}" for q in range(4)], tag=f"b_lx{tch}")
                    ln_chunk(xt[:, 0:D], xtrg, lstat3, f"b_ls{tch % 2}", scol, sqs3[:], "b_sq", f"b_l{tch}")
                    ln_apply(xt[:, 0:D], xtrg, lstat3, f"b_ls{tch % 2}", scol, xbt[:, 0:D], xbrg, f"b_l{tch}")
                    chain = sch.end_collect()

                    def do_T(tch=tch, xbt=xbt, xbrg=xbrg):
                        def evb(bsel, gi, n, brg):
                            sch.add("vector", (lambda bsel=bsel, gi=gi, n=n: (lambda e: e.tensor_copy(
                                xln3_fm.rearrange("p (c t) -> p c t", t=R)[:, gi:gi + n, tch * 128:(tch + 1) * 128],
                                pb[bsel][:].bitcast(BF16)[:, 0:n * 128].rearrange("p (c q) -> p c q", q=128))))(),
                                reads=[brg], writes=["b_xln"], tag=f"b_xe{tch}_{gi}")
                        transpose_pack(xbt, xbrg, [c * 128 for c in range(KC)], [4, 5], evb, f"b_xT{tch}_")
                    chain.append(do_T)
                    return chain

                for pi in range(0, NTO, 2):
                    chains = [b_chain(pi), b_chain(pi + 1)]
                    for st in zip_longest(*chains):
                        for t in st:
                            if t is not None:
                                t()

                wcbb = [sb(f"b_wc{i}", [128, D], BF16, bp) for i in range(2)]
                for g in range(4):
                    wbuf = strip0 if g % 2 == 0 else strip1
                    wrg = "strip0" if g % 2 == 0 else "strip1"
                    mk_wstrip(wbuf, wrg, bWqkv[2 * D:3 * D, :], g, (6, 7), wcbb, "b_wc", f"b_wv{g}")
                    for oo in range(4):
                        for tt in range(2):
                            bank = oo
                            for kc2 in range(KC):
                                sch.add("tensor", (lambda bank=bank, kc2=kc2, oo=oo, tt=tt, wbuf=wbuf: (lambda e: e.matmul(
                                    pb[bank][:], wbuf[:, kc2 * 512 + oo * 128: kc2 * 512 + (oo + 1) * 128],
                                    xln3_fm[:, kc2 * R + tt * 512: kc2 * R + tt * 512 + 512],
                                    start=(kc2 == 0), stop=(kc2 == KC - 1))))(),
                                    reads=([wrg, "b_xln"] + ([PB[bank]] if kc2 else [])),
                                    writes=[PB[bank]], tag=f"b_v{g}_{oo}_{tt}_{kc2}")
                            oc = g * 4 + oo
                            sch.add("vector", (lambda bank=bank, oc=oc, tt=tt: (lambda e: e.tensor_copy(
                                v3_fm[:, oc * R + tt * 512: oc * R + tt * 512 + 512], pb[bank][:])))(),
                                reads=[PB[bank]], writes=["b_v3"], tag=f"b_ve{g}_{oo}_{tt}")

                for ot in range(4):
                    wbuf = strip0 if ot % 2 == 0 else strip1
                    wrg = "strip0" if ot % 2 == 0 else "strip1"
                    mk_wstrip(wbuf, wrg, bWo, ot, (0, 1), wcbb, "b_wc", f"b_wo{ot}")
                    for tch in range(NTO):
                        bank = 4 + tch % 4
                        for ic in range(KC):
                            sch.add("tensor", (lambda bank=bank, ic=ic, tch=tch, wbuf=wbuf: (lambda e: e.matmul(
                                pb[bank][:], v3_fm[:, ic * R + tch * 128: ic * R + (tch + 1) * 128],
                                wbuf[:, ic * 512:(ic + 1) * 512],
                                start=(ic == 0), stop=(ic == KC - 1))))(),
                                reads=(["b_v3", wrg] + ([PB[bank]] if ic else [])),
                                writes=[PB[bank]], tag=f"b_o{ot}_{tch}_{ic}")
                        xsl = xsl3[tch % 2]; xslrg = f"b_xs{tch % 2}"
                        sch.dma("sync", xsl[:], sc_X[tch * 128:(tch + 1) * 128, ot * 512:(ot + 1) * 512],
                                reads=[f"scX{tch}_{ot}"], writes=[xslrg], tag=f"b_lxs{ot}_{tch}")
                        sch.add("vector", (lambda bank=bank, xsl=xsl: (lambda e: e.tensor_tensor(
                            xsl[:], pb[bank][:], xsl[:], ALU.add)))(),
                            reads=[PB[bank], xslrg], writes=[xslrg], tag=f"b_or{ot}_{tch}")
                        sch.dma("gpsimd", sc_X[tch * 128:(tch + 1) * 128, ot * 512:(ot + 1) * 512], xsl[:],
                                reads=[xslrg], writes=[f"scX{tch}_{ot}"], tag=f"b_sxs{ot}_{tch}")
            sch.fence()

        if "ffn2" in phases:
            ffn_phase(2, sc_X, final=True)
        else:
            tmpo = sb("tmpo", [128, D], F32)
            for tch in range(NTO):
                sch.dma("sync", tmpo[:], sc_X[tch * 128:(tch + 1) * 128, :], writes=["tmpo"], tag=f"d_l{tch}")
                sch.dma("sync", out[tch * 128:(tch + 1) * 128, :], tmpo[:], reads=["tmpo"], tag=f"d_s{tch}")

        sch.resolve()
        global LAST_SCHED
        LAST_SCHED = sch
        semkeys = set()
        for o in sch.ops:
            if o.done:
                semkeys.add(o.done[0])
            for k, _ in o.waits:
                semkeys.add(k)
        sems = {k: ctx.enter_context(nc.semaphore("_".join(str(p) for p in k))) for k in sorted(semkeys, key=str)}
        blk = ctx.enter_context(nc.Block())
        sch.emit(nc, blk, sems)

    return nc


def make_in_maps(inputs):
    x = np.ascontiguousarray(np.asarray(inputs["L0_attn_a_x"], np.float32).reshape(SEQ, D))
    ident, pswap = host_ident()
    consts = host_consts()
    base = dict(
        aWq=np.ascontiguousarray(inputs["L0_attn_a_Wq"], dtype=np.float32),
        aWk=np.ascontiguousarray(inputs["L0_attn_a_Wk"], dtype=np.float32),
        aWv=np.ascontiguousarray(inputs["L0_attn_a_Wv"], dtype=np.float32),
        aWo=np.ascontiguousarray(inputs["L0_attn_a_Wo"], dtype=np.float32),
        bWqkv=np.ascontiguousarray(inputs["L0_attn_b_weight_qkv"], dtype=np.float32),
        bWo=np.ascontiguousarray(inputs["L0_attn_b_weight_out"], dtype=np.float32),
        W1=np.ascontiguousarray(inputs["L0_ffn_W1"], dtype=np.float32),
        W2=np.ascontiguousarray(inputs["L0_ffn_W2"], dtype=np.float32),
        b1v=np.ascontiguousarray(inputs["L0_ffn_b1"], dtype=np.float32),
        b2v=np.ascontiguousarray(inputs["L0_ffn_b2"], dtype=np.float32),
        lnw=np.ascontiguousarray(inputs["L0_ffn_ln_w"], dtype=np.float32),
        lnb=np.ascontiguousarray(inputs["L0_ffn_ln_b"], dtype=np.float32),
        identI=ident, pswapI=pswap, constsI=consts,
    )
    maps = []
    for c in range(NCORES):
        p0 = c * R
        if c == 0:
            xs = np.concatenate([np.zeros((HL, D), np.float32), x[:R]], axis=0)
        else:
            xs = x[p0 - HL: p0 + R]
        mm_, b0 = host_masks(c)
        m = dict(base)
        m.update(x_in=np.ascontiguousarray(xs), tabs=host_tables(c), maskM=mm_, maskB0=b0)
        maps.append(m)
    return maps


def kernel(**inputs):
    maps = make_in_maps(inputs)
    nc = build_program()
    res = run_bass_kernel_spmd(nc, maps, core_ids=list(range(NCORES)))
    outs = [np.asarray(r["out"], np.float32) for r in res.results]
    return np.concatenate(outs, axis=0).reshape(1, SEQ, D)


# version 32 (final)
# speedup vs baseline: 1.0613x; 1.0219x over previous
import numpy as np
import ml_dtypes
from contextlib import ExitStack
from itertools import zip_longest

import concourse.bass as bass
import concourse.mybir as mybir
from concourse.bass_utils import run_bass_kernel_spmd

F32 = mybir.dt.float32
BF16 = mybir.dt.bfloat16
I32 = mybir.dt.int32
I8 = mybir.dt.int8
AF = mybir.ActivationFunctionType
ALU = mybir.AluOpType
AX = mybir.AxisListType
BF = ml_dtypes.bfloat16

SEQ, D, NH, HD, WIN = 8192, 2048, 16, 128, 512
NCORES = 8
R = SEQ // NCORES
HL = WIN
T1 = R + HL
DH = 8192
KC = D // 128
KCH = DH // 128
NTC = T1 // 128
NTO = R // 128
EPS = 1e-5
CEXP = -12.0
MASKV = -30000.0
CONSTS = [0.0, 1.0, CEXP, -0.5, -1.0, EPS]

ENGINES = ("tensor", "vector", "scalar", "gpsimd", "sync")
NDMASEM = 8
LAST_SCHED = None


class _Op:
    def __init__(self, eng, fn, reads, writes, dma, tag, idx, fence, fsz=0):
        self.eng, self.fn = eng, fn
        self.reads, self.writes = tuple(reads), tuple(writes)
        self.dma, self.tag, self.idx = dma, tag, idx
        self.fsz = fsz
        self.fence = fence
        self.needs_inc = False
        self.done = None
        self.deps = None
        self.waits = None


class Sched:
    def __init__(self):
        self.ops = []
        self._fence_deps = ()
        self._col = None

    def add(self, eng, fn, reads=(), writes=(), dma=False, tag="", fsz=0):
        assert eng in ENGINES, eng
        if self._col is not None:
            self._col.append(lambda: self.add(eng, fn, reads, writes, dma, tag, fsz))
            return None
        o = _Op(eng, fn, reads, writes, dma, tag, len(self.ops), self._fence_deps, fsz)
        self.ops.append(o)
        return o

    def begin_collect(self):
        assert self._col is None
        self._col = []

    def end_collect(self):
        c, self._col = self._col, None
        return c

    @staticmethod
    def _ordered(d, o):
        return (d.eng == o.eng and not d.dma and not o.dma and
                (d.eng == "tensor" or (d.fsz >= 2048 and d.eng in ("vector", "scalar"))))

    def dma(self, eng, out, in_, reads=(), writes=(), tag="", **kw):
        fn = (lambda out=out, in_=in_, kw=kw: (lambda e: e.dma_start(out=out, in_=in_, **kw)))()
        return self.add(eng, fn, reads=reads, writes=writes, dma=True, tag=tag)

    def fence(self):
        last = {}
        for o in self.ops:
            last[(o.eng, o.dma)] = o
        self._fence_deps = tuple(last.values())

    def resolve(self):
        last_w = {}
        readers = {}
        for o in self.ops:
            deps = set(o.fence)
            for r in o.reads:
                w = last_w.get(r)
                if w is not None:
                    deps.add(w)
            for w_ in o.writes:
                lw = last_w.get(w_)
                if lw is not None:
                    deps.add(lw)
                deps.update(readers.get(w_, ()))
            deps.discard(o)
            o.deps = deps
            for r in o.reads:
                readers.setdefault(r, []).append(o)
            for w_ in o.writes:
                last_w[w_] = o
                readers[w_] = []
        for o in self.ops:
            for d in o.deps:
                if not self._ordered(d, o):
                    d.needs_inc = True
        cnt = {}
        dma_idx = {}
        for o in self.ops:
            if o.dma:
                i = dma_idx.get(o.eng, 0)
                dma_idx[o.eng] = i + 1
                key = ("dma", o.eng, i % NDMASEM)
                cnt[key] = 16 * (i // NDMASEM + 1)
                o.done = (key, cnt[key])
                o.dma_semkey = key
            elif o.needs_inc:
                key = ("eng", o.eng)
                cnt[key] = cnt.get(key, 0) + 1
                o.done = (key, cnt[key])
        self.final_counts = dict(cnt)
        marks = {e: {} for e in ENGINES}
        for o in self.ops:
            wl = {}
            for d in o.deps:
                if self._ordered(d, o):
                    continue
                assert d.done is not None, (d.tag, o.tag)
                k, v = d.done
                if wl.get(k, 0) < v:
                    wl[k] = v
            ww = marks[o.eng]
            o.waits = []
            for k, v in sorted(wl.items(), key=lambda kv: str(kv[0])):
                if ww.get(k, 0) < v:
                    o.waits.append((k, v))
                    ww[k] = v

    def emit(self, nc, blk, sems):
        by_eng = {}
        for o in self.ops:
            by_eng.setdefault(o.eng, []).append(o)
        for eng in ENGINES:
            if eng not in by_eng:
                continue
            ops = by_eng[eng]

            def body(e, ops=ops, eng=eng):
                for o in ops:
                    for k, v in o.waits:
                        e.wait_ge(sems[k], v)
                    inst = o.fn(e)
                    if o.dma:
                        inst.then_inc(sems[o.dma_semkey], 16)
                    elif o.needs_inc:
                        inst.then_inc(sems[("eng", eng)], 1)
                if eng == "sync":
                    for k, c in sorted(self.final_counts.items(), key=lambda kv: str(kv[0])):
                        if k[0] == "dma":
                            e.wait_ge(sems[k], c)
            getattr(blk, eng)(body)


def host_tables(core_id):
    p0 = core_id * R
    pos = np.arange(p0 - HL, p0 + R, dtype=np.float64)
    pos = np.where(pos < 0, 0, pos).astype(np.float32)
    inv = (1.0 / (np.float32(10000.0) ** (np.arange(0, HD, 2, dtype=np.float32) / np.float32(HD)))).astype(np.float32)
    fr = (pos[None, :] * inv[:, None]).astype(np.float32)
    cos = np.cos(fr).astype(np.float32)
    sin = np.sin(fr).astype(np.float32)
    cosF = np.empty((HD, T1), np.float32)
    sinF = np.empty((HD, T1), np.float32)
    cosF[0::2] = cos; cosF[1::2] = cos
    sinF[0::2] = -sin; sinF[1::2] = sin
    s = np.float32(HD ** (-0.5))
    return np.concatenate([cosF, sinF, cosF * s, sinF * s], axis=1).astype(BF)


def host_masks(core_id):
    j = np.arange(128)[:, None]
    i = np.arange(WIN)[None, :]
    tiles = []
    for jc in range(8):
        jl = jc * 128 + j
        valid = (jl > i) & (jl <= i + WIN)
        tiles.append(np.where(valid, 0.0, MASKV).astype(np.float32))
    main = np.concatenate(tiles, axis=1)
    if core_id == 0:
        b0 = np.full((128, 4 * WIN), MASKV, np.float32)
    else:
        b0 = np.concatenate(tiles[:4], axis=1)
    return main.astype(BF), b0.astype(BF)


def host_ident():
    ident = np.eye(128, dtype=np.float32).astype(BF)
    sw = np.arange(128)
    sw = (sw // 2) * 2 + (1 - sw % 2)
    pswap = np.eye(128, dtype=np.float32)[sw].astype(BF)
    return ident, pswap


def host_consts():
    c = np.zeros((128, 8), np.float32)
    c[:, :len(CONSTS)] = np.array(CONSTS, np.float32)[None, :]
    return c


def build_program(phases=("wprep", "attn", "ffn1", "attnb", "ffn2"), debug=()):
    nc = bass.Bass("TRN2", target_bir_lowering=False)
    sch = Sched()

    def din(name, shape, dtype=F32):
        return nc.dram_tensor(name, list(shape), dtype, kind="ExternalInput").ap()

    x_in = din("x_in", [T1, D])
    aWq = din("aWq", [D, D]); aWk = din("aWk", [D, D])
    aWv = din("aWv", [D, D]); aWo = din("aWo", [D, D])
    bWqkv = din("bWqkv", [3 * D, D]); bWo = din("bWo", [D, D])
    W1 = din("W1", [DH, D]); W2 = din("W2", [D, DH])
    b1v = din("b1v", [DH]); b2v = din("b2v", [D])
    lnw = din("lnw", [D]); lnb = din("lnb", [D])
    tabs = din("tabs", [128, 4 * T1], BF16)
    maskM = din("maskM", [128, 8 * WIN], BF16)
    maskB0 = din("maskB0", [128, 4 * WIN], BF16)
    identI = din("identI", [128, 128], BF16)
    pswapI = din("pswapI", [128, 128], BF16)
    constsI = din("constsI", [128, 8])

    out = nc.dram_tensor("out", [R, D], F32, kind="ExternalOutput").ap()

    dbg_tensors = {}

    def dbg(name, ap, rg, shape, dtype):
        if name in debug:
            t = nc.dram_tensor(name, list(shape), dtype, kind="ExternalOutput").ap()
            dbg_tensors[name] = t
            sch.dma("sync", t, ap, reads=[rg], tag="dump_" + name)

    def scr(name, shape, dtype=BF16):
        kind = "ExternalOutput" if name in debug else "Internal"
        return nc.dram_tensor(name, list(shape), dtype, kind=kind).ap()

    sc_qW1T = scr("sc_qW1T", [D, DH]); sc_qW2T = scr("sc_qW2T", [DH, D])
    sc_X = scr("sc_X", [R, D], F32)

    ctx = ExitStack()
    with ctx:
        def sb(name, shape, dtype=BF16, stack=None):
            return (stack or ctx).enter_context(nc.sbuf_tensor(name, shape, dtype))

        pb = [ctx.enter_context(nc.psum_tensor(f"pb{i}", [128, 512], F32)) for i in range(8)]
        PB = [f"pb{i}" for i in range(8)]
        strip0 = sb("strip0", [128, 16 * 512], BF16)
        strip1 = sb("strip1", [128, 16 * 512], BF16)
        sconsts = sb("sconsts", [128, 8], F32)
        ident = sb("ident", [128, 128], BF16)
        pswap = sb("pswap", [128, 128], BF16)
        ones_r = sb("ones_r", [1, 128], BF16)
        ones_rf = sb("ones_rf", [1, 128], F32)
        ones_c = sb("ones_c", [128, 1], BF16)
        ln2stats = sb("ln2stats", [128, 2 * NTO], F32)

        for ci, cv in enumerate(CONSTS):
            nc.const_aps.register(F32, cv, sconsts[:, ci:ci + 1])

        sch.dma("sync", sconsts[:], constsI, writes=["consts"], tag="ld_consts")
        sch.dma("sync", ident[:], identI, writes=["ident"], tag="ld_ident")
        sch.dma("sync", pswap[:], pswapI, writes=["pswap"], tag="ld_pswap")
        sch.add("vector", lambda e: e.memset(ones_r[:], 1.0), writes=["ones_r"])
        sch.add("vector", lambda e: e.memset(ones_rf[:], 1.0), writes=["ones_rf"])
        sch.add("vector", lambda e: e.memset(ones_c[:], 1.0), writes=["ones_c"])

        def quant_chunk(src_ap, src_rg, F, qt, tag):
            G = F // 32
            (amax, amaxr), (ei, eir), (rcp, rcpr), (scl, sclr), (i8t, i8r), (qo, qor) = (
                qt["amax"], qt["ei"], qt["recip"], qt["scale"], qt["i8"], qt["q"])
            sch.add("vector", lambda e: e.tensor_reduce(
                amax[:, 0:G].unsqueeze(2), src_ap.rearrange("p (g b) -> p g b", b=32), AX.X, ALU.max,
                apply_absolute_value=True), reads=[src_rg], writes=[amaxr], tag=tag + "am")
            sch.add("vector", lambda e: e.tensor_single_scalar(
                ei[:, 0:G], amax[:, 0:G].bitcast(I32), 23, ALU.logical_shift_right),
                reads=[amaxr], writes=[eir], tag=tag + "e")
            sch.add("vector", lambda e: e.tensor_scalar(
                rcp[:, 0:G].bitcast(I32), ei[:, 0:G], -1, 260, ALU.mult, ALU.add),
                reads=[eir], writes=[rcpr], tag=tag + "r1")
            sch.add("vector", lambda e: e.tensor_single_scalar(
                rcp[:, 0:G].bitcast(I32), rcp[:, 0:G].bitcast(I32), 23, ALU.logical_shift_left),
                reads=[rcpr], writes=[rcpr], tag=tag + "r2")
            sch.add("vector", lambda e: e.tensor_scalar(
                scl[:, 0:G].bitcast(I32), ei[:, 0:G], 1, -6, ALU.mult, ALU.add),
                reads=[eir], writes=[sclr], tag=tag + "s1")
            sch.add("vector", lambda e: e.tensor_single_scalar(
                scl[:, 0:G].bitcast(I32), scl[:, 0:G].bitcast(I32), 23, ALU.logical_shift_left),
                reads=[sclr], writes=[sclr], tag=tag + "s2")
            rB = rcp[:, 0:G].unsqueeze(2).broadcast_to([128, G, 32])
            sB = scl[:, 0:G].unsqueeze(2).broadcast_to([128, G, 32])
            sch.add("vector", lambda e: e.tensor_tensor(
                i8t[:, 0:F].rearrange("p (g b) -> p g b", b=32),
                src_ap.rearrange("p (g b) -> p g b", b=32), rB, ALU.mult),
                reads=[src_rg, rcpr], writes=[i8r], tag=tag + "i8", fsz=F)
            sch.add("vector", lambda e: e.tensor_tensor(
                qo[:, 0:F].rearrange("p (g b) -> p g b", b=32),
                i8t[:, 0:F].rearrange("p (g b) -> p g b", b=32), sB, ALU.mult),
                reads=[i8r, sclr], writes=[qor], tag=tag + "q", fsz=F)

        def make_qt(pre, Fmax, stack, share_i8=None):
            G = Fmax // 32
            return dict(
                amax=(sb(pre + "amax", [128, G], F32, stack), pre + "amax"),
                ei=(sb(pre + "ei", [128, G], I32, stack), pre + "ei"),
                recip=(sb(pre + "recip", [128, G], F32, stack), pre + "recip"),
                scale=(sb(pre + "scale", [128, G], F32, stack), pre + "scale"),
                i8=(share_i8 if share_i8 is not None else (sb(pre + "i8", [128, Fmax], I8, stack), pre + "i8")),
                q=(sb(pre + "q", [128, Fmax], BF16, stack), pre + "q"),
            )

        def transpose_pack(src_ap, src_rg, chunk_cols, banks, evac, tag, perm=None):
            pmat, pmn = (perm, "pswap") if perm is not None else (ident, "ident")
            nb = len(banks)
            for gi in range(0, len(chunk_cols), 8):
                n = min(8, len(chunk_cols) - gi)
                bsel = banks[(gi // 8) % nb]
                brg = PB[bsel]
                for j in range(n):
                    col = chunk_cols[gi + j]
                    sch.add("tensor", (lambda bsel=bsel, j=j, col=col, pmat=pmat: (lambda e: e.transpose(
                        pb[bsel][:].bitcast(BF16)[:, j * 128:(j + 1) * 128],
                        src_ap[:, col:col + 128], pmat[:])))(),
                        reads=[src_rg, pmn], writes=[brg], tag=tag + f"T{gi + j}")
                evac(bsel, gi, n, brg)

        def ln_chunk(xap, xrg, stats, stats_rg, scol, sqscr, sqrg, tag, save_to=None):
            rN = 1.0 / D
            sch.add("vector", lambda e: e.tensor_reduce(
                stats[:, scol + 2:scol + 3], xap, AX.X, ALU.add),
                reads=[xrg], writes=[stats_rg], tag=tag + "sm")
            sch.add("scalar", lambda e: e.activation(sqscr, xap, AF.Square, bias=0.0, scale=1.0,
                                                     accum_out=stats[:, scol + 3:scol + 4]),
                    reads=[xrg, "consts"], writes=[stats_rg, sqrg], tag=tag + "sq")
            sch.add("vector", lambda e: e.tensor_single_scalar(
                stats[:, scol:scol + 1], stats[:, scol + 2:scol + 3], rN, ALU.mult),
                reads=[stats_rg], writes=[stats_rg], tag=tag + "mn")
            sch.add("vector", lambda e: e.tensor_tensor(
                stats[:, scol + 4:scol + 5], stats[:, scol:scol + 1], stats[:, scol:scol + 1], ALU.mult),
                reads=[stats_rg], writes=[stats_rg], tag=tag + "m2")
            sch.add("vector", lambda e: e.tensor_single_scalar(
                stats[:, scol + 5:scol + 6], stats[:, scol + 3:scol + 4], rN, ALU.mult),
                reads=[stats_rg], writes=[stats_rg], tag=tag + "v1")
            sch.add("vector", lambda e: e.tensor_tensor(
                stats[:, scol + 1:scol + 2], stats[:, scol + 5:scol + 6], stats[:, scol + 4:scol + 5],
                ALU.subtract),
                reads=[stats_rg], writes=[stats_rg], tag=tag + "vr")
            if save_to is not None:
                sv, svrg, svcol = save_to
                sch.add("vector", lambda e: e.tensor_copy(sv[:, svcol:svcol + 1], stats[:, scol:scol + 1]),
                        reads=[stats_rg], writes=[svrg], tag=tag + "sv")
            sch.add("scalar", lambda e: e.activation(stats[:, scol + 1:scol + 2], stats[:, scol + 1:scol + 2],
                                                     AF.Ln, bias=EPS, scale=1.0),
                    reads=[stats_rg, "consts"], writes=[stats_rg], tag=tag + "ln")
            sch.add("scalar", lambda e: e.activation(stats[:, scol + 1:scol + 2], stats[:, scol + 1:scol + 2],
                                                     AF.Exp, bias=0.0, scale=-0.5),
                    reads=[stats_rg, "consts"], writes=[stats_rg], tag=tag + "ex")
            if save_to is not None:
                sv, svrg, svcol = save_to
                sch.add("vector", lambda e: e.tensor_copy(sv[:, svcol + 1:svcol + 2], stats[:, scol + 1:scol + 2]),
                        reads=[stats_rg], writes=[svrg], tag=tag + "sv2")

        def ln_apply(xap, xrg, stats, stats_rg, scol, outap, outrg, tag):
            sch.add("vector", lambda e: e.tensor_scalar(
                outap, xap, stats[:, scol:scol + 1], stats[:, scol + 1:scol + 2],
                ALU.subtract, ALU.mult), reads=[xrg, stats_rg], writes=[outrg], tag=tag + "ap", fsz=D)

        def mk_wstrip(wbuf, wrg, w_dram, grow, tbanks, wcb, wcbpre, tag):
            for rc in range(4):
                wt = wcb[rc % 2]; wtrg = wcbpre + str(rc % 2)
                sch.dma("gpsimd", wt[:],
                        w_dram[grow * 512 + rc * 128: grow * 512 + (rc + 1) * 128, :],
                        writes=[wtrg], tag=f"{tag}ld{rc}")
                for gi in range(2):
                    bsel = tbanks[gi]
                    for j in range(8):
                        ic = gi * 8 + j
                        sch.add("tensor", (lambda bsel=bsel, j=j, ic=ic, wt=wt: (lambda e: e.transpose(
                            pb[bsel][:].bitcast(BF16)[:, j * 128:(j + 1) * 128],
                            wt[:, ic * 128:(ic + 1) * 128], ident[:])))(),
                            reads=[wtrg, "ident"], writes=[PB[bsel]], tag=f"{tag}T{rc}_{ic}")
                    sch.add("vector", (lambda bsel=bsel, gi=gi, rc=rc, wbuf=wbuf: (lambda e: e.tensor_copy(
                        wbuf[:].rearrange("p (kc o) -> p kc o", o=512)[:, gi * 8:gi * 8 + 8, rc * 128:(rc + 1) * 128],
                        pb[bsel][:].bitcast(BF16)[:, 0:1024].rearrange("p (c q) -> p c q", q=128))))(),
                        reads=[PB[bsel]], writes=[wrg], tag=f"{tag}e{rc}_{gi}", fsz=1024)

        if "wprep" in phases:
            wp = ExitStack()
            with wp:
                NWR = 4
                wbig = [sb(f"wbig{i}", [128, 2048], F32, wp) for i in range(NWR)]
                qtw = [make_qt(f"wq{i}_", 2048, wp) for i in range(NWR)]
                wtev = [sb(f"wtev{i}", [128, 1024], BF16, wp) for i in range(2 * NWR)]
                _wn = [0]

                def prep_weight(src_ap, dest_scT, o_chunks, k_cols, do_quant, tag, screg, dest_scT2=None,
                                screg2=None):
                    npc = k_cols // 2048
                    dview = dest_scT.rearrange("(kc p) o -> p kc o", p=128)
                    dview2 = dest_scT2.rearrange("(kc p) o -> p kc o", p=128) if dest_scT2 is not None else None
                    def mk_evac(oc, p, dv, sfx, sreg, par, tag):
                        def evac(bsel, gi, n, brg):
                            ev = wtev[par + (NWR if sfx == "p" else 0)]
                            evrg = f"wtev{par + (NWR if sfx == 'p' else 0)}"
                            sch.add("vector", (lambda ev=ev, bsel=bsel, n=n: (lambda e: e.tensor_copy(
                                ev[:, 0:n * 128], pb[bsel][:].bitcast(BF16)[:, 0:n * 128])))(),
                                reads=[brg], writes=[evrg], tag=f"{tag}tev{sfx}{oc}_{p}_{gi}", fsz=n * 128)
                            sch.dma("gpsimd",
                                    dv[:, p * 16 + gi:p * 16 + gi + n, oc * 128:(oc + 1) * 128],
                                    ev[:, 0:n * 128].rearrange("p (c q) -> p c q", q=128),
                                    reads=[evrg], writes=[f"{sreg}#{oc % 4}"], tag=f"{tag}st{sfx}{oc}_{p}_{gi}")
                        return evac

                    def piece_chain(oc, p):
                        par = _wn[0] % NWR
                        _wn[0] += 1
                        w = wbig[par]; wrg = f"wbig{par}"
                        qtc = qtw[par]
                        sch.begin_collect()
                        sch.dma("sync", w[:], src_ap[oc * 128:(oc + 1) * 128, p * 2048:(p + 1) * 2048],
                                writes=[wrg], tag=f"{tag}ld{oc}_{p}")
                        if do_quant:
                            quant_chunk(w[:], wrg, 2048, qtc, f"{tag}qz{oc}_{p}")
                        else:
                            sch.add("vector", (lambda w=w, qtc=qtc: (lambda e: e.tensor_copy(
                                qtc["q"][0][:, 0:2048], w[:])))(),
                                reads=[wrg], writes=[qtc["q"][1]], tag=f"{tag}cast{oc}_{p}", fsz=2048)
                        chain = sch.end_collect()

                        def do_T(oc=oc, p=p, qtc=qtc, par=par):
                            cols = [c * 128 for c in range(16)]
                            transpose_pack(qtc["q"][0], qtc["q"][1], cols, [4, 5],
                                           mk_evac(oc, p, dview, "n", screg, par, tag), f"{tag}t{oc}_{p}_")
                        chain.append(do_T)
                        return chain

                    pieces = [(oc, p) for oc in range(o_chunks) for p in range(npc)]
                    for i in range(0, len(pieces), NWR):
                        chains = [piece_chain(*pc) for pc in pieces[i:i + NWR]]
                        for st in zip_longest(*chains):
                            for t in st:
                                if t is not None:
                                    t()

                prep_weight(W1, sc_qW1T, KCH, D, True, "pw1", "scqW1T")
                prep_weight(W2, sc_qW2T, KC, DH, True, "pw2", "scqW2T")
            sch.fence()

        if "attn" in phases:
            ap_ = ExitStack()
            with ap_:
                xln_fm = sb("xln_fm", [128, KC * T1], BF16, ap_)
                attnout = sb("attnout", [128, NH * R], BF16, ap_)

                a1 = ExitStack()
                with a1:
                    xc = [sb(f"a_xc{i}", [128, D], F32, a1) for i in range(2)]
                    xlntm = [sb(f"a_xt{i}", [128, D], BF16, a1) for i in range(2)]
                    lstats = sb("a_ls", [128, 16], F32, a1)
                    sqs1 = sb("a_sq", [128, D], F32, a1)

                    def a1_chain(tc):
                        xt = xc[tc % 2]; xtrg = f"a_xc{tc % 2}"
                        lt = xlntm[tc % 2]; ltrg = f"a_xt{tc % 2}"
                        scol = (tc % 2) * 8
                        sch.begin_collect()
                        sch.dma("sync", xt[:], x_in[tc * 128:(tc + 1) * 128, :], writes=[xtrg], tag=f"a_lx{tc}")
                        ln_chunk(xt[:, 0:D], xtrg, lstats, f"a_ls{tc % 2}", scol, sqs1[:], "a_sq", f"a_l{tc}")
                        ln_apply(xt[:, 0:D], xtrg, lstats, f"a_ls{tc % 2}", scol, lt[:, 0:D], ltrg, f"a_l{tc}")
                        chain = sch.end_collect()

                        def do_T(tc=tc, lt=lt, ltrg=ltrg):
                            def evx(bsel, gi, n, brg):
                                sch.add("vector", (lambda bsel=bsel, gi=gi, n=n: (lambda e: e.tensor_copy(
                                    xln_fm.rearrange("p (c t) -> p c t", t=T1)[:, gi:gi + n, tc * 128:(tc + 1) * 128],
                                    pb[bsel][:].bitcast(BF16)[:, 0:n * 128].rearrange("p (c q) -> p c q", q=128))))(),
                                    reads=[brg], writes=["xln_fm"], tag=f"a_xe{tc}_{gi}")
                            transpose_pack(lt, ltrg, [c * 128 for c in range(KC)], [4, 5], evx, f"a_xT{tc}_")
                        chain.append(do_T)
                        return chain

                    for pi in range(0, NTC, 2):
                        chains = [a1_chain(pi), a1_chain(pi + 1)]
                        for st in zip_longest(*chains):
                            for t in st:
                                if t is not None:
                                    t()
                sch.fence()
                dbg("dbg_xln", xln_fm[:], "xln_fm", [128, KC * T1], BF16)

                pass

                a2 = ExitStack()
                with a2:
                    tabsb = sb("tabsb", [128, 4 * T1], BF16, a2)
                    maskm = sb("maskm", [128, 8 * WIN], BF16, a2)
                    maskb0 = sb("maskb0", [128, 4 * WIN], BF16, a2)
                    sch.dma("sync", tabsb[:], tabs, writes=["tabsb"], tag="ld_tabs")
                    sch.dma("sync", maskm[:], maskM, writes=["maskm"], tag="ld_maskm")
                    sch.dma("sync", maskb0[:], maskB0, writes=["maskb0"], tag="ld_maskb0")
                    q_fm = sb("q_fm", [128, 4 * R], BF16, a2)
                    k_fm = sb("k_fm", [128, 4 * T1], BF16, a2)
                    v_tm = sb("v_tm", [128, NTC * 512], BF16, a2)
                    p_sbr = [sb(f"p_sb{i}", [128, 8 * WIN], BF16, a2) for i in range(2)]
                    rt = [sb(f"a_rt{i}", [128, 512], BF16, a2) for i in range(4)]
                    ssumr = [sb(f"a_ss{i}", [1, 512], F32, a2) for i in range(2)]
                    srecipr = [sb(f"a_sr{i}", [1, 512], F32, a2) for i in range(2)]
                    recipBr = [sb(f"a_rB{i}", [128, 512], F32, a2) for i in range(2)]
                    wcb2 = [strip1[:, 0:D], strip1[:, D:2 * D]]

                    SWAPM = []
                    for i in range(0, 32, 2):
                        SWAPM += [i + 1, i]
                    for g in range(4):
                        for (wsrc, dst, dstrg, nt, toff, ctab, stab) in (
                            (aWq, q_fm, "q_fm", 2, HL, 2 * T1, 3 * T1),
                            (aWk, k_fm, "k_fm", 3, 0, 0, T1),
                        ):
                            mk_wstrip(strip0, "strip0", wsrc, g, (3, 4), wcb2, "a_wc", f"a_w{dstrg}{g}")
                            for ho in range(4):
                                for tt in range(nt):
                                    t0 = toff + tt * 512
                                    bank = (ho * nt + tt) % 2
                                    for kc2 in range(KC):
                                        sch.add("tensor", (lambda bank=bank, kc2=kc2, ho=ho, t0=t0: (lambda e: e.matmul(
                                            pb[bank][:],
                                            strip0[:, kc2 * 512 + ho * 128: kc2 * 512 + (ho + 1) * 128],
                                            xln_fm[:, kc2 * T1 + t0: kc2 * T1 + t0 + 512],
                                            start=(kc2 == 0), stop=(kc2 == KC - 1))))(),
                                            reads=(["strip0", "xln_fm"] + ([PB[bank]] if kc2 else [])),
                                            writes=[PB[bank]], tag=f"a_p{dstrg}{g}_{ho}_{tt}_{kc2}")
                                    ctt = ctab + t0
                                    stt_ = stab + t0
                                    dcol = ho * (R if dstrg == "q_fm" else T1) + tt * 512
                                    pr = ((ho * nt + tt) % 2) * 2
                                    r0, r1_ = rt[pr], rt[pr + 1]
                                    r0n, r1n = f"a_rt{pr}", f"a_rt{pr + 1}"
                                    sch.add("vector", (lambda bank=bank, r0=r0: (lambda e: e.tensor_copy(
                                        r0[:], pb[bank][:])))(),
                                        reads=[PB[bank]], writes=[r0n], tag=f"a_rv{dstrg}{g}_{ho}_{tt}")
                                    sch.add("vector", (lambda r0=r0, r1_=r1_: (lambda e: e.stream_shuffle(
                                        r1_[:], r0[:], SWAPM)))(),
                                        reads=[r0n], writes=[r1n], tag=f"a_rs{dstrg}{g}_{ho}_{tt}")
                                    sch.add("vector", (lambda ctt=ctt, r0=r0: (lambda e: e.tensor_tensor(
                                        r0[:], r0[:], tabsb[:, ctt:ctt + 512], ALU.mult)))(),
                                        reads=[r0n, "tabsb"], writes=[r0n], tag=f"a_r1{dstrg}{g}_{ho}_{tt}")
                                    sch.add("vector", (lambda stt_=stt_, r1_=r1_: (lambda e: e.tensor_tensor(
                                        r1_[:], r1_[:], tabsb[:, stt_:stt_ + 512], ALU.mult)))(),
                                        reads=[r1n, "tabsb"], writes=[r1n], tag=f"a_r2{dstrg}{g}_{ho}_{tt}")
                                    sch.add("vector", (lambda dst=dst, dcol=dcol, r0=r0, r1_=r1_: (lambda e: e.tensor_tensor(
                                        dst[:, dcol:dcol + 512], r0[:], r1_[:], ALU.add)))(),
                                        reads=[r0n, r1n], writes=[dstrg], tag=f"a_r3{dstrg}{g}_{ho}_{tt}")

                        mk_wstrip(strip0, "strip0", aWv, g, (3, 4), wcb2, "a_wc", f"a_wv{g}")
                        for tc in range(NTC):
                            for kc2 in range(KC):
                                sch.add("tensor", (lambda kc2=kc2, tc=tc: (lambda e: e.matmul(
                                    pb[2][:],
                                    xln_fm[:, kc2 * T1 + tc * 128: kc2 * T1 + (tc + 1) * 128],
                                    strip0[:, kc2 * 512:(kc2 + 1) * 512],
                                    start=(kc2 == 0), stop=(kc2 == KC - 1))))(),
                                    reads=(["strip0", "xln_fm"] + ([PB[2]] if kc2 else [])),
                                    writes=[PB[2]], tag=f"a_pv{g}_{tc}_{kc2}")
                            sch.add("vector", (lambda tc=tc: (lambda e: e.tensor_copy(
                                v_tm[:, tc * 512:(tc + 1) * 512], pb[2][:])))(),
                                reads=[PB[2]], writes=["v_tm"], tag=f"a_ve{g}_{tc}")

                        if g == 0:
                            dbg("dbg_q", q_fm[:], "q_fm", [128, 4 * R], BF16)
                            dbg("dbg_k", k_fm[:], "k_fm", [128, 4 * T1], BF16)
                            dbg("dbg_v", v_tm[:], "v_tm", [128, NTC * 512], BF16)
                        def u_stage1(ho, b):
                            h = g * 4 + ho
                            up = b
                            p_sb = p_sbr[up]; p_rg = f"p_sb{up}"
                            for jc in range(8):
                                bank = 3 + (jc % 2)
                                kcol = ho * T1 + b * 512 + jc * 128
                                qcol = ho * R + b * 512
                                sch.add("tensor", (lambda bank=bank, kcol=kcol, qcol=qcol: (lambda e: e.matmul(
                                    pb[bank][:], k_fm[:, kcol:kcol + 128], q_fm[:, qcol:qcol + 512],
                                    start=True, stop=False)))(),
                                    reads=["k_fm", "q_fm"], writes=[PB[bank]], tag=f"a_s{h}_{b}_{jc}")
                                if b == 0 and jc < 4:
                                    mt, mtrg, mcol = maskb0, "maskb0", jc * 512
                                else:
                                    mt, mtrg, mcol = maskm, "maskm", jc * 512
                                sch.add("tensor", (lambda bank=bank, mt=mt, mcol=mcol: (lambda e: e.matmul(
                                    pb[bank][:], ident[:], mt[:, mcol:mcol + 512],
                                    start=False, stop=True)))(),
                                    reads=[mtrg, "ident", PB[bank]], writes=[PB[bank]], tag=f"a_m{h}_{b}_{jc}")
                                sch.add("scalar", (lambda bank=bank, jc=jc, p_sb=p_sb: (lambda e: e.activation(
                                    p_sb[:, jc * 512:(jc + 1) * 512], pb[bank][:], AF.Exp,
                                    bias=CEXP, scale=1.0)))(),
                                    reads=[PB[bank], "consts"], writes=[p_rg], tag=f"a_e{h}_{b}_{jc}")

                        def u_sums(ho, b, su_b):
                            h = g * 4 + ho
                            up = b
                            p_sb = p_sbr[up]; p_rg = f"p_sb{up}"
                            ssum = ssumr[up]; ss_rg = f"a_ss{up}"
                            for jc in range(8):
                                sch.add("tensor", (lambda jc=jc, p_sb=p_sb, su_b=su_b: (lambda e: e.matmul(
                                    pb[su_b][0:1, :], ones_c[:], p_sb[:, jc * 512:(jc + 1) * 512],
                                    start=(jc == 0), stop=(jc == 7))))(),
                                    reads=(["ones_c", p_rg] + ([PB[su_b]] if jc else [])),
                                    writes=[PB[su_b]], tag=f"a_su{h}_{b}_{jc}")
                            sch.add("vector", (lambda ssum=ssum, su_b=su_b: (lambda e: e.tensor_copy(
                                ssum[:], pb[su_b][0:1, :])))(),
                                reads=[PB[su_b]], writes=[ss_rg], tag=f"a_se{h}_{b}")
                            sch.add("scalar", (lambda ssum=ssum, srecip=srecipr[up]: (lambda e: e.activation(
                                srecip[:], ssum[:], AF.Ln, bias=0.0, scale=1.0)))(),
                                reads=[ss_rg, "consts"], writes=[f"a_sr{up}"], tag=f"a_rl{h}_{b}")
                            sch.add("scalar", (lambda srecip=srecipr[up]: (lambda e: e.activation(
                                srecip[:], srecip[:], AF.Exp, bias=0.0, scale=-1.0)))(),
                                reads=[f"a_sr{up}", "consts"], writes=[f"a_sr{up}"], tag=f"a_re{h}_{b}")

                        def u_stage2(ho, b, rb_b, av_b):
                            h = g * 4 + ho
                            up = b
                            p_sb = p_sbr[up]; p_rg = f"p_sb{up}"
                            ssum = ssumr[up]; ss_rg = f"a_ss{up}"
                            srecip = srecipr[up]; sr_rg = f"a_sr{up}"
                            recipB = recipBr[up]; rB_rg = f"a_rB{up}"
                            sch.add("tensor", (lambda srecip=srecip, rb_b=rb_b: (lambda e: e.matmul(
                                pb[rb_b][:], ones_rf[:], srecip[:], start=True, stop=True)))(),
                                reads=[sr_rg, "ones_rf"], writes=[PB[rb_b]], tag=f"a_rb{h}_{b}")
                            sch.add("vector", (lambda recipB=recipB, rb_b=rb_b: (lambda e: e.tensor_copy(
                                recipB[:], pb[rb_b][:])))(),
                                reads=[PB[rb_b]], writes=[rB_rg], tag=f"a_rbe{h}_{b}")
                            for jc in range(8):
                                vcol = (4 * b + jc) * 512 + ho * 128
                                sch.add("tensor", (lambda jc=jc, vcol=vcol, p_sb=p_sb, av_b=av_b: (lambda e: e.matmul(
                                    pb[av_b][:], v_tm[:, vcol:vcol + 128], p_sb[:, jc * 512:(jc + 1) * 512],
                                    start=(jc == 0), stop=(jc == 7))))(),
                                    reads=(["v_tm", p_rg] + ([PB[av_b]] if jc else [])),
                                    writes=[PB[av_b]], tag=f"a_av{h}_{b}_{jc}")
                            acol = h * R + b * 512
                            sch.add("vector", (lambda acol=acol, recipB=recipB, av_b=av_b: (lambda e: e.tensor_tensor(
                                attnout[:, acol:acol + 512], pb[av_b][:], recipB[:], ALU.mult)))(),
                                reads=[PB[av_b], rB_rg], writes=["attnout"], tag=f"a_ao{h}_{b}")
                            if h == 0 and b == 1:
                                dbg("dbg_p", p_sb[:], p_rg, [128, 8 * WIN], BF16)
                                dbg("dbg_sum", ssum[:], ss_rg, [1, 512], F32)
                                dbg("dbg_recipB", recipB[:], rB_rg, [128, 512], F32)

                        for ho in range(4):
                            u_stage1(ho, 0)
                            u_stage1(ho, 1)
                            u_sums(ho, 0, 5)
                            u_sums(ho, 1, 0)
                            u_stage2(ho, 0, 6, 7)
                            u_stage2(ho, 1, 1, 2)
                sch.fence()
                dbg("dbg_attnout", attnout[:], "attnout", [128, NH * R], BF16)

                a3 = ExitStack()
                with a3:
                    xres = [sb(f"a_xr{i}", [128, D], F32, a3) for i in range(NTO)]
                    wcb3 = [sb(f"a_w3c{i}", [128, D], BF16, a3) for i in range(2)]
                    for tch in range(NTO):
                        sch.dma("sync", xres[tch][:], x_in[HL + tch * 128: HL + (tch + 1) * 128, :],
                                writes=[f"a_xr{tch}"], tag=f"a_lr{tch}")
                    for ot in range(4):
                        wbuf = strip0 if ot % 2 == 0 else strip1
                        wrg = "strip0" if ot % 2 == 0 else "strip1"
                        mk_wstrip(wbuf, wrg, aWo, ot, (6, 7), wcb3, "a_w3c", f"a_wo{ot}")
                        for tch in range(NTO):
                            bank = tch % 4
                            for ic in range(KC):
                                sch.add("tensor", (lambda bank=bank, ic=ic, tch=tch, wbuf=wbuf: (lambda e: e.matmul(
                                    pb[bank][:], attnout[:, ic * R + tch * 128: ic * R + (tch + 1) * 128],
                                    wbuf[:, ic * 512:(ic + 1) * 512],
                                    start=(ic == 0), stop=(ic == KC - 1))))(),
                                    reads=(["attnout", wrg] + ([PB[bank]] if ic else [])),
                                    writes=[PB[bank]], tag=f"a_o{ot}_{tch}_{ic}")
                            sch.add("vector", (lambda bank=bank, tch=tch, ot=ot: (lambda e: e.tensor_tensor(
                                xres[tch][:, ot * 512:(ot + 1) * 512], pb[bank][:],
                                xres[tch][:, ot * 512:(ot + 1) * 512], ALU.add)))(),
                                reads=[PB[bank], f"a_xr{tch}"], writes=[f"a_xr{tch}"], tag=f"a_or{ot}_{tch}")
                    for tch in range(NTO):
                        sch.dma("gpsimd", sc_X[tch * 128:(tch + 1) * 128, :], xres[tch][:],
                                reads=[f"a_xr{tch}"], writes=[f"scX{tch}_{q}" for q in range(4)],
                                tag=f"a_sx{tch}")
            sch.fence()

        def ffn_phase(idx, x_src, final):
            fp_ = ExitStack()
            with fp_:
                pre = f"f{idx}"
                gamB = sb(pre + "gamB", [128, D], BF16, fp_)
                betB = sb(pre + "betB", [128, D], BF16, fp_)
                b1t = [sb(pre + f"b1t{i}", [1, 512], BF16, fp_) for i in range(2)]
                b2t = [sb(pre + f"b2t{i}", [1, 512], BF16, fp_) for i in range(2)]
                xc2 = [sb(pre + f"xc{i}", [128, D], F32, fp_) for i in range(2)]
                xb2 = [sb(pre + f"xb{i}", [128, D], BF16, fp_) for i in range(2)]
                t0b = sb(pre + "t0", [128, D], F32, fp_)
                lstat = sb(pre + "ls", [128, 16], F32, fp_)
                lnq_fm = sb(pre + "lnq", [128, KC * 512], BF16, fp_)
                hq_fm = sb(pre + "hq", [128, KCH * 512], BF16, fp_)
                hg = [sb(pre + f"hg{i}", [128, 4 * 512], BF16, fp_) for i in range(2)]
                res = [sb(pre + f"res{i}", [128, D], BF16, fp_) for i in range(4)]
                xs2 = [sb(pre + f"xs{i}", [128, 512], F32, fp_) for i in range(2)]
                f2r = [sb(pre + f"f2{i}", [128, 512], BF16, fp_) for i in range(4)]
                qt2 = [make_qt(pre + "q0", D, fp_), make_qt(pre + "q1", D, fp_)]

                rw_ = ExitStack()
                with rw_:
                    rows = [sb(pre + f"lnr{i}", [1, 512], BF16, rw_) for i in range(2)]
                    for nm, srcv, dst, dstrg in (("g", lnw, gamB, pre + "gamB"),
                                                 ("b", lnb, betB, pre + "betB")):
                        for j in range(4):
                            row = rows[j % 2]; rowrg = pre + f"lnr{j % 2}"
                            sch.dma("gpsimd", row[:], srcv.rearrange("(a b) -> a b", a=1)[0:1, j * 512:(j + 1) * 512],
                                    writes=[rowrg], tag=f"{pre}ld{nm}{j}")
                            sch.add("tensor", (lambda row=row: (lambda e: e.matmul(
                                pb[0][:], ones_r[:], row[0:1, :], start=True, stop=True)))(),
                                reads=[rowrg, "ones_r"], writes=[PB[0]], tag=f"{pre}gb{nm}{j}")
                            sch.add("vector", (lambda dst=dst, j=j: (lambda e: e.tensor_copy(
                                dst[:, j * 512:(j + 1) * 512], pb[0][:])))(),
                                reads=[PB[0]], writes=[dstrg], tag=f"{pre}ge{nm}{j}")
                sch.fence()

                scT_w1 = sc_qW1T.rearrange("(kc p) o -> p kc o", p=128)
                scT_w2 = sc_qW2T.rearrange("(kc p) o -> p kc o", p=128)

                def ln_chain(half, tci, tbanks):
                    tch = half * 4 + tci
                    xt = xc2[tci % 2]; xtrg = pre + f"xc{tci % 2}"
                    xbt = xb2[tci % 2]; xbrg = pre + f"xb{tci % 2}"
                    scol = (tci % 2) * 8
                    lsrg = pre + f"ls{tci % 2}"
                    qtc = qt2[tci % 2]
                    sch.begin_collect()
                    sch.dma("sync", xt[:], x_src[tch * 128:(tch + 1) * 128, :], writes=[xtrg],
                            reads=[f"scX{tch}_{q}" for q in range(4)], tag=f"{pre}lx{tch}")
                    ln_chunk(xt[:, 0:D], xtrg, lstat, lsrg, scol, t0b[:], pre + "t0", f"{pre}l2{tch}",
                             save_to=(ln2stats, "ln2stats", tch * 2))
                    ln_apply(xt[:, 0:D], xtrg, lstat, lsrg, scol, xbt[:, 0:D], xbrg, f"{pre}l2{tch}")
                    ln_chunk(xbt[:, 0:D], xbrg, lstat, lsrg, scol, t0b[:], pre + "t0", f"{pre}l3{tch}")
                    sch.add("vector", (lambda xbt=xbt, xt=xt, scol=scol: (lambda e: e.tensor_scalar(
                        xt[:], xbt[:], lstat[:, scol:scol + 1], lstat[:, scol + 1:scol + 2],
                        ALU.subtract, ALU.mult)))(),
                        reads=[xbrg, lsrg], writes=[xtrg], tag=f"{pre}l3a{tch}")
                    sch.add("vector", (lambda xt=xt: (lambda e: e.tensor_tensor(
                        xt[:], xt[:], gamB[:], ALU.mult)))(),
                        reads=[xtrg, pre + "gamB"], writes=[xtrg], tag=f"{pre}l3b{tch}", fsz=D)
                    sch.add("vector", (lambda xt=xt, xbt=xbt: (lambda e: e.tensor_tensor(
                        xbt[:], xt[:], betB[:], ALU.add)))(),
                        reads=[xtrg, pre + "betB"], writes=[xbrg], tag=f"{pre}l3c{tch}", fsz=D)
                    quant_chunk(xbt[:, 0:D], xbrg, D, qtc, f"{pre}qz{tch}")
                    chain = sch.end_collect()

                    def do_T(tci=tci, tch=tch, qtc=qtc, tbanks=tbanks):
                        def ev2(bsel, gi, n, brg):
                            sch.add("vector", (lambda bsel=bsel, gi=gi, n=n: (lambda e: e.tensor_copy(
                                lnq_fm.rearrange("p (c t) -> p c t", t=512)[:, gi:gi + n, tci * 128:(tci + 1) * 128],
                                pb[bsel][:].bitcast(BF16)[:, 0:n * 128].rearrange("p (c q) -> p c q", q=128))))(),
                                reads=[brg], writes=[pre + f"lnq{tci}"], tag=f"{pre}lqe{tch}_{gi}", fsz=n * 128)
                        transpose_pack(qtc["q"][0], qtc["q"][1], [c * 128 for c in range(KC)], tbanks,
                                       ev2, f"{pre}lqT{tch}_")
                    chain.append(do_T)
                    return chain

                def ln_stage(half, tbanks=(4, 5)):
                    for pi in range(2):
                        chains = [ln_chain(half, pi * 2 + j, tbanks) for j in (0, 1)]
                        for st in zip_longest(*chains):
                            for t in st:
                                if t is not None:
                                    t()

                def w1_stage(half):
                    pend_T = []
                    for ot in range(16):
                        wbuf = strip0 if ot % 2 == 0 else strip1
                        wrg = "strip0" if ot % 2 == 0 else "strip1"
                        for hf in range(2):
                            sch.dma("sync" if ot % 2 == 0 else "gpsimd",
                                    wbuf[:, hf * 4096:(hf + 1) * 4096].rearrange("p (kc o) -> p kc o", o=512),
                                    scT_w1[:, hf * 8:(hf + 1) * 8, ot * 512:(ot + 1) * 512],
                                    reads=[f"scqW1T#{i}" for i in range(4)], writes=[wrg + "ab"[hf]],
                                    tag=f"{pre}w1l{half}_{ot}_{hf}")
                        hgb = hg[ot % 2]; hgrg = pre + f"hg{ot % 2}"
                        bt = b1t[ot % 2]; btrg = pre + f"b1t{ot % 2}"
                        sch.dma("gpsimd", bt[:], b1v.rearrange("(a b) -> a b", a=1)[0:1, ot * 512:(ot + 1) * 512],
                                writes=[btrg], tag=f"{pre}w1bl{half}_{ot}")
                        for tci in range(4):
                            bank = tci
                            for kc2 in range(KC):
                                sch.add("tensor", (lambda bank=bank, kc2=kc2, tci=tci, wbuf=wbuf: (lambda e: e.matmul(
                                    pb[bank][:], lnq_fm[:, kc2 * 512 + tci * 128: kc2 * 512 + (tci + 1) * 128],
                                    wbuf[:, kc2 * 512:(kc2 + 1) * 512],
                                    start=(kc2 == 0), stop=False)))(),
                                    reads=([pre + f"lnq{tci}", wrg + "ab"[kc2 // 8]] + ([PB[bank]] if kc2 else [])),
                                    writes=[PB[bank]], tag=f"{pre}w1m{half}_{ot}_{tci}_{kc2}")
                            sch.add("tensor", (lambda bank=bank, bt=bt: (lambda e: e.matmul(
                                pb[bank][:], ones_r[:], bt[:],
                                start=False, stop=True)))(),
                                reads=[btrg, "ones_r", PB[bank]], writes=[PB[bank]], tag=f"{pre}w1b{half}_{ot}_{tci}")
                            fb = f2r[tci % 2]; fbrg = pre + f"f2{tci % 2}"
                            sch.add("vector", (lambda bank=bank, fb=fb: (lambda e: e.tensor_copy(
                                fb[:], pb[bank][:])))(),
                                reads=[PB[bank]], writes=[fbrg], tag=f"{pre}w1e{half}_{ot}_{tci}")
                            sch.add("scalar", (lambda hgb=hgb, tci=tci, fb=fb: (lambda e: e.activation(
                                hgb[:, tci * 512:(tci + 1) * 512], fb[:], AF.Gelu_apprx_tanh,
                                bias=0.0, scale=1.0)))(),
                                reads=[fbrg, "consts"], writes=[hgrg], tag=f"{pre}w1g{half}_{ot}_{tci}")
                        qtc = qt2[ot % 2]
                        quant_chunk(hgb[:, 0:2048], hgrg, 2048, qtc, f"{pre}hz{half}_{ot}")

                        def hq_transposes(ot=ot, qtc=qtc):
                            for grp2 in range(2):
                                bsel = 4 + grp2
                                for oc_ in range(2):
                                    for tci_ in range(4):
                                        col = tci_ * 512 + (grp2 * 2 + oc_) * 128
                                        slot = oc_ * 4 + tci_
                                        sch.add("tensor", (lambda bsel=bsel, slot=slot, col=col, qtc=qtc: (lambda e: e.transpose(
                                            pb[bsel][:].bitcast(BF16)[:, slot * 128:(slot + 1) * 128],
                                            qtc["q"][0][:, col:col + 128], ident[:])))(),
                                            reads=[qtc["q"][1], "ident"], writes=[PB[bsel]], tag=f"{pre}hT{half}_{ot}_{grp2}_{oc_}_{tci_}")
                                    dest = hq_fm[:, (ot * 4 + grp2 * 2) * 512: (ot * 4 + grp2 * 2 + 2) * 512]
                                    sch.add("vector", (lambda bsel=bsel, dest=dest: (lambda e: e.tensor_copy(
                                        dest.rearrange("p (a b q) -> p a b q", b=4, q=128),
                                        pb[bsel][:].bitcast(BF16)[:, 0:1024].rearrange("p (a b q) -> p a b q", b=4, q=128))))(),
                                        reads=[PB[bsel]], writes=[pre + "hq"], tag=f"{pre}hqe{half}_{ot}_{grp2}")
                        pend_T.append(hq_transposes)
                        if len(pend_T) > 1:
                            pend_T.pop(0)()
                    while pend_T:
                        pend_T.pop(0)()

                def w2_stage(half, hook=None):
                    tbase = half * 4
                    for tci in range(4):
                        tch = tbase + tci
                        xt = xc2[tci % 2]; xtrg = pre + f"xc{tci % 2}"
                        sch.dma("sync", xt[:], x_src[tch * 128:(tch + 1) * 128, :], writes=[xtrg],
                            reads=[f"scX{tch}_{q}" for q in range(4)], tag=f"{pre}rx{tch}")
                        sch.add("vector", (lambda xt=xt, tci=tci, tch=tch: (lambda e: e.tensor_scalar(
                            res[tci][:], xt[:], ln2stats[:, tch * 2:tch * 2 + 1], ln2stats[:, tch * 2 + 1:tch * 2 + 2],
                            ALU.subtract, ALU.mult)))(),
                            reads=[xtrg, "ln2stats"], writes=[pre + f"res{tci}"], tag=f"{pre}rr{tch}", fsz=D)

                    for o2 in range(4):
                        bt2 = b2t[o2 % 2]; bt2rg = pre + f"b2t{o2 % 2}"
                        sch.dma("gpsimd", bt2[:], b2v.rearrange("(a b) -> a b", a=1)[0:1, o2 * 512:(o2 + 1) * 512],
                                writes=[bt2rg], tag=f"{pre}w2bl{half}_{o2}")
                        for kg in range(4):
                            wbuf = strip0 if kg % 2 == 0 else strip1
                            wrg = "strip0" if kg % 2 == 0 else "strip1"
                            for hf in range(2):
                                sch.dma("sync" if kg % 2 == 0 else "gpsimd",
                                        wbuf[:, hf * 4096:(hf + 1) * 4096].rearrange("p (kc o) -> p kc o", o=512),
                                        scT_w2[:, kg * 16 + hf * 8: kg * 16 + (hf + 1) * 8, o2 * 512:(o2 + 1) * 512],
                                        reads=[f"scqW2T#{i}" for i in range(4)], writes=[wrg + "ab"[hf]],
                                        tag=f"{pre}w2l{half}_{o2}_{kg}_{hf}")
                            for tci in range(4):
                                bank = 4 + tci
                                for kk in range(16):
                                    kc2 = kg * 16 + kk
                                    sch.add("tensor", (lambda bank=bank, kc2=kc2, kk=kk, tci=tci, wbuf=wbuf: (lambda e: e.matmul(
                                        pb[bank][:], hq_fm[:, kc2 * 512 + tci * 128: kc2 * 512 + (tci + 1) * 128],
                                        wbuf[:, kk * 512:(kk + 1) * 512],
                                        start=(kc2 == 0), stop=False)))(),
                                        reads=([pre + "hq", wrg + "ab"[kk // 8]] + ([PB[bank]] if kc2 else [])),
                                        writes=[PB[bank]], tag=f"{pre}w2m{half}_{o2}_{tci}_{kc2}")
                        for tci in range(4):
                            bank = 4 + tci
                            sch.add("tensor", (lambda bank=bank, bt2=bt2: (lambda e: e.matmul(
                                pb[bank][:], ones_r[:], bt2[:],
                                start=False, stop=True)))(),
                                reads=[bt2rg, "ones_r", PB[bank]], writes=[PB[bank]], tag=f"{pre}w2b{half}_{o2}_{tci}")
                        for tci in range(4):
                            bank = 4 + tci
                            fb = f2r[tci]; fbrg = pre + f"f2{tci}"
                            sch.add("vector", (lambda bank=bank, tci=tci, o2=o2, fb=fb: (lambda e: e.scalar_tensor_tensor(
                                fb[:], pb[bank][:], 1.0, res[tci][:, o2 * 512:(o2 + 1) * 512], ALU.mult, ALU.add)))(),
                                reads=[PB[bank], pre + f"res{tci}"], writes=[fbrg], tag=f"{pre}w2f{half}_{o2}_{tci}")
                        for tci in range(4):
                            tch = tbase + tci
                            fb = f2r[tci]; fbrg = pre + f"f2{tci}"
                            xsl = xs2[tci % 2]; xslrg = pre + f"xs{tci % 2}"
                            sch.dma("sync", xsl[:], x_src[tch * 128:(tch + 1) * 128, o2 * 512:(o2 + 1) * 512],
                                    reads=[f"scX{tch}_{o2}"], writes=[xslrg], tag=f"{pre}w2x{half}_{o2}_{tci}")
                            sch.add("vector", (lambda xsl=xsl, fb=fb: (lambda e: e.tensor_tensor(
                                xsl[:], fb[:], xsl[:], ALU.add)))(),
                                reads=[fbrg, xslrg], writes=[xslrg], tag=f"{pre}w2a{half}_{o2}_{tci}")
                            dst = out if final else sc_X
                            sch.dma("gpsimd", dst[tch * 128:(tch + 1) * 128, o2 * 512:(o2 + 1) * 512], xsl[:],
                                    reads=[xslrg], writes=([f"scX{tch}_{o2}"] if not final else []),
                                    tag=f"{pre}w2s{half}_{o2}_{tci}")
                        if hook is not None:
                            hook(o2)

                pend_lnT = []

                def ln1_hook(o2):
                    while pend_lnT and o2 in (1, 2):
                        pend_lnT.pop(0)()
                    if o2 in (0, 1):
                        chains = [ln_chain(1, o2 * 2 + j, (0, 1)) for j in (0, 1)]
                        for ch in chains:
                            pend_lnT.append(ch.pop())
                        for st in zip_longest(*chains):
                            for t in st:
                                if t is not None:
                                    t()

                ln_stage(0)
                w1_stage(0)
                w2_stage(0, hook=ln1_hook)
                w1_stage(1)
                w2_stage(1)
            sch.fence()

        if "ffn1" in phases:
            ffn_phase(1, sc_X, final=False)

        if "attnb" in phases:
            bp = ExitStack()
            with bp:
                xln3_fm = sb("b_xln", [128, KC * R], BF16, bp)
                v3_fm = sb("b_v3", [128, KC * R], BF16, bp)
                xc3 = [sb(f"b_xc{i}", [128, D], F32, bp) for i in range(2)]
                xb3 = [sb(f"b_xb{i}", [128, D], BF16, bp) for i in range(2)]
                lstat3 = sb("b_ls", [128, 16], F32, bp)
                sqs3 = sb("b_sq", [128, D], F32, bp)
                xsl3 = [sb(f"b_xs{i}", [128, 512], F32, bp) for i in range(2)]

                def b_chain(tch):
                    xt = xc3[tch % 2]; xtrg = f"b_xc{tch % 2}"
                    xbt = xb3[tch % 2]; xbrg = f"b_xb{tch % 2}"
                    scol = (tch % 2) * 8
                    sch.begin_collect()
                    sch.dma("sync", xt[:], sc_X[tch * 128:(tch + 1) * 128, :], writes=[xtrg],
                            reads=[f"scX{tch}_{q}" for q in range(4)], tag=f"b_lx{tch}")
                    ln_chunk(xt[:, 0:D], xtrg, lstat3, f"b_ls{tch % 2}", scol, sqs3[:], "b_sq", f"b_l{tch}")
                    ln_apply(xt[:, 0:D], xtrg, lstat3, f"b_ls{tch % 2}", scol, xbt[:, 0:D], xbrg, f"b_l{tch}")
                    chain = sch.end_collect()

                    def do_T(tch=tch, xbt=xbt, xbrg=xbrg):
                        def evb(bsel, gi, n, brg):
                            sch.add("vector", (lambda bsel=bsel, gi=gi, n=n: (lambda e: e.tensor_copy(
                                xln3_fm.rearrange("p (c t) -> p c t", t=R)[:, gi:gi + n, tch * 128:(tch + 1) * 128],
                                pb[bsel][:].bitcast(BF16)[:, 0:n * 128].rearrange("p (c q) -> p c q", q=128))))(),
                                reads=[brg], writes=["b_xln"], tag=f"b_xe{tch}_{gi}")
                        transpose_pack(xbt, xbrg, [c * 128 for c in range(KC)], [4, 5], evb, f"b_xT{tch}_")
                    chain.append(do_T)
                    return chain

                for pi in range(0, NTO, 2):
                    chains = [b_chain(pi), b_chain(pi + 1)]
                    for st in zip_longest(*chains):
                        for t in st:
                            if t is not None:
                                t()

                wcbb = [sb(f"b_wc{i}", [128, D], BF16, bp) for i in range(2)]
                for g in range(4):
                    wbuf = strip0 if g % 2 == 0 else strip1
                    wrg = "strip0" if g % 2 == 0 else "strip1"
                    mk_wstrip(wbuf, wrg, bWqkv[2 * D:3 * D, :], g, (6, 7), wcbb, "b_wc", f"b_wv{g}")
                    for oo in range(4):
                        for tt in range(2):
                            bank = oo
                            for kc2 in range(KC):
                                sch.add("tensor", (lambda bank=bank, kc2=kc2, oo=oo, tt=tt, wbuf=wbuf: (lambda e: e.matmul(
                                    pb[bank][:], wbuf[:, kc2 * 512 + oo * 128: kc2 * 512 + (oo + 1) * 128],
                                    xln3_fm[:, kc2 * R + tt * 512: kc2 * R + tt * 512 + 512],
                                    start=(kc2 == 0), stop=(kc2 == KC - 1))))(),
                                    reads=([wrg, "b_xln"] + ([PB[bank]] if kc2 else [])),
                                    writes=[PB[bank]], tag=f"b_v{g}_{oo}_{tt}_{kc2}")
                            oc = g * 4 + oo
                            sch.add("vector", (lambda bank=bank, oc=oc, tt=tt: (lambda e: e.tensor_copy(
                                v3_fm[:, oc * R + tt * 512: oc * R + tt * 512 + 512], pb[bank][:])))(),
                                reads=[PB[bank]], writes=["b_v3"], tag=f"b_ve{g}_{oo}_{tt}")

                for ot in range(4):
                    wbuf = strip0 if ot % 2 == 0 else strip1
                    wrg = "strip0" if ot % 2 == 0 else "strip1"
                    mk_wstrip(wbuf, wrg, bWo, ot, (0, 1), wcbb, "b_wc", f"b_wo{ot}")
                    for tch in range(NTO):
                        bank = 4 + tch % 4
                        for ic in range(KC):
                            sch.add("tensor", (lambda bank=bank, ic=ic, tch=tch, wbuf=wbuf: (lambda e: e.matmul(
                                pb[bank][:], v3_fm[:, ic * R + tch * 128: ic * R + (tch + 1) * 128],
                                wbuf[:, ic * 512:(ic + 1) * 512],
                                start=(ic == 0), stop=(ic == KC - 1))))(),
                                reads=(["b_v3", wrg] + ([PB[bank]] if ic else [])),
                                writes=[PB[bank]], tag=f"b_o{ot}_{tch}_{ic}")
                        xsl = xsl3[tch % 2]; xslrg = f"b_xs{tch % 2}"
                        sch.dma("sync", xsl[:], sc_X[tch * 128:(tch + 1) * 128, ot * 512:(ot + 1) * 512],
                                reads=[f"scX{tch}_{ot}"], writes=[xslrg], tag=f"b_lxs{ot}_{tch}")
                        sch.add("vector", (lambda bank=bank, xsl=xsl: (lambda e: e.tensor_tensor(
                            xsl[:], pb[bank][:], xsl[:], ALU.add)))(),
                            reads=[PB[bank], xslrg], writes=[xslrg], tag=f"b_or{ot}_{tch}")
                        sch.dma("gpsimd", sc_X[tch * 128:(tch + 1) * 128, ot * 512:(ot + 1) * 512], xsl[:],
                                reads=[xslrg], writes=[f"scX{tch}_{ot}"], tag=f"b_sxs{ot}_{tch}")
            sch.fence()

        if "ffn2" in phases:
            ffn_phase(2, sc_X, final=True)
        else:
            tmpo = sb("tmpo", [128, D], F32)
            for tch in range(NTO):
                sch.dma("sync", tmpo[:], sc_X[tch * 128:(tch + 1) * 128, :], writes=["tmpo"], tag=f"d_l{tch}")
                sch.dma("sync", out[tch * 128:(tch + 1) * 128, :], tmpo[:], reads=["tmpo"], tag=f"d_s{tch}")

        sch.resolve()
        global LAST_SCHED
        LAST_SCHED = sch
        semkeys = set()
        for o in sch.ops:
            if o.done:
                semkeys.add(o.done[0])
            for k, _ in o.waits:
                semkeys.add(k)
        sems = {k: ctx.enter_context(nc.semaphore("_".join(str(p) for p in k))) for k in sorted(semkeys, key=str)}
        blk = ctx.enter_context(nc.Block())
        sch.emit(nc, blk, sems)

    return nc


def make_in_maps(inputs):
    x = np.ascontiguousarray(np.asarray(inputs["L0_attn_a_x"], np.float32).reshape(SEQ, D))
    ident, pswap = host_ident()
    consts = host_consts()
    base = dict(
        aWq=np.ascontiguousarray(inputs["L0_attn_a_Wq"], dtype=np.float32),
        aWk=np.ascontiguousarray(inputs["L0_attn_a_Wk"], dtype=np.float32),
        aWv=np.ascontiguousarray(inputs["L0_attn_a_Wv"], dtype=np.float32),
        aWo=np.ascontiguousarray(inputs["L0_attn_a_Wo"], dtype=np.float32),
        bWqkv=np.ascontiguousarray(inputs["L0_attn_b_weight_qkv"], dtype=np.float32),
        bWo=np.ascontiguousarray(inputs["L0_attn_b_weight_out"], dtype=np.float32),
        W1=np.ascontiguousarray(inputs["L0_ffn_W1"], dtype=np.float32),
        W2=np.ascontiguousarray(inputs["L0_ffn_W2"], dtype=np.float32),
        b1v=np.ascontiguousarray(inputs["L0_ffn_b1"], dtype=np.float32),
        b2v=np.ascontiguousarray(inputs["L0_ffn_b2"], dtype=np.float32),
        lnw=np.ascontiguousarray(inputs["L0_ffn_ln_w"], dtype=np.float32),
        lnb=np.ascontiguousarray(inputs["L0_ffn_ln_b"], dtype=np.float32),
        identI=ident, pswapI=pswap, constsI=consts,
    )
    maps = []
    for c in range(NCORES):
        p0 = c * R
        if c == 0:
            xs = np.concatenate([np.zeros((HL, D), np.float32), x[:R]], axis=0)
        else:
            xs = x[p0 - HL: p0 + R]
        mm_, b0 = host_masks(c)
        m = dict(base)
        m.update(x_in=np.ascontiguousarray(xs), tabs=host_tables(c), maskM=mm_, maskB0=b0)
        maps.append(m)
    return maps


def kernel(**inputs):
    maps = make_in_maps(inputs)
    nc = build_program()
    res = run_bass_kernel_spmd(nc, maps, core_ids=list(range(NCORES)))
    outs = [np.asarray(r["out"], np.float32) for r in res.results]
    return np.concatenate(outs, axis=0).reshape(1, SEQ, D)
